# version 02
# baseline (speedup 1.0000x reference)
import numpy as np
import ml_dtypes
import concourse.bass as bass
import concourse.mybir as mybir
from concourse.bass_utils import run_bass_kernel_spmd

F32 = mybir.dt.float32
BF16 = mybir.dt.bfloat16
AF = mybir.ActivationFunctionType
ALU = mybir.AluOpType

B, T, D = 4, 4096, 1024
TO = 2048
DT = 8
NEGREF = -12.0
MAGIC = 12582912.0
PAIRS = [[0, 1], [2, 3], [4, 5], [6, 7]]
HMOE = 2048
CH = 128
NCH = T // CH
PAD = 64
STAGE = 10
DBG_SPILL_LIN = False

ENGS = ("sync", "gpsimd", "scalar", "vector", "tensor")


class EM:

    def __init__(self, nc):
        self.nc = nc
        self.cur = None
        self.h = None
        self.cnt = {e: 0 for e in ("pe", "dve", "act", "sp", "gp")}
        self.dmac = {q: 0 for q in ("sp", "gp", "act")}
        self.cc = 0
        self._sem_ctx = []
        self.sem = {}
        for name in ("pe", "dve", "act", "sp", "gp", "dsp", "dgp", "dact", "ccs"):
            g = nc.semaphore("sem_" + name)
            self.sem[name] = g.__enter__()
            self._sem_ctx.append(g)

    ENG2NAME = {"tensor": "pe", "vector": "dve", "scalar": "act",
                "sync": "sp", "gpsimd": "gp"}
    NORM = {"tensor": "pe", "vector": "dve", "scalar": "act", "sync": "sp",
            "gpsimd": "gp", "pe": "pe", "dve": "dve", "act": "act",
            "sp": "sp", "gp": "gp"}

    def phase(self, fn):
        snap = (dict(self.cnt), dict(self.dmac), self.cc)
        results = []
        with self.nc.Block() as blk:
            for pi, eng in enumerate(ENGS):
                self.cnt, self.dmac, self.cc = (dict(snap[0]), dict(snap[1]), snap[2])
                self.cur = self.ENG2NAME[eng]
                self._oplog_ref = getattr(self, "_oplog", None) if pi > 0 else None
                self._oplog = []
                self._opi = 0
                holder = {}

                def body(h, _fn=fn, _holder=holder):
                    self.h = h
                    _fn(self)
                    _holder["state"] = (dict(self.cnt), dict(self.dmac), self.cc)

                getattr(blk, eng)(body)
                results.append(holder["state"])
        for r in results[1:]:
            assert r == results[0], f"engine passes disagree: {results[0]} vs {r}"
        self.cnt, self.dmac, self.cc = (dict(results[0][0]), dict(results[0][1]),
                                       results[0][2])
        self.cur = None

    def _ck(self, tag):
        state = (tag, tuple(sorted(self.cnt.items())), tuple(sorted(self.dmac.items())))
        if self._oplog_ref is not None:
            assert self._opi < len(self._oplog_ref), f"extra op {state} in pass {self.cur}"
            ref = self._oplog_ref[self._opi]
            assert ref == state, f"pass {self.cur} op {self._opi}: {state} != ref {ref}"
        self._oplog.append(state)
        self._opi += 1

    def _emit(self, eng, f, inc):
        self._ck(("op", eng, inc))
        if self.cur == eng:
            inst = f(self.h)
            if inc:
                inst.then_inc(self.sem[eng], 1)
        if inc:
            self.cnt[eng] += 1

    def pe(self, f, inc=False):
        self._emit("pe", f, inc)

    def dve(self, f, inc=False):
        self._emit("dve", f, inc)

    def act(self, f, inc=False):
        self._emit("act", f, inc)

    def gp(self, f, inc=False):
        self._emit("gp", f, inc)

    def wait(self, waiter, target, lag=0):
        waiter, target = self.NORM[waiter], self.NORM[target]
        self._ck(("wait", waiter, target, lag))
        v = self.cnt[target] - lag
        if v <= 0:
            return
        if self.cur == waiter:
            self.h.wait_ge(self.sem[target], v)

    def mark(self, target):
        return self.cnt[self.NORM[target]]

    def wait_abs(self, waiter, target, value):
        waiter, target = self.NORM[waiter], self.NORM[target]
        self._ck(("wait_abs", waiter, target, value))
        if value <= 0:
            return
        if self.cur == waiter:
            self.h.wait_ge(self.sem[target], value)

    def dma(self, q, out, in_, inc=True):
        self._ck(("dma", q, inc))
        if self.cur == q:
            inst = self.h.dma_start(out=out, in_=in_)
            if inc:
                inst.then_inc(self.sem["d" + q], 16)
        if inc:
            self.dmac[q] += 16

    def mark_dma(self, q):
        return self.dmac[q]

    def wait_dma(self, waiter, q, lag=0, value=None):
        waiter = self.NORM[waiter]
        self._ck(("wait_dma", waiter, q, lag, value))
        v = (self.dmac[q] - 16 * lag) if value is None else value
        if v <= 0:
            return
        if self.cur == waiter:
            self.h.wait_ge(self.sem["d" + q], v)

    def collective(self, kind, op, ins, outs):
        if self.cur == "gp":
            self.h.collective_compute(kind, op, replica_groups=PAIRS,
                                      ins=ins, outs=outs).then_inc(self.sem["ccs"])
        self.cc += 1

    def wait_cc(self, waiter):
        waiter = self.NORM[waiter]
        if self.cur == waiter and self.cc > 0:
            self.h.wait_ge(self.sem["ccs"], self.cc)


def build_nc(debug_taps=()):
    nc = bass.Bass("TRN2", target_bir_lowering=False, debug=False,
                   detect_race_conditions=False)

    inp = {}

    def di(name, shape, dtype=F32):
        inp[name] = nc.dram_tensor(name, shape, dtype, kind="ExternalInput").ap()
        return inp[name]

    x_own = di("x_own", [D, TO])
    x_full = di("x_full", [D, T])
    row_idx = di("row_idx", [1, TO])
    col_idx = di("col_idx", [128, T // 128])
    ident_b = di("ident_b", [128, 128], BF16)
    ident_f = di("ident_f", [128, 128])
    msl_neg = di("msl_neg", [128, 128])
    msu_neg = di("msu_neg", [128, 128])
    tril_f = di("tril_f", [128, 128])
    triu_f = di("triu_f", [128, 128])
    sel01 = di("sel01", [128, 2])
    sel8 = di("sel8", [8, 8 * 128])
    sel10 = di("sel10", [10, 10 * 128])
    selh = di("selh", [2, 256], BF16)

    wq0 = di("wq0", [D, D])
    wk0 = di("wk0", [D, D])
    wv0 = di("wv0", [D, D])
    bq0 = di("bq0", [128, DT])
    bk0 = di("bk0", [128, DT])
    bv0 = di("bv0", [1, D])
    gw0 = di("gw0", [D, 8])
    ew1 = di("ew1", [8, D, HMOE])
    ew2 = di("ew2", [8, D, HMOE])
    ew3 = di("ew3", [8, HMOE, D])
    sw1 = di("sw1", [D, HMOE])
    sw2 = di("sw2", [D, HMOE])
    sw3 = di("sw3", [HMOE, D])

    wq1 = di("wq1", [D, 512])
    wk1 = di("wk1", [D, 512])
    wv1 = di("wv1", [D, 512])
    wb1 = di("wb1", [D, 2])
    cvq = di("cvq", [128, 16])
    cvk = di("cvk", [128, 16])
    cvv = di("cvv", [128, 16])
    firs = di("firs", [128, 4 * 3])
    firm = di("firm", [128, 4 * 15])
    firl = di("firl", [128, 4 * 63])
    gw1 = di("gw1", [D, 10])
    gb1 = di("gb1", [1, 10])
    ltmp = di("ltmp", [1, 2])
    epsp = di("epsp", [1, 2])
    gamm = di("gamm", [128, 2])
    wo1 = di("wo1", [D, D])
    w1t = di("w1t", [D, 4096])
    w2t = di("w2t", [4096, D])
    b1f = di("b1f", [128, 32])
    b2f = di("b2f", [128, 8])

    y = nc.dram_tensor("y", [D, TO], F32, kind="ExternalOutput").ap()

    s_xn0o = nc.dram_tensor("s_xn0o", [D, TO], BF16).ap()
    s_xn0f = nc.dram_tensor("s_xn0f", [D, T], BF16).ap()
    s_oun = nc.dram_tensor("s_oun", [D, TO], BF16).ap()
    s_x1 = nc.dram_tensor("s_x1", [D, TO], F32).ap()
    s_xq = nc.dram_tensor("s_xq", [D, TO], BF16).ap()
    s_ffn1 = nc.dram_tensor("s_ffn1", [D, TO], F32).ap()
    s_q = nc.dram_tensor("s_q", [D, TO], BF16).ap()
    s_qc = nc.dram_tensor("s_qc", [512, T], BF16).ap()
    s_kc = nc.dram_tensor("s_kc", [512, T], BF16).ap()

    cc_z_i = nc.dram_tensor("cc_z_i", [1, 128], F32).ap()
    cc_z_o = nc.dram_tensor("cc_z_o", [2, 128], F32).ap()
    cc_x_i = nc.dram_tensor("cc_x_i", [D, TO], BF16).ap()
    cc_x_o = nc.dram_tensor("cc_x_o", [2 * D, TO], BF16).ap()
    cc_o_i = nc.dram_tensor("cc_o_i", [512, T], BF16).ap()
    cc_o_o = nc.dram_tensor("cc_o_o", [1024, T], BF16).ap()

    taps = {}
    for tname in debug_taps:
        shp = {"xn0o": [D, TO], "q0": [D, TO], "attn0": [D, TO], "zz": [1, 260],
               "wdump": [128, 4096],
               "ffn0": [D, TO], "x1": [D, TO], "xn1f": [D, T], "ffn1": [D, TO],
               "qc1": [512, T], "kc1": [512, T], "vc1": [512, T], "beta": [2, T],
               "onrm": [512, T], "probs": [10, T], "dlt": [512, T],
               "dbg0": [128, 4096], "wef": [8, TO],
               "gdbg": [128, 128]}[tname]
        taps[tname] = nc.dram_tensor("tap_" + tname, shp, F32,
                                     kind="ExternalOutput").ap()

    em = EM(nc)
    fm = lambda ap, n=DT: ap.rearrange("(dt p) t -> p dt t", dt=n)

    alloc_stack = {"left": [], "right": []}

    def sb(name, shape, dtype, side="left"):
        g = nc.sbuf_tensor(name, shape, dtype, side=side)
        h = g.__enter__()
        alloc_stack[side].append((name, g))
        return h

    def sb_free(upto_name, side="left"):
        stk = alloc_stack[side]
        while stk:
            name, g = stk.pop()
            g.__exit__(None, None, None)
            if name == upto_name:
                return
        raise KeyError(upto_name)

    c_ones_b = sb("c_ones_b", [128, 1], BF16)
    c_ones_f = sb("c_ones_f", [128, 1], F32)
    c_onesr_f = sb("c_onesr_f", [1, 128], F32)
    c_bq = sb("c_bq", [128, DT], F32)
    c_bk = sb("c_bk", [128, DT], F32)
    c_sel = sb("c_sel", [128, 2], F32)
    c_small = sb("c_small", [128, 16], F32)
    c_zpad = sb("c_zpad", [1, 128], F32)
    STAT = sb("STAT", [2, 1024], F32)
    ST = sb("ST", [128, 1024], F32)
    c_ident_f = sb("c_ident_f", [128, 128], F32)
    c_rowbc = sb("c_rowbc", [128, TO], F32, side="right")
    c_colidx = sb("c_colidx", [128, 32], F32, side="right")

    ps1 = nc.psum_tensor("ps1", [128, 512], F32).__enter__()
    ps2 = nc.psum_tensor("ps2", [128, 512], F32).__enter__()
    ps3 = nc.psum_tensor("ps3", [128, 512], F32).__enter__()
    ps4 = nc.psum_tensor("ps4", [128, 512], F32).__enter__()
    ps5 = nc.psum_tensor("ps5", [128, 512], F32).__enter__()
    ps6 = nc.psum_tensor("ps6", [128, 512], F32).__enter__()
    ps7 = nc.psum_tensor("ps7", [128, 512], F32).__enter__()
    psT = nc.psum_tensor("psT", [128, 1024], BF16).__enter__()
    OB = [ps1, ps2, ps3, ps4]

    def p0(em):
        em.dma("sp", c_ident_f[:], ident_f)
        em.dma("sp", c_colidx[:], col_idx)
        em.dma("sp", c_bq[:], bq0)
        em.dma("sp", c_bk[:], bk0)
        em.dma("sp", c_sel[:], sel01)
        em.wait_dma("vector", "sp")
        em.dve(lambda h: h.memset(c_ones_b[:], 1.0), inc=True)
        em.dve(lambda h: h.memset(c_ones_f[:], 1.0), inc=True)
        em.dve(lambda h: h.memset(c_onesr_f[:], 1.0), inc=True)
        em.dve(lambda h: h.memset(c_zpad[:], 0.0), inc=True)
        em.dve(lambda h: h.memset(c_small[:, 0:4], 0.0), inc=True)
        em.dve(lambda h: h.memset(c_small[:, 4:5], NEGREF), inc=True)
        em.dve(lambda h: h.memset(c_small[:, 5:6], MAGIC), inc=True)
        em.dve(lambda h: h.memset(c_small[:, 6:16], 0.0), inc=True)
        em.dve(lambda h: h.tensor_single_scalar(out=c_bq[:], in_=c_bq[:],
                                                scalar=1.0 / 32.0, op=ALU.mult),
               inc=True)
        em.wait("tensor", "dve")
        for i in range(TO // 512):
            em.wait("sp", "pe")
            em.dma("sp", ST[0:1, 0:512], row_idx[0:1, i * 512:(i + 1) * 512])
            em.wait_dma("tensor", "sp")
            em.pe(lambda h, i=i: h.matmul(ps1[:, 0:512], c_onesr_f[:],
                                          ST[0:1, 0:512],
                                          start=True, stop=True), inc=True)
            em.wait("vector", "pe")
            em.dve(lambda h, i=i: h.tensor_copy(out=c_rowbc[:, i * 512:(i + 1) * 512],
                                                in_=ps1[:, 0:512]), inc=True)
            em.wait("tensor", "dve")

    def ln_chunk(em, SC, SD, cw, dst_fn):
        for dt in range(DT):
            em.pe(lambda h, dt=dt: h.matmul(
                ps5[0:1, 0:cw], c_ones_f[:], SC[:, dt * cw:(dt + 1) * cw],
                start=(dt == 0), stop=(dt == DT - 1)), inc=(dt == DT - 1))
        em.wait("scalar", "pe", lag=1)
        for dt in range(DT):
            em.act(lambda h, dt=dt: h.activation(
                SD[:, dt * cw:(dt + 1) * cw], SC[:, dt * cw:(dt + 1) * cw],
                AF.Square), inc=(dt == DT - 1))
        em.wait("tensor", "act")
        for dt in range(DT):
            em.pe(lambda h, dt=dt: h.matmul(
                ps6[0:1, 0:cw], c_ones_f[:], SD[:, dt * cw:(dt + 1) * cw],
                start=(dt == 0), stop=(dt == DT - 1)), inc=(dt == DT - 1))
        em.wait("vector", "pe")
        em.dve(lambda h: h.tensor_single_scalar(out=STAT[0:1, 0:cw], in_=ps5[0:1, 0:cw],
                                                scalar=1.0 / D, op=ALU.mult), inc=True)
        em.dve(lambda h: h.tensor_single_scalar(out=STAT[0:1, cw:2 * cw],
                                                in_=ps6[0:1, 0:cw],
                                                scalar=1.0 / D, op=ALU.mult), inc=True)
        em.dve(lambda h: h.tensor_tensor(out=ST[0:1, 0:cw], in0=STAT[0:1, 0:cw],
                                         in1=STAT[0:1, 0:cw], op=ALU.mult), inc=True)
        em.dve(lambda h: h.tensor_tensor(out=ST[0:1, 0:cw], in0=STAT[0:1, cw:2 * cw],
                                         in1=ST[0:1, 0:cw], op=ALU.subtract), inc=True)
        em.dve(lambda h: h.tensor_single_scalar(out=ST[0:1, 0:cw], in_=ST[0:1, 0:cw],
                                                scalar=1e-5, op=ALU.add), inc=True)
        em.wait("scalar", "dve")
        em.act(lambda h: h.activation(ST[0:1, cw:2 * cw], ST[0:1, 0:cw], AF.Sqrt),
               inc=True)
        em.wait("vector", "act")
        em.dve(lambda h: h.reciprocal(out=ST[0:1, 0:cw], in_=ST[0:1, cw:2 * cw]),
               inc=True)
        em.wait("tensor", "dve")
        nb = max(cw // 512, 1)
        bw = min(cw, 512)
        for i in range(nb):
            em.pe(lambda h, i=i: h.matmul(ps5[:, 0:bw], c_onesr_f[:],
                                          STAT[0:1, i * bw:(i + 1) * bw],
                                          start=True, stop=True), inc=True)
            em.pe(lambda h, i=i: h.matmul(ps6[:, 0:bw], c_onesr_f[:],
                                          ST[0:1, i * bw:(i + 1) * bw],
                                          start=True, stop=True), inc=True)
            em.wait("vector", "pe")
            for dt in range(DT):
                o0 = dt * cw + i * bw
                em.dve(lambda h, o0=o0, bw=bw: h.tensor_tensor(
                    out=SD[:, o0:o0 + bw], in0=SC[:, o0:o0 + bw],
                    in1=ps5[:, 0:bw], op=ALU.subtract), inc=True)
                em.dve(lambda h, o0=o0, bw=bw: h.tensor_tensor(
                    out=SD[:, o0:o0 + bw], in0=SD[:, o0:o0 + bw],
                    in1=ps6[:, 0:bw], op=ALU.mult), inc=True)
                dst_fn(em, dt, i * bw, bw)
            em.wait("tensor", "dve")

    CW = 512

    def mk_p1():
        Aq = sb("Aq", [128, DT * TO], BF16)
        Ax = sb("Ax", [128, DT * TO], BF16)
        Wst = sb("Wst_p1", [128, DT * 1024], BF16)
        SC = sb("SC_p1", [128, DT * CW], F32)
        SD = sb("SD_p1", [128, DT * CW], F32)

        def p1(em):
            em.dma("gp", Wst[:].rearrange("p (kt n) -> p kt n", kt=DT), fm(wq0))
            for tch in range(TO // CW):
                em.wait("sp", "pe")
                em.wait("sp", "dve")
                em.dma("sp", SC[:].rearrange("p (dt t) -> p dt t", dt=DT),
                       fm(x_own)[:, :, tch * CW:(tch + 1) * CW])
                em.wait_dma("tensor", "sp")
                em.wait_dma("scalar", "sp")
                em.wait_dma("vector", "sp")

                def dst(em, dt, off, bw, tch=tch):
                    em.dve(lambda h, dt=dt, off=off, bw=bw, tch=tch: h.tensor_copy(
                        out=Aq[:, 0:1][:, 0:0] if False else
                        Ax[:, dt * TO + tch * CW + off:dt * TO + tch * CW + off + bw],
                        in_=SD[:, dt * CW + off:dt * CW + off + bw]), inc=True)

                ln_chunk(em, SC, SD, CW, dst)
            em.wait("sync", "dve")
            em.dma("sp", fm(s_xn0o), Ax[:].rearrange("p (dt t) -> p dt t", dt=DT))
            em.wait_dma("tensor", "gp")
            for dqt in range(DT):
                for tch in range(TO // 512):
                    bank = ps1 if (dqt * 4 + tch) % 2 == 0 else ps2
                    em.wait("tensor", "act", lag=1)
                    for kt in range(DT):
                        em.pe(lambda h, dqt=dqt, tch=tch, kt=kt, bank=bank: h.matmul(
                            bank[:, 0:512],
                            Wst[:, kt * 1024 + dqt * 128:kt * 1024 + dqt * 128 + 128],
                            Ax[:, kt * TO + tch * 512:kt * TO + tch * 512 + 512],
                            start=(kt == 0), stop=(kt == DT - 1)), inc=(kt == DT - 1))
                    em.wait("scalar", "pe")
                    em.act(lambda h, dqt=dqt, tch=tch, bank=bank: h.activation(
                        Aq[:, dqt * TO + tch * 512:dqt * TO + tch * 512 + 512],
                        bank[:, 0:512], AF.Identity, scale=1.0 / 32.0,
                        bias=c_bq[:, dqt:dqt + 1]), inc=True)
            em.wait("sync", "act")
            em.dma("sp", fm(s_q), Aq[:].rearrange("p (dt t) -> p dt t", dt=DT))
            if "xn0o" in taps:
                em.wait("gpsimd", "act")
                em.dma("gp", fm(taps["xn0o"]),
                       Ax[:].rearrange("p (dt t) -> p dt t", dt=DT))
            if "q0" in taps:
                em.wait("gpsimd", "act")
                em.dma("gp", fm(taps["q0"]),
                       Aq[:].rearrange("p (dt t) -> p dt t", dt=DT))
            em.wait_dma("sync", "sp")
            em.wait_dma("gpsimd", "gp")

        em.phase(p1)
        sb_free("Aq")
        return None

    def mk_p2():
        Ak = sb("Ak", [128, DT * T], BF16)
        Wst = sb("Wst_p2", [128, DT * 1024], BF16)
        SC = sb("SC_p2", [128, DT * CW], F32)
        SD = sb("SD_p2", [128, DT * CW], F32)

        def p2(em):
            em.dma("gp", Wst[:].rearrange("p (kt n) -> p kt n", kt=DT), fm(wk0))
            em.wait_dma("tensor", "gp")
            for tch in range(T // CW):
                em.wait("sp", "pe")
                em.wait("sp", "dve")
                em.dma("sp", SC[:].rearrange("p (dt t) -> p dt t", dt=DT),
                       fm(x_full)[:, :, tch * CW:(tch + 1) * CW])
                em.wait_dma("tensor", "sp")
                em.wait_dma("scalar", "sp")
                em.wait_dma("vector", "sp")

                def dst(em, dt, off, bw, tch=tch):
                    pass

                ln_chunk(em, SC, SD, CW, dst)
                SCb = SC.bitcast(BF16)
                em.wait("vector", "pe")
                em.wait_dma("vector", "sp")
                em.dve(lambda h: h.tensor_copy(out=SCb[:, 0:DT * CW],
                                               in_=SD[:, 0:DT * CW]), inc=True)
                em.wait("sync", "dve")
                em.dma("sp", fm(s_xn0f)[:, :, tch * CW:(tch + 1) * CW],
                       SCb[:, 0:DT * CW].rearrange("p (dt t) -> p dt t", dt=DT))
                for dqt in range(DT):
                    bank = ps1 if dqt % 2 == 0 else ps2
                    em.wait("tensor", "act", lag=1)
                    em.wait("tensor", "dve")
                    for kt in range(DT):
                        em.pe(lambda h, dqt=dqt, kt=kt, bank=bank, tch=tch: h.matmul(
                            bank[:, 0:CW],
                            Wst[:, kt * 1024 + dqt * 128:kt * 1024 + dqt * 128 + 128],
                            SCb[:, kt * CW:(kt + 1) * CW],
                            start=(kt == 0), stop=(kt == DT - 1)), inc=(kt == DT - 1))
                    em.wait("scalar", "pe")
                    em.act(lambda h, dqt=dqt, tch=tch, bank=bank: h.activation(
                        Ak[:, dqt * T + tch * CW:dqt * T + tch * CW + CW],
                        bank[:, 0:CW], AF.Identity, bias=c_bk[:, dqt:dqt + 1]),
                        inc=True)
                em.wait("sp", "act")
            em.wait_dma("gpsimd", "gp")
            em.wait_dma("sync", "sp")

        em.phase(p2)
        sb_free("Wst_p2")
        return Ak

    def mk_p3():
        Av = sb("Av", [128, 32 * 1024], BF16)
        Wst = sb("Wst_p3", [128, DT * 1024], BF16)
        Xc = sb("Xc_p3", [128, DT * CW], BF16)
        bvb = ST

        def p3(em):
            em.dma("gp", Wst[:].rearrange("p (kt n) -> p kt n", kt=DT), fm(wv0))
            em.dma("sp", STAT[0:1, 0:D], bv0)
            em.wait_dma("tensor", "sp")
            for i in range(D // 512):
                em.pe(lambda h, i=i: h.matmul(ps5[:, 0:512], c_onesr_f[:],
                                              STAT[0:1, i * 512:(i + 1) * 512],
                                              start=True, stop=True), inc=True)
                em.wait("vector", "pe")
                em.dve(lambda h, i=i: h.tensor_copy(out=bvb[:, i * 512:(i + 1) * 512],
                                                    in_=ps5[:, 0:512]), inc=True)
                em.wait("tensor", "dve")
            em.wait_dma("tensor", "gp")
            for g in range(T // CW):
                em.wait("sp", "pe")
                em.dma("sp", Xc[:].rearrange("p (dt t) -> p dt t", dt=DT),
                       fm(s_xn0f)[:, :, g * CW:(g + 1) * CW])
                em.wait_dma("tensor", "sp")
                for tt in range(4):
                    for n in range(2):
                        bank = ps1 if (tt * 2 + n) % 2 == 0 else ps2
                        em.wait("tensor", "dve", lag=1)
                        for kt in range(DT):
                            em.pe(lambda h, tt=tt, n=n, kt=kt, bank=bank, g=g:
                                  h.matmul(
                                      bank[:, 0:512],
                                      Xc[:, kt * CW + tt * 128:
                                         kt * CW + tt * 128 + 128],
                                      Wst[:, kt * 1024 + n * 512:
                                          kt * 1024 + n * 512 + 512],
                                      start=(kt == 0), stop=(kt == DT - 1)),
                                  inc=(kt == DT - 1))
                        em.wait("vector", "pe")
                        gt = g * 4 + tt
                        em.dve(lambda h, gt=gt, n=n, bank=bank:
                               h.scalar_tensor_tensor(
                                   out=Av[:, gt * 1024 + n * 512:
                                          gt * 1024 + n * 512 + 512],
                                   in0=bank[:, 0:512], scalar=1.0,
                                   in1=bvb[:, n * 512:(n + 1) * 512],
                                   op0=ALU.mult, op1=ALU.add), inc=True)
            em.wait("sync", "dve")
            em.wait_dma("gpsimd", "gp")

        em.phase(p3)
        sb_free("Wst_p3")
        return Av

    def mk_p4(Ak, Av):
        Eall = sb("Eall_p4", [128, 32 * 256], BF16)
        Qrb = sb("Qrb_p4", [128, DT * 256], BF16)
        Wsc = sb("Wsc_p4", [128, 1536], BF16)

        def p4(em):
            W32 = Wsc.bitcast(F32)
            em.dve(lambda h: h.memset(c_small[:, 0:4], 0.0), inc=True)
            qld = [0]
            for rb in range(8):
                ncols = 2048 if rb < 4 else T
                ncts = ncols // 128
                em.wait("sp", "pe")
                em.dma("sp", Qrb[:].rearrange("p (dt t) -> p dt t", dt=DT),
                       fm(s_q)[:, :, rb * 256:(rb + 1) * 256])
                qld[0] = em.mark_dma("sp")
                s_done = [0] * ncts
                exp_done = [0] * ncts
                stt_done = [0] * ncts

                def emit_sgroup(ct, rb=rb, s_done=s_done, exp_done=exp_done):
                    sbank = ps5 if ct % 2 == 0 else ps6
                    if ct >= 2:
                        em.wait_abs("tensor", "act", exp_done[ct - 2])
                    if ct == 0:
                        em.wait_dma("tensor", "sp", value=qld[0])
                    for kt in range(DT):
                        em.pe(lambda h, ct=ct, kt=kt, sbank=sbank, rb=rb: h.matmul(
                            sbank[:, 0:256],
                            Ak[:, kt * T + ct * 128:kt * T + ct * 128 + 128],
                            Qrb[:, kt * 256:kt * 256 + 256],
                            start=(kt == 0), stop=(kt == DT - 1)),
                            inc=(kt == DT - 1))
                    s_done[ct] = em.mark("pe")

                emit_sgroup(0)
                for ct in range(ncts):
                    if ct + 1 < ncts:
                        emit_sgroup(ct + 1)
                    sbank = ps5 if ct % 2 == 0 else ps6
                    em.dve(lambda h, rb=rb, ct=ct: h.tensor_single_scalar(
                        out=W32[:, 0:256],
                        in_=c_rowbc[:, rb * 256:rb * 256 + 256],
                        scalar=c_colidx[:, ct:ct + 1], op=ALU.is_ge), inc=True)
                    em.wait_abs("scalar", "pe", s_done[ct])
                    if ct >= 1:
                        em.wait_abs("scalar", "dve", stt_done[ct - 1])
                    em.act(lambda h, sbank=sbank: h.activation(
                        Wsc[:, 1024:1280], sbank[:, 0:256], AF.Exp,
                        bias=c_small[:, 4:5]), inc=True)
                    exp_done[ct] = em.mark("act")
                    em.wait("vector", "act")
                    em.dve(lambda h, ct=ct: h.scalar_tensor_tensor(
                        out=Eall[:, ct * 256:ct * 256 + 256], in0=Wsc[:, 1024:1280],
                        scalar=1.0, in1=W32[:, 0:256], op0=ALU.mult, op1=ALU.mult,
                        accum_out=c_small[:, 2:3]), inc=True)
                    em.dve(lambda h: h.tensor_tensor(
                        out=c_small[:, 0:1], in0=c_small[:, 0:1],
                        in1=c_small[:, 2:3], op=ALU.add), inc=True)
                    stt_done[ct] = em.mark("dve")
                em.wait_abs("tensor", "dve", stt_done[ncts - 1])
                em.wait("tensor", "dve")
                for dvh in range(2):
                    if dvh == 1:
                        em.wait("tensor", "dve")
                    for ct in range(ncts):
                        for i in range(4):
                            dvt = dvh * 4 + i
                            em.pe(lambda h, ct=ct, dvt=dvt, i=i: h.matmul(
                                OB[i][:, 0:256],
                                Av[:, ct * 1024 + dvt * 128:
                                   ct * 1024 + dvt * 128 + 128],
                                Eall[:, ct * 256:ct * 256 + 256],
                                start=(ct == 0), stop=(ct == ncts - 1)),
                                inc=(ct == ncts - 1 and i == 3))
                    em.wait("vector", "pe")
                    em.wait_dma("vector", "sp")
                    for i in range(4):
                        em.dve(lambda h, i=i: h.tensor_copy(
                            out=ST[:, i * 256:(i + 1) * 256],
                            in_=OB[i][:, 0:256]), inc=(i == 3))
                    em.wait("gpsimd", "dve")
                    em.dma("gp", fm(s_oun)[:, dvh * 4:dvh * 4 + 4,
                                           rb * 256:(rb + 1) * 256],
                           ST[:, 0:1024].rearrange("p (dt t) -> p dt t", dt=4))
                    em.wait_dma("vector", "gp")
            em.wait("tensor", "dve")
            em.pe(lambda h: h.matmul(ps5[0:1, 0:1], c_small[:, 0:1],
                                     c_ones_f[:], start=True, stop=True), inc=True)
            em.wait("vector", "pe")
            em.dve(lambda h: h.tensor_copy(out=c_zpad[0:1, 0:1], in_=ps5[0:1, 0:1]),
                   inc=True)
            em.wait("sync", "dve")
            em.dma("sp", cc_z_i, c_zpad[:])
            em.wait_dma("gpsimd", "sp")
            em.collective("AllGather", ALU.bypass, [cc_z_i.opt()], [cc_z_o.opt()])

        em.phase(p4)
        sb_free("Eall_p4")

    def mk_p5():
        ACC = sb("ACC_p5", [128, DT * TO], F32)
        WEF = sb("WEF_p5", [8, TO], F32)
        W1s = sb("W1s_p5", [128, DT * HMOE], BF16)
        W2s = sb("W2s_p5", [128, DT * HMOE], BF16)
        W3s = sb("W3s_p5", [128, 16 * 1024], BF16)
        Xt = sb("Xt_p5", [128, DT * 512], BF16)
        Gt = sb("Gt_p5", [128, 16 * 512], BF16)
        WB = sb("WB_p5", [128, 512], F32)
        Tm = sb("Tm_p5", [128, 1024], F32)
        S8 = ST.ap()[0:8, 0:1024] if False else ST

        def p5(em):
            GWB = Tm.bitcast(BF16)
            for tch in range(4):
                em.wait("sp", "pe")
                em.wait("sp", "dve")
                em.dma("sp", Xt[:].rearrange("p (dt t) -> p dt t", dt=DT),
                       fm(s_xn0o)[:, :, tch * 512:(tch + 1) * 512])
                if tch == 0:
                    em.dma("sp", ST[0:8, 0:1024], sel8)
                    em.dma("gp",
                           GWB[:, 1920:1984].rearrange("p (kt e) -> p kt e", kt=DT),
                           gw0.rearrange("(kt p) e -> p kt e", kt=DT))
                em.wait_dma("tensor", "sp")
                em.wait_dma("tensor", "gp")
                sb_ = lambda st: 64 + st * 128
                g_ = lambda st: Tm[:, sb_(st) + 0:sb_(st) + 8]
                m1_ = lambda st: Tm[:, sb_(st) + 8:sb_(st) + 9]
                m2_ = lambda st: Tm[:, sb_(st) + 9:sb_(st) + 10]
                den_ = lambda st: Tm[:, sb_(st) + 10:sb_(st) + 11]
                rde_ = lambda st: Tm[:, sb_(st) + 11:sb_(st) + 12]
                eq1_ = lambda st: Tm[:, sb_(st) + 16:sb_(st) + 24]
                eq2_ = lambda st: Tm[:, sb_(st) + 24:sb_(st) + 32]
                w_ = lambda st: Tm[:, sb_(st) + 32:sb_(st) + 40]
                gp2_ = lambda st: Tm[:, sb_(st) + 40:sb_(st) + 48]
                nm1_ = lambda st: Tm[:, sb_(st) + 48:sb_(st) + 49]
                ex_ = lambda st: Tm[:, sb_(st) + 56:sb_(st) + 64]
                for tt in range(4):
                    em.wait("tensor", "dve")
                    for kt in range(DT):
                        em.pe(lambda h, tch=tch, tt=tt, kt=kt: h.matmul(
                            ps7[:, 0:8],
                            Xt[:, kt * 512 + tt * 128:kt * 512 + tt * 128 + 128],
                            GWB[:, 1920 + kt * 8:1920 + kt * 8 + 8],
                            start=(kt == 0), stop=(kt == DT - 1)),
                            inc=(kt == DT - 1))
                    em.wait("vector", "pe")
                    em.dve(lambda h, tt=tt: h.tensor_copy(out=g_(tt),
                                                          in_=ps7[:, 0:8]),
                           inc=True)
                for st in range(4):
                    em.dve(lambda h, st=st: h.tensor_reduce(
                        out=m1_(st), in_=g_(st), axis=mybir.AxisListType.X,
                        op=ALU.max), inc=(st == 3))
                for st in range(4):
                    em.dve(lambda h, st=st: h.tensor_single_scalar(
                        out=eq1_(st), in_=g_(st), scalar=m1_(st),
                        op=ALU.is_ge), inc=(st == 3))
                for st in range(4):
                    em.dve(lambda h, st=st: h.scalar_tensor_tensor(
                        out=gp2_(st), in0=eq1_(st), scalar=-1e30, in1=g_(st),
                        op0=ALU.mult, op1=ALU.add), inc=(st == 3))
                for st in range(4):
                    em.dve(lambda h, st=st: h.tensor_reduce(
                        out=m2_(st), in_=gp2_(st), axis=mybir.AxisListType.X,
                        op=ALU.max), inc=(st == 3))
                for st in range(4):
                    em.dve(lambda h, st=st: h.tensor_single_scalar(
                        out=eq2_(st), in_=gp2_(st), scalar=m2_(st),
                        op=ALU.is_ge), inc=(st == 3))
                for st in range(4):
                    em.dve(lambda h, st=st: h.tensor_single_scalar(
                        out=nm1_(st), in_=m1_(st), scalar=-1.0,
                        op=ALU.mult), inc=(st == 3))
                em.wait("scalar", "dve")
                for st in range(4):
                    em.act(lambda h, st=st: h.activation(
                        ex_(st), g_(st), AF.Exp, bias=nm1_(st)),
                        inc=(st == 3))
                em.wait("vector", "act")
                for st in range(4):
                    em.dve(lambda h, st=st: h.tensor_tensor(
                        out=w_(st), in0=eq1_(st), in1=eq2_(st),
                        op=ALU.add), inc=(st == 3))
                for st in range(4):
                    em.dve(lambda h, st=st: h.tensor_tensor(
                        out=w_(st), in0=w_(st), in1=ex_(st),
                        op=ALU.mult), inc=(st == 3))
                for st in range(4):
                    em.dve(lambda h, st=st: h.tensor_reduce(
                        out=den_(st), in_=w_(st), axis=mybir.AxisListType.X,
                        op=ALU.add), inc=(st == 3))
                for st in range(4):
                    em.dve(lambda h, st=st: h.reciprocal(
                        out=rde_(st), in_=den_(st)), inc=(st == 3))
                for st in range(4):
                    em.dve(lambda h, st=st: h.tensor_single_scalar(
                        out=w_(st), in_=w_(st), scalar=rde_(st),
                        op=ALU.mult), inc=(st == 3))
                for st in range(4):
                    em.wait("tensor", "dve")
                    em.pe(lambda h, st=st: h.transpose(
                        ps6[0:8, 0:128], w_(st), c_ident_f[:]), inc=True)
                    em.wait("vector", "pe")
                    em.dve(lambda h, tch=tch, st=st: h.tensor_copy(
                        out=WEF[0:8, tch * 512 + st * 128:
                                tch * 512 + st * 128 + 128],
                        in_=ps6[0:8, 0:128]), inc=True)
            if "wef" in taps:
                em.wait("gp", "dve")
                em.dma("gp", taps["wef"], WEF[0:8, :])
                em.wait_dma("tensor", "gp")
            wdma = [0]

            def load_weights(e):
                if e < 8:
                    em.dma("gp", W1s[:].rearrange("p (kt n) -> p kt n", kt=DT),
                           ew1[e].rearrange("(kt p) n -> p kt n", kt=DT))
                    em.dma("gp", W2s[:].rearrange("p (kt n) -> p kt n", kt=DT),
                           ew2[e].rearrange("(kt p) n -> p kt n", kt=DT))
                    em.dma("gp", W3s[:].rearrange("p (kt n) -> p kt n", kt=16),
                           ew3[e].rearrange("(kt p) n -> p kt n", kt=16))
                else:
                    em.dma("gp", W1s[:].rearrange("p (kt n) -> p kt n", kt=DT),
                           sw1.rearrange("(kt p) n -> p kt n", kt=DT))
                    em.dma("gp", W2s[:].rearrange("p (kt n) -> p kt n", kt=DT),
                           sw2.rearrange("(kt p) n -> p kt n", kt=DT))
                    em.dma("gp", W3s[:].rearrange("p (kt n) -> p kt n", kt=16),
                           sw3.rearrange("(kt p) n -> p kt n", kt=16))
                wdma[0] = em.mark_dma("gp")

            load_weights(0)
            for e in range(9):
                em.wait_dma("tensor", "gp", value=wdma[0])
                for tch in range(4):
                    em.wait("sp", "pe")
                    em.dma("sp", Xt[:].rearrange("p (dt t) -> p dt t", dt=DT),
                           fm(s_xn0o)[:, :, tch * 512:(tch + 1) * 512])
                    em.wait_dma("tensor", "sp")
                    hdone = [0] * 16
                    for ht in range(16):
                        hb = ps5 if ht % 2 == 0 else ps6
                        h2b = OB[ht % 2]
                        em.wait("tensor", "dve", lag=1)
                        for kt in range(DT):
                            em.pe(lambda h, ht=ht, kt=kt, hb=hb: h.matmul(
                                hb[:, 0:512],
                                W1s[:, kt * HMOE + ht * 128:kt * HMOE + ht * 128 + 128],
                                Xt[:, kt * 512:kt * 512 + 512],
                                start=(kt == 0), stop=(kt == DT - 1)),
                                inc=(kt == DT - 1))
                        for kt in range(DT):
                            em.pe(lambda h, ht=ht, kt=kt, h2b=h2b: h.matmul(
                                h2b[:, 0:512],
                                W2s[:, kt * HMOE + ht * 128:kt * HMOE + ht * 128 + 128],
                                Xt[:, kt * 512:kt * 512 + 512],
                                start=(kt == 0), stop=(kt == DT - 1)),
                                inc=(kt == DT - 1))
                        em.wait("scalar", "pe", lag=1)
                        em.act(lambda h, hb=hb, ht=ht: h.activation(
                            Tm.bitcast(BF16)[:, 1024 + (ht % 2) * 512:
                                             1024 + (ht % 2) * 512 + 512],
                            hb[:, 0:512], AF.Silu), inc=True)
                        em.wait("vector", "act")
                        em.wait("vector", "pe")
                        em.dve(lambda h, ht=ht, h2b=h2b: h.tensor_tensor(
                            out=Gt[:, ht * 512:(ht + 1) * 512],
                            in0=Tm.bitcast(BF16)[:, 1024 + (ht % 2) * 512:
                                                 1024 + (ht % 2) * 512 + 512],
                            in1=h2b[:, 0:512],
                            op=ALU.mult), inc=True)
                        hdone[ht] = em.mark("dve")
                    em.wait("tensor", "dve")
                    if e < 8:
                        em.pe(lambda h, e=e, tch=tch: h.matmul(
                            ps7[:, 0:512], ST[0:8, e * 128:(e + 1) * 128],
                            WEF[0:8, tch * 512:(tch + 1) * 512],
                            start=True, stop=True), inc=True)
                        em.wait("vector", "pe")
                        em.dve(lambda h: h.tensor_copy(out=WB[:, 0:512],
                                                       in_=ps7[:, 0:512]), inc=True)
                        em.wait("tensor", "dve")
                    for dth in range(2):
                        if dth == 1:
                            em.wait("tensor", "dve")
                        for ht in range(16):
                            for i in range(4):
                                dt = dth * 4 + i
                                em.pe(lambda h, ht=ht, i=i, dt=dt: h.matmul(
                                    OB[i][:, 0:512],
                                    W3s[:, ht * 1024 + dt * 128:
                                        ht * 1024 + dt * 128 + 128],
                                    Gt[:, ht * 512:(ht + 1) * 512],
                                    start=(ht == 0), stop=(ht == 15)),
                                    inc=(ht == 15 and i == 3))
                        em.wait("vector", "pe")
                        for i in range(4):
                            dt = dth * 4 + i
                            dst = ACC[:, dt * TO + tch * 512:dt * TO + tch * 512 + 512]
                            if e == 0:
                                em.dve(lambda h, i=i, dst=dst, tch=tch:
                                       h.tensor_tensor(out=dst, in0=OB[i][:, 0:512],
                                                       in1=WB[:, 0:512],
                                                       op=ALU.mult), inc=(i == 3))
                            elif e < 8:
                                em.dve(lambda h, i=i, tch=tch:
                                       h.tensor_tensor(out=Tm[:, 0:512],
                                                       in0=OB[i][:, 0:512],
                                                       in1=WB[:, 0:512],
                                                       op=ALU.mult), inc=False)
                                em.dve(lambda h, dst=dst: h.tensor_tensor(
                                    out=dst, in0=dst,
                                    in1=Tm[:, 0:512],
                                    op=ALU.add), inc=(i == 3))
                            else:
                                em.dve(lambda h, i=i, dst=dst: h.tensor_tensor(
                                    out=dst, in0=dst, in1=OB[i][:, 0:512],
                                    op=ALU.add), inc=(i == 3))
                    if tch == 3 and e < 8:
                        em.wait("gp", "pe")
                        load_weights(e + 1)
            if "ffn0" in taps:
                em.wait("gpsimd", "dve")
                em.dma("gp", fm(taps["ffn0"]),
                       ACC[:].rearrange("p (dt t) -> p dt t", dt=DT))
                em.wait_dma("gpsimd", "gp")

        em.phase(p5)
        sb_free("WEF_p5")
        return ACC

    def mk_p6(ACC):
        SC = sb("SC_p6", [128, DT * 512], F32)
        SD = sb("SD_p6", [128, DT * 512], F32)
        OC = sb("OC_p6", [128, DT * 512], BF16)

        def p6(em):
            em.wait_cc("gpsimd")
            em.dma("gp", ST[0:1, 0:256].rearrange("o (a n) -> o a n", a=2),
                   cc_z_o.rearrange("(a o) n -> o a n", a=2))
            em.wait_dma("vector", "gp")
            em.dve(lambda h: h.tensor_tensor(out=ST[0:1, 512:640],
                                             in0=ST[0:1, 0:128],
                                             in1=ST[0:1, 128:256], op=ALU.add),
                   inc=True)
            em.dve(lambda h: h.tensor_single_scalar(out=ST[0:1, 512:640],
                                                    in_=ST[0:1, 512:640],
                                                    scalar=1e-20, op=ALU.max),
                   inc=True)
            em.dve(lambda h: h.reciprocal(out=ST[0:1, 640:768],
                                          in_=ST[0:1, 512:640]), inc=True)
            em.wait("tensor", "dve")
            em.pe(lambda h: h.matmul(ps5[:, 0:128], c_onesr_f[:], ST[0:1, 640:768],
                                     start=True, stop=True), inc=True)
            em.wait("vector", "pe")
            em.dve(lambda h: h.tensor_copy(out=c_small[:, 3:4], in_=ps5[:, 0:1]),
                   inc=True)
            for tch in range(4):
                em.wait("sp", "pe")
                em.wait("sp", "dve")
                em.wait("sp", "act")
                em.dma("sp", SC[:].rearrange("p (dt t) -> p dt t", dt=DT),
                       fm(x_own)[:, :, tch * 512:(tch + 1) * 512])
                em.dma("sp", OC[:].rearrange("p (dt t) -> p dt t", dt=DT),
                       fm(s_oun)[:, :, tch * 512:(tch + 1) * 512])
                em.wait_dma("vector", "sp")
                em.wait_dma("tensor", "sp")
                em.wait_dma("scalar", "sp")
                for dt in range(DT):
                    em.dve(lambda h, dt=dt: h.scalar_tensor_tensor(
                        out=SC[:, dt * 512:(dt + 1) * 512],
                        in0=OC[:, dt * 512:(dt + 1) * 512],
                        scalar=c_small[:, 3:4],
                        in1=SC[:, dt * 512:(dt + 1) * 512],
                        op0=ALU.mult, op1=ALU.add), inc=True)
                    em.dve(lambda h, dt=dt, tch=tch: h.tensor_tensor(
                        out=SC[:, dt * 512:(dt + 1) * 512],
                        in0=SC[:, dt * 512:(dt + 1) * 512],
                        in1=ACC[:, dt * TO + tch * 512:dt * TO + tch * 512 + 512],
                        op=ALU.add), inc=True)
                em.wait("sync", "dve")
                em.dma("sp", fm(s_x1)[:, :, tch * 512:(tch + 1) * 512],
                       SC[:].rearrange("p (dt t) -> p dt t", dt=DT))

                def dst(em, dt, off, bw, tch=tch):
                    em.dve(lambda h, dt=dt, off=off, bw=bw: h.tensor_copy(
                        out=OC[:, dt * 512 + off:dt * 512 + off + bw],
                        in_=SD[:, dt * 512 + off:dt * 512 + off + bw]), inc=True)

                em.wait("tensor", "dve")
                em.wait("scalar", "dve")
                ln_chunk(em, SC, SD, 512, dst)
                em.wait("sync", "dve")
                em.dma("sp", fm(cc_x_i)[:, :, tch * 512:(tch + 1) * 512],
                       OC[:].rearrange("p (dt t) -> p dt t", dt=DT))
                em.wait("scalar", "dve")
                for dt in range(DT):
                    sl = slice(dt * 512, (dt + 1) * 512)
                    em.act(lambda h, sl=sl: h.activation(
                        SC[:, sl], SD[:, sl], AF.Identity, scale=16.0,
                        bias=c_small[:, 5:6]), inc=True)
                em.wait("vector", "act")
                em.wait_dma("vector", "sp")
                for dt in range(DT):
                    sl = slice(dt * 512, (dt + 1) * 512)
                    em.dve(lambda h, sl=sl: h.tensor_scalar(
                        out=SC[:, sl], in0=SC[:, sl], scalar1=MAGIC, scalar2=127.0,
                        op0=ALU.subtract, op1=ALU.min), inc=True)
                    em.dve(lambda h, sl=sl: h.tensor_scalar(
                        out=OC[:, sl], in0=SC[:, sl], scalar1=-128.0, scalar2=0.0625,
                        op0=ALU.max, op1=ALU.mult), inc=True)
                em.wait("sync", "dve")
                em.dma("sp", fm(s_xq)[:, :, tch * 512:(tch + 1) * 512],
                       OC[:].rearrange("p (dt t) -> p dt t", dt=DT))
                if "x1" in taps:
                    em.dma("sp", fm(taps["x1"])[:, :, tch * 512:(tch + 1) * 512],
                           SC[:, 0:0] if False else
                           fm(s_x1)[:, :, tch * 512:(tch + 1) * 512])
            em.wait_dma("gpsimd", "sp")
            em.collective("AllGather", ALU.bypass, [cc_x_i.opt()], [cc_x_o.opt()])

        em.phase(p6)
        sb_free("SC_p6")

    def mk_p7():
        W1q = sb("W1q_p7", [128, DT * 4096], BF16)
        W2q = sb("W2q_p7", [128, 32 * 1024], BF16)
        CB = sb("CB_p7", [128, 48], F32)

        def quant_weights(em):
            em.dma("sp", CB[:, 0:32], b1f)
            em.dma("sp", CB[:, 32:40], b2f)
            em.wait_dma("scalar", "sp")
            em.wait_dma("vector", "sp")
            em.act(lambda h: h.activation(CB[:, 0:32], CB[:, 0:32], AF.Identity,
                                          scale=64.0, bias=c_small[:, 5:6]),
                   inc=True)
            em.act(lambda h: h.activation(CB[:, 32:40], CB[:, 32:40], AF.Identity,
                                          scale=64.0, bias=c_small[:, 5:6]),
                   inc=True)
            em.wait("vector", "act")
            em.dve(lambda h: h.tensor_scalar(out=CB[:, 0:32], in0=CB[:, 0:32],
                                             scalar1=MAGIC, scalar2=127.0,
                                             op0=ALU.subtract, op1=ALU.min),
                   inc=True)
            em.dve(lambda h: h.tensor_scalar(out=CB[:, 32:40], in0=CB[:, 32:40],
                                             scalar1=MAGIC, scalar2=127.0,
                                             op0=ALU.subtract, op1=ALU.min), inc=True)
            em.dve(lambda h: h.tensor_single_scalar(out=CB[:, 0:32], in_=CB[:, 0:32],
                                                    scalar=-128.0, op=ALU.max),
                   inc=True)
            em.dve(lambda h: h.tensor_single_scalar(out=CB[:, 32:40],
                                                    in_=CB[:, 32:40],
                                                    scalar=-128.0, op=ALU.max),
                   inc=True)
            em.wait("scalar", "dve")
            em.act(lambda h: h.activation(CB[:, 0:32], CB[:, 0:32], AF.Identity,
                                          scale=0.25, bias=c_small[:, 5:6]),
                   inc=True)
            em.act(lambda h: h.activation(CB[:, 32:40], CB[:, 32:40], AF.Identity,
                                          scale=1.0 / 64.0), inc=True)
            for src, dstW, nkt, ncols in ((w1t, W1q, DT, 4096), (w2t, W2q, 32, 1024)):
                nch = (nkt * ncols) // 4096
                for ch in range(nch):
                    em.wait("sp", "act")
                    em.wait("sp", "dve")
                    cpk = 4096 // nkt
                    em.dma("sp", SC[:].rearrange("p (kt t) -> p kt t", kt=nkt),
                           src.rearrange("(kt p) n -> p kt n", kt=nkt)
                           [:, :, ch * cpk:(ch + 1) * cpk])
                    em.wait_dma("scalar", "sp")
                    em.act(lambda h: h.activation(SC[:], SC[:], AF.Identity,
                                                  scale=64.0, bias=c_small[:, 5:6]),
                           inc=True)
                    em.wait("vector", "act")
                    em.dve(lambda h: h.tensor_scalar(out=SC[:], in0=SC[:],
                                                     scalar1=MAGIC, scalar2=127.0,
                                                     op0=ALU.subtract, op1=ALU.min),
                           inc=True)
                    em.dve(lambda h, dstW=dstW, nkt=nkt, ch=ch, cpk=cpk:
                           h.tensor_scalar(
                               out=dstW[:].rearrange("p (kt t) -> p kt t", kt=nkt)
                               [:, :, ch * cpk:(ch + 1) * cpk],
                               in0=SC[:].rearrange("p (kt t) -> p kt t", kt=nkt),
                               scalar1=-128.0, scalar2=1.0 / 64.0,
                               op0=ALU.max, op1=ALU.mult), inc=True)

        def p7main(em):
            Xq = SD.bitcast(BF16)
            Hq = SC.bitcast(BF16)
            em.wait("tensor", "dve")
            for tch in range(4):
                em.wait("sp", "pe")
                em.wait("sp", "dve")
                em.dma("sp", Xq[:, 0:DT * 512].rearrange("p (dt t) -> p dt t", dt=DT),
                       fm(s_xq)[:, :, tch * 512:(tch + 1) * 512])
                em.wait_dma("tensor", "sp")
                em.wait("vector", "pe")
                for ht in range(32):
                    hb = ps5 if ht % 2 == 0 else ps6
                    em.wait("tensor", "dve", lag=1)
                    for kt in range(DT):
                        em.pe(lambda h, ht=ht, kt=kt, hb=hb: h.matmul(
                            hb[:, 0:512],
                            W1q[:, kt * 4096 + ht * 128:kt * 4096 + ht * 128 + 128],
                            Xq[:, kt * 512:kt * 512 + 512],
                            start=(kt == 0), stop=(kt == DT - 1)),
                            inc=(kt == DT - 1))
                    em.wait("scalar", "pe")
                    em.act(lambda h, hb=hb, ht=ht: h.activation(
                        ST[:, 0:512], hb[:, 0:512], AF.Identity,
                        scale=16.0, bias=CB[:, ht:ht + 1]), inc=True)
                    em.wait("vector", "act")
                    em.dve(lambda h: h.tensor_scalar(
                        out=ST[:, 512:1024], in0=ST[:, 0:512],
                        scalar1=MAGIC, scalar2=127.0,
                        op0=ALU.subtract, op1=ALU.min), inc=True)
                    em.dve(lambda h, ht=ht: h.tensor_scalar(
                        out=HQ[:, ht * 512:(ht + 1) * 512], in0=ST[:, 512:1024],
                        scalar1=0.0, scalar2=0.0625,
                        op0=ALU.max, op1=ALU.mult), inc=True)
                em.wait("tensor", "dve")
                for dth in range(2):
                    if dth == 1:
                        em.wait("tensor", "act")
                    for ht in range(32):
                        for i in range(4):
                            dt = dth * 4 + i
                            em.pe(lambda h, ht=ht, i=i, dt=dt: h.matmul(
                                OB[i][:, 0:512],
                                W2q[:, ht * 1024 + dt * 128:ht * 1024 + dt * 128 + 128],
                                HQ[:, ht * 512:(ht + 1) * 512],
                                start=(ht == 0), stop=(ht == 31)),
                                inc=(ht == 31 and i == 3))
                    em.wait("scalar", "pe")
                    for i in range(4):
                        dt = dth * 4 + i
                        em.act(lambda h, i=i, dt=dt: h.activation(
                            SC[:, i * 512:(i + 1) * 512], OB[i][:, 0:512],
                            AF.Identity, scale=1.0,
                            bias=CB[:, 32 + dt:32 + dt + 1]), inc=(i == 3))
                    em.wait("sync", "act")
                    em.dma("sp", fm(s_ffn1)[:, dth * 4:dth * 4 + 4,
                                            tch * 512:(tch + 1) * 512],
                           SC[:, 0:2048].rearrange("p (dt t) -> p dt t", dt=4))
                    em.wait_dma("scalar", "sp")
            if "ffn1" in taps:
                em.dma("sp", taps["ffn1"], s_ffn1)
                em.wait_dma("sync", "sp")

        SC = sb("SC_p7", [128, 4096], F32)
        SD = sb("SD_p7", [128, 4096], F32)

        def p7(em):
            quant_weights(em)
            p7main(em)

        HQ = sb("HQ_p7", [128, 32 * 512], BF16)
        em.phase(p7)
        sb_free("W1q_p7")

    LPAD = PAD
    CTW = LPAD + T

    def mk_p8():
        Vc = sb("Vc_p8", [128, 4 * CTW], BF16)
        PFM = sb("PFM_p8", [10, T], F32)
        PB = sb("PB_p8", [2, T], BF16)
        CGT = sb("CGT_p8", [128, 8], F32)
        Xf = sb("Xf_p8", [128, DT * T], BF16)
        Wp = sb("Wp_p8", [128, DT * 512], BF16)
        DG = sb("DG_p8", [128, 4 * 128], BF16)
        CVW = sb("CVW_p8", [128, 16], F32)
        IDB = sb("IDB_p8", [128, 128], BF16)
        LIN = sb("LIN_p8", [128, 4 * CTW], BF16)
        OUT = sb("OUT_p8", [128, 4 * CTW], BF16)

        def p8(em):
            em.wait_cc("gpsimd")
            srcmap = {0: (0, 0), 1: (1, 0), 2: (1, 1), 3: (0, 1)}
            for tq in range(4):
                r, pos = srcmap[tq]
                em.dma("gp", Xf[:].rearrange("p (dt t) -> p dt t", dt=DT)
                       [:, :, tq * 1024:(tq + 1) * 1024],
                       cc_x_o[r * D:(r + 1) * D, pos * 1024:(pos + 1) * 1024]
                       .rearrange("(dt p) t -> p dt t", dt=DT))
            em.dma("sp", IDB[:], ident_b)
            em.wait_dma("vector", "gp")
            em.wait_dma("vector", "sp")

            if "xn1f" in taps:
                em.wait_dma("gp", "gp")
                em.dma("gp", fm(taps["xn1f"]),
                       Xf[:].rearrange("p (dt t) -> p dt t", dt=DT))
                em.wait_dma("gp", "gp")
            def proj_conv(wten, cvten, dst, do_norm, spill=None):
                em.wait("gp", "pe")
                em.dma("gp", Wp[:].rearrange("p (kt n) -> p kt n", kt=DT),
                       wten.rearrange("(kt p) n -> p kt n", kt=DT))
                em.dma("sp", CVW[:], cvten)
                em.wait("vector", "pe")
                em.dve(lambda h: h.memset(LIN[:].rearrange(
                    "p (ct t) -> p ct t", ct=4)[:, :, 0:LPAD], 0.0), inc=True)
                em.dve(lambda h, dst=dst: h.memset(dst[:].rearrange(
                    "p (ct t) -> p ct t", ct=4)[:, :, 0:LPAD], 0.0), inc=True)
                em.wait_dma("tensor", "gp")
                em.wait_dma("tensor", "sp")
                em.wait("tensor", "dve")
                for ct in range(4):
                    for tch in range(8):
                        bank = ps5 if (ct * 8 + tch) % 2 == 0 else ps6
                        em.wait("tensor", "act", lag=1)
                        for kt in range(DT):
                            em.pe(lambda h, ct=ct, tch=tch, kt=kt, bank=bank:
                                  h.matmul(
                                      bank[:, 0:512],
                                      Wp[:, kt * 512 + ct * 128:
                                         kt * 512 + ct * 128 + 128],
                                      Xf[:, kt * T + tch * 512:
                                         kt * T + tch * 512 + 512],
                                      start=(kt == 0), stop=(kt == DT - 1)),
                                  inc=(kt == DT - 1))
                        em.wait("scalar", "pe")
                        em.act(lambda h, ct=ct, tch=tch, bank=bank: h.activation(
                            LIN[:, ct * CTW + LPAD + tch * 512:
                                ct * CTW + LPAD + tch * 512 + 512],
                            bank[:, 0:512], AF.Identity), inc=True)
                if DBG_SPILL_LIN and spill is not None and "wdump" in taps:
                    em.wait("gpsimd", "pe")
                    em.dma("gp", taps["wdump"], Wp[:, 0:4096])
                if DBG_SPILL_LIN and spill is not None:
                    em.wait("sync", "act")
                    em.dma("sp", spill.rearrange("(ct p) t -> p ct t", ct=4),
                           LIN[:].rearrange("p (ct t) -> p ct t", ct=4)
                           [:, :, LPAD:LPAD + T])
                    em.wait_dma("tensor", "sp")
                    em.wait_dma("vector", "sp")
                em.wait("vector", "act")
                for ct in range(4):
                    em.wait("vector", "pe")
                    for j in range(4):
                        em.dve(lambda h, ct=ct, j=j: h.tensor_single_scalar(
                            out=DG[:, j * 128:j * 128 + 128],
                            in_=IDB[:], scalar=CVW[:, ct * 4 + j:ct * 4 + j + 1],
                            op=ALU.mult), inc=(j == 3))
                    for tch in range(8):
                        bank = ps5 if tch % 2 == 0 else ps6
                        em.wait("tensor", "act", lag=1)
                        em.wait("tensor", "dve")
                        for j in range(4):
                            em.pe(lambda h, ct=ct, tch=tch, j=j, bank=bank:
                                  h.matmul(
                                      bank[:, 0:512],
                                      DG[:, j * 128:j * 128 + 128],
                                      LIN[:, ct * CTW + LPAD + tch * 512 - 3 + j:
                                          ct * CTW + LPAD + tch * 512 - 3 + j + 512],
                                      start=(j == 0), stop=(j == 3)),
                                  inc=(j == 3))
                        em.wait("scalar", "pe")
                        em.act(lambda h, ct=ct, tch=tch, bank=bank, dst=dst:
                               h.activation(
                                   dst[:, ct * CTW + LPAD + tch * 512:
                                       ct * CTW + LPAD + tch * 512 + 512],
                                   bank[:, 0:512], AF.Silu), inc=True)
                em.wait("vector", "act")
                if do_norm:
                    for hh in range(2):
                        for tch in range(8):
                            em.wait("scalar", "pe")
                            for k in range(2):
                                ct = hh * 2 + k
                                em.act(lambda h, ct=ct, tch=tch, k=k: h.activation(
                                    LIN[:, k * 512:(k + 1) * 512],
                                    dst[:, ct * CTW + LPAD + tch * 512:
                                        ct * CTW + LPAD + tch * 512 + 512],
                                    AF.Square), inc=(k == 1))
                            em.wait("tensor", "act")
                            for k in range(2):
                                em.pe(lambda h, k=k: h.matmul(
                                    ps7[0:1, 0:512], c_ones_b[:],
                                    LIN[:, k * 512:(k + 1) * 512],
                                    start=(k == 0), stop=(k == 1)), inc=(k == 1))
                            em.wait("vector", "pe")
                            em.dve(lambda h: h.tensor_single_scalar(
                                out=ST[0:1, 0:512], in_=ps7[0:1, 0:512],
                                scalar=1e-6, op=ALU.add), inc=True)
                            em.wait("scalar", "dve")
                            em.act(lambda h: h.activation(ST[0:1, 512:1024],
                                                          ST[0:1, 0:512], AF.Sqrt),
                                   inc=True)
                            em.wait("vector", "act")
                            em.dve(lambda h: h.reciprocal(out=ST[0:1, 0:512],
                                                          in_=ST[0:1, 512:1024]),
                                   inc=True)
                            em.wait("tensor", "dve")
                            em.pe(lambda h: h.matmul(ps7[:, 0:512], c_onesr_f[:],
                                                     ST[0:1, 0:512],
                                                     start=True, stop=True),
                                  inc=True)
                            em.wait("vector", "pe")
                            for k in range(2):
                                ct = hh * 2 + k
                                sl = slice(ct * CTW + LPAD + tch * 512,
                                           ct * CTW + LPAD + tch * 512 + 512)
                                em.dve(lambda h, sl=sl: h.tensor_tensor(
                                    out=dst[:, sl], in0=dst[:, sl],
                                    in1=ps7[:, 0:512], op=ALU.mult), inc=(k == 1))
                            em.wait("tensor", "dve")
                            em.wait("scalar", "dve")
                if spill is not None and not DBG_SPILL_LIN:
                    em.wait("sync", "act")
                    em.wait("sync", "dve")
                    em.dma("sp", spill.rearrange("(ct p) t -> p ct t", ct=4),
                           dst[:].rearrange("p (ct t) -> p ct t", ct=4)
                           [:, :, LPAD:LPAD + T])

            proj_conv(wq1, cvq, OUT, True, spill=s_qc)
            em.wait_dma("tensor", "sp")
            proj_conv(wk1, cvk, OUT, True, spill=s_kc)
            em.wait_dma("tensor", "sp")
            proj_conv(wv1, cvv, Vc, False, spill=None)

            em.wait("gp", "pe")
            em.dma("gp", Wp[:, 0:DT * 2].rearrange("p (kt n) -> p kt n", kt=DT),
                   wb1.rearrange("(kt p) n -> p kt n", kt=DT))
            em.wait_dma("tensor", "gp")
            for tt8 in range(8):
                em.wait("tensor", "act")
                for kt in range(DT):
                    em.pe(lambda h, tt8=tt8, kt=kt: h.matmul(
                        ps5[0:2, 0:512],
                        Wp[:, kt * 2:kt * 2 + 2],
                        Xf[:, kt * T + tt8 * 512:kt * T + tt8 * 512 + 512],
                        start=(kt == 0), stop=(kt == DT - 1)),
                        inc=(kt == DT - 1))
                em.wait("scalar", "pe")
                em.act(lambda h, tt8=tt8: h.activation(
                    PB[0:2, tt8 * 512:tt8 * 512 + 512], ps5[0:2, 0:512],
                    AF.Sigmoid), inc=True)
            if "vc1" in taps:
                em.wait("gpsimd", "act")
                em.wait("gpsimd", "dve")
                em.dma("gp", taps["vc1"].rearrange("(ct p) t -> p ct t", ct=4),
                       Vc[:].rearrange("p (ct t) -> p ct t", ct=4)
                       [:, :, LPAD:LPAD + T])
            if "qc1" in taps:
                em.wait_dma("gp", "sp")
                em.dma("gp", taps["qc1"], s_qc)
            if "kc1" in taps:
                em.wait_dma("gp", "sp")
                em.dma("gp", taps["kc1"], s_kc)

            em.wait("vector", "act")
            em.dve(lambda h: h.memset(ST[0:1, 0:128], 0.0), inc=True)
            em.wait("sp", "dve")
            em.dma("sp", ST[0:1, 0:2], ltmp)
            em.dma("sp", ST[0:1, 128:130], epsp)
            em.wait_dma("scalar", "sp")
            em.act(lambda h: h.activation(ST[0:1, 256:384], ST[0:1, 0:128],
                                          AF.Exp), inc=True)
            em.wait("vector", "act")
            em.dve(lambda h: h.reciprocal(out=ST[0:1, 0:128], in_=ST[0:1, 256:384]),
                   inc=True)
            em.dve(lambda h: h.tensor_scalar(out=ST[0:1, 128:256],
                                             in0=ST[0:1, 128:256],
                                             scalar1=0.0, scalar2=0.2,
                                             op0=ALU.max, op1=ALU.min), inc=True)
            em.dve(lambda h: h.tensor_scalar(out=ST[0:1, 384:512],
                                             in0=ST[0:1, 128:256],
                                             scalar1=-5.0, scalar2=1.0,
                                             op0=ALU.mult, op1=ALU.add), inc=True)
            em.wait("tensor", "dve")
            for src_off, dst_col in ((0, 0), (128, 4), (384, 2)):
                em.pe(lambda h, src_off=src_off: h.matmul(
                    ps7[:, 0:128], c_onesr_f[:], ST[0:1, src_off:src_off + 128],
                    start=True, stop=True), inc=True)
                em.wait("vector", "pe")
                em.dve(lambda h, dst_col=dst_col: h.tensor_copy(
                    out=CGT[:, dst_col:dst_col + 2], in_=ps7[:, 0:2]), inc=True)
                em.wait("tensor", "dve")

            em.wait("gp", "pe")
            em.dma("gp", Wp[:, 0:DT * 16].rearrange("p (kt n) -> p kt n", kt=DT)
                   [:, :, 0:10], gw1.rearrange("(kt p) n -> p kt n", kt=DT))
            em.dma("sp", ST[0:1, 0:10], gb1)
            em.wait_dma("tensor", "sp")
            em.pe(lambda h: h.matmul(ps7[:, 0:128], c_onesr_f[:],
                                     ST[0:1, 0:128], start=True, stop=True),
                  inc=True)
            em.wait("vector", "pe")
            em.dve(lambda h: h.tensor_copy(out=c_small[:, 6:16], in_=ps7[:, 0:10]),
                   inc=True)
            em.wait_dma("tensor", "gp")
            Tg = ST
            bs_ = lambda sid: 64 * sid
            lg_ = lambda t: Tg[:, bs_(t * 2):bs_(t * 2) + 10]
            g5_ = lambda sid: Tg[:, bs_(sid) + 32:bs_(sid) + 37]
            mx_ = lambda sid: Tg[:, bs_(sid) + 16:bs_(sid) + 17]
            nmx_ = lambda sid: Tg[:, bs_(sid) + 17:bs_(sid) + 18]
            sm_ = lambda sid: Tg[:, bs_(sid) + 18:bs_(sid) + 19]
            rs_ = lambda sid: Tg[:, bs_(sid) + 19:bs_(sid) + 20]
            ex_ = lambda sid: Tg[:, bs_(sid) + 24:bs_(sid) + 29]
            ou_ = lambda t, hh: Tg[:, 512 + t * 16 + hh * 5:
                                   512 + t * 16 + hh * 5 + 5]
            ou10_ = lambda t: Tg[:, 512 + t * 16:512 + t * 16 + 10]
            pbank = [ps7, ps5]
            for tp in range(16):
                em.wait("tensor", "dve")
                for t in range(2):
                    for kt in range(DT):
                        em.pe(lambda h, tp=tp, t=t, kt=kt: h.matmul(
                            pbank[t][:, 0:10],
                            Xf[:, kt * T + (tp * 2 + t) * 128:
                               kt * T + (tp * 2 + t) * 128 + 128],
                            Wp[:, kt * 16:kt * 16 + 10],
                            start=(kt == 0), stop=(kt == DT - 1)),
                            inc=(kt == DT - 1))
                em.wait("vector", "pe")
                for t in range(2):
                    em.dve(lambda h, t=t: h.tensor_tensor(
                        out=lg_(t), in0=pbank[t][:, 0:10],
                        in1=c_small[:, 6:16], op=ALU.add), inc=(t == 1))
                ss = [(0, 0), (0, 1), (1, 0), (1, 1)]
                for t, hh in ss:
                    sid = t * 2 + hh
                    em.dve(lambda h, t=t, hh=hh, sid=sid: h.tensor_single_scalar(
                        out=g5_(sid), in_=lg_(t)[:, hh * 5:hh * 5 + 5],
                        scalar=CGT[:, hh:hh + 1], op=ALU.mult),
                        inc=(sid == 3))
                for t, hh in ss:
                    sid = t * 2 + hh
                    em.dve(lambda h, sid=sid: h.tensor_reduce(
                        out=mx_(sid), in_=g5_(sid), axis=mybir.AxisListType.X,
                        op=ALU.max), inc=(sid == 3))
                for t, hh in ss:
                    sid = t * 2 + hh
                    em.dve(lambda h, sid=sid: h.tensor_single_scalar(
                        out=nmx_(sid), in_=mx_(sid), scalar=-1.0,
                        op=ALU.mult), inc=(sid == 3))
                em.wait("scalar", "dve")
                for t, hh in ss:
                    sid = t * 2 + hh
                    em.act(lambda h, sid=sid: h.activation(
                        ex_(sid), g5_(sid), AF.Exp, bias=nmx_(sid)),
                        inc=(sid == 3))
                em.wait("vector", "act")
                for t, hh in ss:
                    sid = t * 2 + hh
                    em.dve(lambda h, sid=sid: h.tensor_reduce(
                        out=sm_(sid), in_=ex_(sid), axis=mybir.AxisListType.X,
                        op=ALU.add), inc=(sid == 3))
                for t, hh in ss:
                    sid = t * 2 + hh
                    em.dve(lambda h, sid=sid: h.reciprocal(
                        out=rs_(sid), in_=sm_(sid)), inc=(sid == 3))
                for t, hh in ss:
                    sid = t * 2 + hh
                    em.dve(lambda h, sid=sid: h.tensor_single_scalar(
                        out=ex_(sid), in_=ex_(sid), scalar=rs_(sid),
                        op=ALU.mult), inc=(sid == 3))
                for t, hh in ss:
                    sid = t * 2 + hh
                    em.dve(lambda h, sid=sid, hh=hh: h.tensor_single_scalar(
                        out=ex_(sid), in_=ex_(sid),
                        scalar=CGT[:, 2 + hh:3 + hh], op=ALU.mult),
                        inc=(sid == 3))
                for t, hh in ss:
                    sid = t * 2 + hh
                    em.dve(lambda h, sid=sid, t=t, hh=hh: h.tensor_single_scalar(
                        out=ou_(t, hh), in_=ex_(sid),
                        scalar=CGT[:, 4 + hh:5 + hh], op=ALU.add),
                        inc=(sid == 3))
                em.wait("tensor", "dve")
                for t in range(2):
                    em.pe(lambda h, t=t: h.transpose(
                        pbank[t][0:10, 128:256], ou10_(t), c_ident_f[:]),
                        inc=(t == 1))
                em.wait("vector", "pe")
                for t in range(2):
                    em.dve(lambda h, tp=tp, t=t: h.tensor_copy(
                        out=PFM[0:10, (tp * 2 + t) * 128:(tp * 2 + t) * 128 + 128],
                        in_=pbank[t][0:10, 128:256]), inc=(t == 1))
            if "probs" in taps:
                em.wait("gpsimd", "dve")
                em.dma("gp", taps["probs"], PFM[0:10, 0:T])
                em.wait_dma("gpsimd", "gp")
            em.wait_dma("gpsimd", "gp")
            em.wait_dma("sync", "sp")

        em.phase(p8)
        sb_free("Xf_p8")
        return Vc, PFM, PB, CGT

    def mk_p9(Vc, PFM, PB, CGT):
        c_selh = sb("c_selh9", [2, 256], BF16)
        c_msl = sb("c_msl9", [128, 128], F32)
        c_msu = sb("c_msu9", [128, 128], F32)
        c_tril = sb("c_tril9", [128, 128], F32)
        c_triu = sb("c_triu9", [128, 128], F32)
        FIRW = sb("FIRW_p9", [128, 4 * 81], F32)
        OACC = sb("OACC_p9", [128, 4 * TO], F32)
        ONRM = sb("ONRM_p9", [128, 4 * T], BF16)
        DGF = sb("DGF_p9", [128, 63 * 128], BF16)
        QS = sb("QS_p9", [128, 2 * 512], F32)
        KS = sb("KS_p9", [128, 2 * 512], F32)
        GAM = sb("GAM_p9", [128, 2], F32)
        CH = sb("CH_p9", [128, 16 * 128], F32)
        UW = sb("UW_p9", [128, 6 * 256], F32)
        SF = sb("SF_p9", [128, 2 * 512], F32)
        TMPF = sb("TMPF_p9", [128, 512], F32)
        PBS = sb("PBS_p9", [128, 512], F32)
        SEL10 = sb("SEL10_p9", [10, 10 * 128], F32)

        def cht(i):
            return CH[:, i * 128:(i + 1) * 128]

        UWr = UW[:, 0:512]
        UT = UW[:, 512:768]
        WT = UW[:, 768:1024]
        WF = UW[:, 1024:1280]
        KT = UW[:, 1280:1536]

        def p9(em):
            em.dma("sp", c_selh[:], selh)
            em.dma("sp", c_msl[:], msl_neg)
            em.dma("sp", c_msu[:], msu_neg)
            em.dma("sp", c_tril[:], tril_f)
            em.dma("sp", c_triu[:], triu_f)
            em.dma("sp", FIRW[:, 0:12], firs)
            em.dma("sp", FIRW[:, 12:72], firm)
            em.dma("sp", FIRW[:, 72:324], firl)
            em.dma("sp", GAM[:], gamm)
            em.dma("sp", SEL10[:], sel10)
            em.wait_dma("vector", "sp")
            em.dve(lambda h: h.memset(SF[:], 0.0), inc=True)

            def pbc_load(brow, toff, width):
                em.wait("tensor", "dve")
                em.pe(lambda h, brow=brow, toff=toff, width=width: h.matmul(
                    ps7[:, 0:width], SEL10[0:10, brow * 128:(brow + 1) * 128],
                    PFM[0:10, toff:toff + width],
                    start=True, stop=True), inc=True)
                em.wait("vector", "pe")
                em.dve(lambda h, width=width: h.tensor_copy(
                    out=PBS[:, 0:width], in_=ps7[:, 0:width]), inc=True)

            for half in range(2):
                hb = half * TO
                em.wait("vector", "pe")
                em.dve(lambda h: h.memset(OACC[:], 0.0), inc=True)
                for fi, (K, fw0) in enumerate(((3, 0), (15, 12), (63, 72))):
                    for ct in range(4):
                        em.wait("vector", "pe")
                        for j in range(K):
                            em.dve(lambda h, ct=ct, j=j, fw0=fw0, K=K:
                                   h.tensor_single_scalar(
                                       out=DGF[:, j * 128:j * 128 + 128],
                                       in_=IDB9[:],
                                       scalar=FIRW[:, fw0 + ct * K + j:
                                                   fw0 + ct * K + j + 1],
                                       op=ALU.mult), inc=(j == K - 1))
                        hh = ct // 2
                        for tch in range(4):
                            bank = OB[tch % 2]
                            pbc_load(hh * 5 + fi, hb + tch * 512, 512)
                            em.wait("tensor", "dve")
                            for j in range(K):
                                base = ct * CTW + LPAD + hb + tch * 512 - (K - 1) + j
                                em.pe(lambda h, j=j, base=base, bank=bank: h.matmul(
                                    bank[:, 0:512],
                                    DGF[:, j * 128:j * 128 + 128],
                                    Vc[:, base:base + 512],
                                    start=(j == 0), stop=(j == K - 1)),
                                    inc=(j == K - 1))
                            em.wait("vector", "pe")
                            em.dve(lambda h, bank=bank: h.tensor_tensor(
                                out=TMPF[:], in0=bank[:, 0:512], in1=PBS[:, 0:512],
                                op=ALU.mult), inc=True)
                            osl = slice(ct * TO + tch * 512, ct * TO + tch * 512 + 512)
                            em.dve(lambda h, osl=osl: h.tensor_tensor(
                                out=OACC[:, osl], in0=OACC[:, osl], in1=TMPF[:],
                                op=ALU.add), inc=True)
                for ct in range(4):
                    hh = ct // 2
                    for tch in range(4):
                        pbc_load(hh * 5 + 4, hb + tch * 512, 512)
                        em.dve(lambda h, ct=ct, tch=tch, hb=hb: h.tensor_tensor(
                            out=TMPF[:],
                            in0=Vc[:, ct * CTW + LPAD + hb + tch * 512:
                                   ct * CTW + LPAD + hb + tch * 512 + 512],
                            in1=PBS[:, 0:512], op=ALU.mult), inc=True)
                        osl = slice(ct * TO + tch * 512, ct * TO + tch * 512 + 512)
                        em.dve(lambda h, osl=osl: h.tensor_tensor(
                            out=OACC[:, osl], in0=OACC[:, osl], in1=TMPF[:],
                            op=ALU.add), inc=True)
                for hh in range(2):
                    for g in range(4):
                        gi = half * 16 + g * 4
                        em.wait("gp", "pe")
                        em.wait("gp", "dve")
                        em.dma("gp", QS[:].rearrange("p (k t) -> p k t", k=2),
                               s_qc[hh * 2 * 128:(hh * 2 + 2) * 128,
                                    gi * 128:gi * 128 + 512]
                               .rearrange("(k p) t -> p k t", k=2))
                        em.dma("gp", KS[:].rearrange("p (k t) -> p k t", k=2),
                               s_kc[hh * 2 * 128:(hh * 2 + 2) * 128,
                                    gi * 128:gi * 128 + 512]
                               .rearrange("(k p) t -> p k t", k=2))
                        em.wait_dma("tensor", "gp")
                        em.wait_dma("vector", "gp")
                        for i4 in range(4):
                            ci = gi + i4
                            qn = [QS[:, k * 512 + i4 * 128:k * 512 + i4 * 128 + 128]
                                  for k in range(2)]
                            kn = [KS[:, k * 512 + i4 * 128:k * 512 + i4 * 128 + 128]
                                  for k in range(2)]
                            vsl = [Vc[:, (hh * 2 + k) * CTW + LPAD + ci * 128:
                                      (hh * 2 + k) * CTW + LPAD + ci * 128 + 128]
                                   for k in range(2)]
                            em.wait("tensor", "dve")
                            em.pe(lambda h, ci=ci, hh=hh: h.matmul(
                                ps7[:, 0:128],
                                c_selh[0:2, hh * 128:hh * 128 + 128],
                                PB[0:2, ci * 128:ci * 128 + 128],
                                start=True, stop=True), inc=True)
                            em.wait("vector", "pe")
                            for k in range(2):
                                em.dve(lambda h, k=k, kn=kn: h.tensor_tensor(
                                    out=CH[:, (12 + k) * 128:(12 + k) * 128 + 128],
                                    in0=kn[k], in1=ps7[:, 0:128], op=ALU.mult),
                                    inc=True)
                                em.dve(lambda h, k=k, vsl=vsl: h.tensor_tensor(
                                    out=CH[:, (14 + k) * 128:(14 + k) * 128 + 128],
                                    in0=vsl[k], in1=ps7[:, 0:128], op=ALU.mult),
                                    inc=(k == 1))
                            if "dbg0" in taps and half == 0 and hh == 0 \
                                    and ci == 0:
                                em.dve(lambda h: h.tensor_copy(
                                    out=CH[:, 0:128], in_=ps7[:, 0:128]),
                                    inc=True)
                                em.dve(lambda h, kn=kn: h.tensor_copy(
                                    out=CH[:, 128:256], in_=kn[0]), inc=True)
                            kb = [CH[:, (12 + k) * 128:(12 + k) * 128 + 128]
                                  for k in range(2)]
                            vb = [CH[:, (14 + k) * 128:(14 + k) * 128 + 128]
                                  for k in range(2)]
                            em.wait("tensor", "dve")
                            for k in range(2):
                                em.pe(lambda h, k=k, kb=kb, kn=kn: h.matmul(
                                    ps5[:, 0:128], kb[k], kn[k],
                                    start=(k == 0), stop=(k == 1)), inc=(k == 1))
                            em.wait("vector", "pe")
                            em.dve(lambda h: h.tensor_tensor(
                                out=cht(2), in0=ps5[:, 0:128], in1=c_msl[:],
                                op=ALU.mult), inc=True)
                            em.wait("tensor", "dve")
                            for k in range(2):
                                em.pe(lambda h, k=k, kb=kb, kn=kn: h.matmul(
                                    ps5[:, 0:128], kn[k], kb[k],
                                    start=(k == 0), stop=(k == 1)), inc=(k == 1))
                            em.wait("vector", "pe")
                            em.dve(lambda h: h.tensor_tensor(
                                out=TMPF[:, 0:128], in0=ps5[:, 0:128],
                                in1=c_msu[:], op=ALU.mult), inc=True)
                            em.dve(lambda h: h.tensor_copy(out=cht(3),
                                                           in_=TMPF[:, 0:128]),
                                   inc=True)
                            em.dve(lambda h: h.tensor_tensor(
                                out=cht(7), in0=TMPF[:, 0:128], in1=c_ident_f[:],
                                op=ALU.add), inc=True)
                            for lev in range(1, 7):
                                em.wait("tensor", "dve")
                                em.pe(lambda h: h.matmul(ps6[:, 0:128], cht(3),
                                                         cht(2), start=True,
                                                         stop=True), inc=True)
                                em.wait("vector", "pe")
                                em.dve(lambda h: h.tensor_copy(out=cht(4),
                                                               in_=ps6[:, 0:128]),
                                       inc=True)
                                em.dve(lambda h: h.tensor_tensor(
                                    out=cht(6), in0=ps6[:, 0:128], in1=c_ident_f[:],
                                    op=ALU.add), inc=True)
                                em.wait("tensor", "dve")
                                em.pe(lambda h: h.matmul(ps6[:, 0:128], cht(2),
                                                         cht(3), start=True,
                                                         stop=True), inc=True)
                                em.wait("vector", "pe")
                                em.dve(lambda h: h.tensor_copy(out=cht(5),
                                                               in_=ps6[:, 0:128]),
                                       inc=True)
                                em.wait("tensor", "dve")
                                em.pe(lambda h: h.matmul(ps6[:, 0:128], cht(6),
                                                         cht(7), start=True,
                                                         stop=True), inc=True)
                                em.wait("vector", "pe")
                                em.dve(lambda h: h.tensor_copy(out=cht(7),
                                                               in_=ps6[:, 0:128]),
                                       inc=True)
                                em.dve(lambda h: h.tensor_copy(out=cht(2),
                                                               in_=cht(4)), inc=True)
                                em.dve(lambda h: h.tensor_copy(out=cht(3),
                                                               in_=cht(5)), inc=True)
                            for k in range(2):
                                em.wait("tensor", "dve")
                                em.pe(lambda h, k=k, vb=vb: h.transpose(
                                    ps6[:, 0:128], vb[k], c_ident_f[:]), inc=True)
                                em.wait("vector", "pe")
                                em.dve(lambda h, k=k: h.tensor_copy(
                                    out=UWr[:, k * 128:k * 128 + 128],
                                    in_=ps6[:, 0:128]), inc=True)
                                em.wait("tensor", "dve")
                                em.pe(lambda h, k=k, kb=kb: h.transpose(
                                    ps6[:, 0:128], kb[k], c_ident_f[:]), inc=True)
                                em.wait("vector", "pe")
                                em.dve(lambda h, k=k: h.tensor_copy(
                                    out=UWr[:, 256 + k * 128:256 + k * 128 + 128],
                                    in_=ps6[:, 0:128]), inc=True)
                            em.wait("tensor", "dve")
                            em.pe(lambda h: h.matmul(ps6[:, 0:512], cht(7), UWr,
                                                     start=True, stop=True),
                                  inc=True)
                            em.wait("vector", "pe")
                            em.dve(lambda h: h.tensor_copy(out=UT, in_=ps6[:, 0:256]),
                                   inc=True)
                            em.dve(lambda h: h.tensor_copy(out=WT,
                                                           in_=ps6[:, 256:512]),
                                   inc=True)
                            for k in range(2):
                                em.wait("tensor", "dve")
                                em.pe(lambda h, k=k: h.transpose(
                                    ps6[:, 0:128], WT[:, k * 128:k * 128 + 128],
                                    c_ident_f[:]), inc=True)
                                em.wait("vector", "pe")
                                em.dve(lambda h, k=k: h.tensor_copy(
                                    out=WF[:, k * 128:k * 128 + 128],
                                    in_=ps6[:, 0:128]), inc=True)
                                em.wait("tensor", "dve")
                                em.pe(lambda h, k=k, kn=kn: h.transpose(
                                    ps6[:, 0:128], kn[k], c_ident_f[:]), inc=True)
                                em.wait("vector", "pe")
                                em.dve(lambda h, k=k: h.tensor_copy(
                                    out=KT[:, k * 128:k * 128 + 128],
                                    in_=ps6[:, 0:128]), inc=True)
                            em.wait("tensor", "dve")
                            for k in range(2):
                                em.pe(lambda h, k=k, qn=qn, kn=kn: h.matmul(
                                    ps5[:, 0:128], qn[k], kn[k],
                                    start=(k == 0), stop=(k == 1)), inc=(k == 1))
                            em.wait("vector", "pe")
                            em.dve(lambda h: h.tensor_tensor(
                                out=cht(10), in0=ps5[:, 0:128], in1=c_tril[:],
                                op=ALU.mult), inc=True)
                            em.wait("tensor", "dve")
                            for k in range(2):
                                em.pe(lambda h, k=k, qn=qn, kn=kn: h.matmul(
                                    ps5[:, 0:128], kn[k], qn[k],
                                    start=(k == 0), stop=(k == 1)), inc=(k == 1))
                            em.wait("vector", "pe")
                            em.dve(lambda h: h.tensor_tensor(
                                out=cht(11), in0=ps5[:, 0:128], in1=c_triu[:],
                                op=ALU.mult), inc=True)
                            em.wait("tensor", "dve")
                            for k in range(2):
                                em.pe(lambda h, k=k, hh=hh: h.matmul(
                                    ps6[:, 0:256],
                                    WF[:, k * 128:k * 128 + 128],
                                    SF[:, hh * 512 + k * 256:hh * 512 + k * 256
                                       + 256],
                                    start=(k == 0), stop=(k == 1)), inc=(k == 1))
                            em.wait("vector", "pe")
                            em.dve(lambda h: h.tensor_tensor(
                                out=TMPF[:, 0:256], in0=UT, in1=ps6[:, 0:256],
                                op=ALU.subtract), inc=True)
                            em.dve(lambda h: h.tensor_copy(
                                out=CH[:, 1792:2048], in_=TMPF[:, 0:256]), inc=True)
                            UADJ = CH[:, 1792:2048]
                            em.wait("tensor", "dve")
                            for dvt in range(2):
                                for k in range(2):
                                    em.pe(lambda h, dvt=dvt, k=k, hh=hh, qn=qn:
                                          h.matmul(
                                              OB[dvt][:, 0:128],
                                              SF[:, hh * 512 + k * 256 + dvt * 128:
                                                 hh * 512 + k * 256 + dvt * 128
                                                 + 128],
                                              qn[k], start=(k == 0), stop=False),
                                          inc=False)
                                em.pe(lambda h, dvt=dvt: h.matmul(
                                    OB[dvt][:, 0:128],
                                    UADJ[:, dvt * 128:dvt * 128 + 128],
                                    cht(11), start=False, stop=True), inc=True)
                            for k in range(2):
                                em.pe(lambda h, k=k: h.matmul(
                                    ps6[:, 0:256],
                                    KT[:, k * 128:k * 128 + 128],
                                    UADJ[:, 0:256],
                                    start=True, stop=True), inc=True)
                                em.wait("vector", "pe")
                                ssl = slice(hh * 512 + k * 256, hh * 512 + k * 256
                                            + 256)
                                em.dve(lambda h, ssl=ssl: h.tensor_tensor(
                                    out=SF[:, ssl], in0=SF[:, ssl],
                                    in1=ps6[:, 0:256], op=ALU.add), inc=True)
                                em.wait("tensor", "dve")
                            if "dbg0" in taps and half == 0 and hh == 0 \
                                    and ci == 0:
                                em.wait("gp", "dve")
                                em.wait("gp", "pe")
                                em.dma("gp", taps["dbg0"][0:2, 0:256],
                                       PB[0:2, 0:256])
                                em.dma("gp", taps["dbg0"][:, 1024:1280],
                                       CH[:, 0:256])
                                em.dma("gp", taps["dbg0"][:, 256:1024],
                                       CH[:, 256:1024])
                                em.dma("gp", taps["dbg0"][:, 1280:2048],
                                       CH[:, 1280:2048])
                                em.dma("gp", taps["dbg0"][:, 2048:3584],
                                       UW[:, 0:1536])
                                em.dma("gp", taps["dbg0"][:, 3584:4096],
                                       SF[:, 0:512])
                                em.wait_dma("vector", "gp")
                                em.wait_dma("tensor", "gp")
                            pbc_load(hh * 5 + 3, ci * 128, 128)
                            for dvt in range(2):
                                em.dve(lambda h, dvt=dvt: h.tensor_tensor(
                                    out=TMPF[:, 256:384], in0=OB[dvt][:, 0:128],
                                    in1=PBS[:, 0:128], op=ALU.mult), inc=True)
                                osl = slice((hh * 2 + dvt) * TO + (ci - half * 16)
                                            * 128,
                                            (hh * 2 + dvt) * TO + (ci - half * 16)
                                            * 128 + 128)
                                em.dve(lambda h, osl=osl: h.tensor_tensor(
                                    out=OACC[:, osl], in0=OACC[:, osl],
                                    in1=TMPF[:, 256:384], op=ALU.add), inc=True)
                                if "dlt" in taps:
                                    em.wait("gp", "dve")
                                    em.dma("gp", taps["dlt"]
                                           [(hh * 2 + dvt) * 128:
                                            (hh * 2 + dvt + 1) * 128,
                                            ci * 128:ci * 128 + 128],
                                           TMPF[:, 256:384])
                                    em.wait_dma("vector", "gp")
                for hh in range(2):
                    for tch in range(4):
                        em.wait("scalar", "dve")
                        em.wait("scalar", "pe")
                        for k in range(2):
                            em.act(lambda h, k=k, hh=hh, tch=tch: h.activation(
                                QS[:, k * 512:(k + 1) * 512],
                                OACC[:, (hh * 2 + k) * TO + tch * 512:
                                     (hh * 2 + k) * TO + tch * 512 + 512],
                                AF.Square), inc=(k == 1))
                        em.wait("tensor", "act")
                        for k in range(2):
                            em.pe(lambda h, k=k: h.matmul(
                                ps7[0:1, 0:512], c_ones_f[:],
                                QS[:, k * 512:(k + 1) * 512],
                                start=(k == 0), stop=(k == 1)), inc=(k == 1))
                        em.wait("vector", "pe")
                        em.dve(lambda h: h.tensor_scalar(
                            out=ST[0:1, 0:512], in0=ps7[0:1, 0:512],
                            scalar1=1.0 / 256.0, scalar2=1e-5,
                            op0=ALU.mult, op1=ALU.add), inc=True)
                        em.wait("scalar", "dve")
                        em.act(lambda h: h.activation(ST[0:1, 512:1024],
                                                      ST[0:1, 0:512], AF.Sqrt),
                               inc=True)
                        em.wait("vector", "act")
                        em.dve(lambda h: h.reciprocal(out=ST[0:1, 0:512],
                                                      in_=ST[0:1, 512:1024]),
                               inc=True)
                        em.wait("tensor", "dve")
                        em.pe(lambda h: h.matmul(ps7[:, 0:512], c_onesr_f[:],
                                                 ST[0:1, 0:512],
                                                 start=True, stop=True), inc=True)
                        em.wait("vector", "pe")
                        for k in range(2):
                            ct = hh * 2 + k
                            em.dve(lambda h, ct=ct, tch=tch: h.tensor_tensor(
                                out=TMPF[:],
                                in0=OACC[:, ct * TO + tch * 512:ct * TO + tch * 512
                                        + 512],
                                in1=ps7[:, 0:512], op=ALU.mult), inc=True)
                            em.dve(lambda h, ct=ct, tch=tch, hb=hb:
                                   h.tensor_single_scalar(
                                       out=ONRM[:, ct * T + hb + tch * 512:
                                               ct * T + hb + tch * 512 + 512],
                                       in_=TMPF[:], scalar=GAM[:, k:k + 1],
                                       op=ALU.mult), inc=(k == 1))
                        em.wait("tensor", "dve")
                em.wait("sync", "dve")
                for tq in (half * 2, half * 2 + 1):
                    pos = {0: 0, 1: 2, 2: 3, 3: 1}[tq]
                    em.dma("sp", cc_o_i[:, pos * 1024:(pos + 1) * 1024]
                           .rearrange("(ct p) t -> p ct t", ct=4),
                           ONRM[:].rearrange("p (ct t) -> p ct t", ct=4)
                           [:, :, tq * 1024:(tq + 1) * 1024])
            em.wait_dma("gpsimd", "sp")
            em.collective("AllGather", ALU.bypass, [cc_o_i.opt()], [cc_o_o.opt()])
            if "onrm" in taps:
                em.wait("gpsimd", "dve")
                em.dma("gp", taps["onrm"].rearrange("(ct p) t -> p ct t", ct=4),
                       ONRM[:].rearrange("p (ct t) -> p ct t", ct=4))
                em.wait_dma("gpsimd", "gp")

        IDB9 = sb("IDB9_p9", [128, 128], BF16)

        def p9pre(em):
            em.dma("sp", IDB9[:], ident_b)
            em.wait_dma("sync", "sp")

        em.phase(p9pre)
        em.phase(p9)
        sb_free("c_msl9")

    def mk_p10():
        OGA = sb("OGA_p10", [128, 8 * T], BF16)
        OSL = sb("OSL_p10", [128, 8 * TO], BF16)
        WOT = sb("WOT_p10", [128, 8 * 1024], BF16)
        SC = sb("SC_p10", [128, 4096], F32)
        SD = sb("SD_p10", [128, 4096], F32)

        def p10(em):
            em.wait_cc("gpsimd")
            em.dma("gp", OGA[:].rearrange("p (ct t) -> p ct t", ct=8),
                   cc_o_o.rearrange("(ct p) t -> p ct t", ct=8))
            em.dma("gp", WOT[:].rearrange("p (kt n) -> p kt n", kt=DT),
                   wo1.rearrange("(kt p) n -> p kt n", kt=DT))
            em.wait_dma("vector", "gp")
            for ct in range(8):
                for tch in range(4):
                    em.dve(lambda h, ct=ct, tch=tch: h.tensor_single_scalar(
                        out=OSL[:, ct * TO + tch * 512:ct * TO + tch * 512 + 512],
                        in_=OGA[:, ct * T + TO + tch * 512:
                               ct * T + TO + tch * 512 + 512],
                        scalar=c_sel[:, 1:2], op=ALU.mult), inc=True)
                    em.dve(lambda h, ct=ct, tch=tch: h.scalar_tensor_tensor(
                        out=OSL[:, ct * TO + tch * 512:ct * TO + tch * 512 + 512],
                        in0=OGA[:, ct * T + tch * 512:ct * T + tch * 512 + 512],
                        scalar=c_sel[:, 0:1],
                        in1=OSL[:, ct * TO + tch * 512:ct * TO + tch * 512 + 512],
                        op0=ALU.mult, op1=ALU.add), inc=True)
            for tch in range(4):
                em.wait("sp", "dve")
                em.dma("sp", SC[:].rearrange("p (dt t) -> p dt t", dt=DT),
                       fm(s_x1)[:, :, tch * 512:(tch + 1) * 512])
                em.dma("sp", SD[:].rearrange("p (dt t) -> p dt t", dt=DT),
                       fm(s_ffn1)[:, :, tch * 512:(tch + 1) * 512])
                em.wait_dma("vector", "sp")
                for dt in range(DT):
                    bank = ps5 if dt % 2 == 0 else ps6
                    em.wait("tensor", "dve", lag=1)
                    for ct in range(8):
                        em.pe(lambda h, dt=dt, ct=ct, tch=tch, bank=bank: h.matmul(
                            bank[:, 0:512],
                            WOT[:, ct * 1024 + dt * 128:ct * 1024 + dt * 128 + 128],
                            OSL[:, ct * TO + tch * 512:ct * TO + tch * 512 + 512],
                            start=(ct == 0), stop=(ct == 7)), inc=(ct == 7))
                    em.wait("vector", "pe")
                    em.dve(lambda h, dt=dt, bank=bank: h.tensor_tensor(
                        out=SC[:, dt * 512:(dt + 1) * 512],
                        in0=SC[:, dt * 512:(dt + 1) * 512], in1=bank[:, 0:512],
                        op=ALU.add), inc=True)
                    em.dve(lambda h, dt=dt: h.tensor_tensor(
                        out=SC[:, dt * 512:(dt + 1) * 512],
                        in0=SC[:, dt * 512:(dt + 1) * 512],
                        in1=SD[:, dt * 512:(dt + 1) * 512],
                        op=ALU.add), inc=True)
                em.wait("sync", "dve")
                em.dma("sp", fm(y)[:, :, tch * 512:(tch + 1) * 512],
                       SC[:].rearrange("p (dt t) -> p dt t", dt=DT))
            em.wait_dma("sync", "sp")
            em.wait_dma("gpsimd", "gp")

        em.phase(p10)
        sb_free("OGA_p10")

    if STAGE >= 0:
        em.phase(p0)
    if STAGE >= 1:
        mk_p1()
    Ak = Av = None
    if STAGE >= 2:
        Ak = mk_p2()
        Av = mk_p3()
    if STAGE >= 4:
        mk_p4(Ak, Av)
        sb_free("Ak")
        sb_free("c_rowbc", side="right")
    if STAGE >= 5:
        ACC = mk_p5()
    if STAGE >= 6:
        mk_p6(ACC)
        sb_free("ACC_p5")
    if STAGE >= 7:
        mk_p7()
    if STAGE >= 8:
        Vc, PFM, PB, CGT = mk_p8()
    if STAGE >= 9:
        mk_p9(Vc, PFM, PB, CGT)
        sb_free("Vc_p8")
    if STAGE >= 10:
        mk_p10()

    def p4dbg_mk():
        Abuf = sb("Abuf_dbg", [128, DT * 512], BF16)
        SCd = sb("SCd_dbg", [128, DT * 512], F32)

        def p4dbg(em):
            em.wait_cc("gpsimd")
            em.dma("gp", ST[0:1, 0:256].rearrange("o (a n) -> o a n", a=2),
                   cc_z_o.rearrange("(a o) n -> o a n", a=2))
            em.wait_dma("vector", "gp")
            em.dve(lambda h: h.tensor_tensor(out=ST[0:1, 512:640],
                                             in0=ST[0:1, 0:128],
                                             in1=ST[0:1, 128:256], op=ALU.add),
                   inc=True)
            em.dve(lambda h: h.tensor_single_scalar(out=ST[0:1, 512:640],
                                                    in_=ST[0:1, 512:640],
                                                    scalar=1e-20, op=ALU.max),
                   inc=True)
            em.dve(lambda h: h.reciprocal(out=ST[0:1, 640:768],
                                          in_=ST[0:1, 512:640]), inc=True)
            em.wait("tensor", "dve")
            em.pe(lambda h: h.matmul(ps5[:, 0:128], c_onesr_f[:], ST[0:1, 640:768],
                                     start=True, stop=True), inc=True)
            em.wait("vector", "pe")
            em.dve(lambda h: h.tensor_copy(out=c_small[:, 3:4], in_=ps5[:, 0:1]),
                   inc=True)
            if "zz" in taps:
                em.dve(lambda h: h.tensor_copy(out=ST[0:1, 256:260],
                                               in_=c_small[0:1, 0:4]), inc=True)
                em.wait("sync", "dve")
                em.dma("sp", taps["zz"], ST[0:1, 0:260])
            for tch in range(TO // 512):
                em.dma("sp", Abuf[:].rearrange("p (dt t) -> p dt t", dt=DT),
                       fm(s_oun)[:, :, tch * 512:(tch + 1) * 512])
                em.wait_dma("vector", "sp")
                for dt in range(DT):
                    em.dve(lambda h, dt=dt: h.tensor_single_scalar(
                        out=SCd[:, dt * 512:(dt + 1) * 512],
                        in_=Abuf[:, dt * 512:(dt + 1) * 512],
                        scalar=c_small[:, 3:4], op=ALU.mult), inc=True)
                em.wait("sync", "dve")
                em.dma("sp", fm(taps["attn0"])[:, :, tch * 512:(tch + 1) * 512],
                       SCd[:].rearrange("p (dt t) -> p dt t", dt=DT))
                em.wait_dma("vector", "sp")
            em.wait_dma("sync", "sp")

        em.phase(p4dbg)
        sb_free("Abuf_dbg")

    if "attn0" in taps and STAGE >= 4:
        p4dbg_mk()

    def pfin_stub_mk():
        SCd = sb("SC_stub", [128, DT * 512], F32)

        def pfin_stub(em):
            em.dve(lambda h: h.memset(SCd[:], 0.0), inc=True)
            em.wait("sync", "dve")
            for tch in range(TO // 512):
                em.dma("sp", fm(y)[:, :, tch * 512:(tch + 1) * 512],
                       SCd[:].rearrange("p (dt t) -> p dt t", dt=DT))
            em.wait_dma("sync", "sp")

        em.phase(pfin_stub)
        sb_free("SC_stub")

    if STAGE < 10:
        pfin_stub_mk()

    return nc, taps



def _prep_core_inputs(inputs, core):
    b, par = core // 2, core % 2
    f32 = np.float32
    x = np.asarray(inputs["L0_attn_x"], f32)
    if par == 0:
        own = np.r_[np.arange(0, 1024), np.arange(3072, 4096)]
    else:
        own = np.arange(1024, 3072)
    d = {}
    d["x_own"] = np.ascontiguousarray(x[b][own].T)
    d["x_full"] = np.ascontiguousarray(x[b].T)
    d["row_idx"] = np.ascontiguousarray(own[None, :].astype(f32))
    d["col_idx"] = np.ascontiguousarray(
        np.arange(T, dtype=f32).reshape(T // 128, 128).T)
    d["ident_b"] = np.eye(128, dtype=ml_dtypes.bfloat16)
    d["ident_f"] = np.eye(128, dtype=f32)
    sl = np.tril(np.ones((128, 128), f32), -1)
    d["msl_neg"] = -sl
    d["msu_neg"] = -sl.T
    d["tril_f"] = np.tril(np.ones((128, 128), f32))
    d["triu_f"] = np.triu(np.ones((128, 128), f32))
    s01 = np.zeros((128, 2), f32)
    s01[:, par] = 1.0
    d["sel01"] = s01
    sh = np.zeros((2, 256), ml_dtypes.bfloat16)
    sh[0, 0:128] = 1.0
    sh[1, 128:256] = 1.0
    d["selh"] = sh
    s8 = np.zeros((8, 8, 128), f32)
    for e_ in range(8):
        s8[e_, e_, :] = 1.0
    d["sel8"] = np.ascontiguousarray(s8.reshape(8, 8 * 128))
    s10 = np.zeros((10, 10, 128), f32)
    for e_ in range(10):
        s10[e_, e_, :] = 1.0
    d["sel10"] = np.ascontiguousarray(s10.reshape(10, 10 * 128))

    d["wq0"] = np.asarray(inputs["L0_attn_Wq"], f32)
    d["wk0"] = np.asarray(inputs["L0_attn_Wk"], f32)
    d["wv0"] = np.asarray(inputs["L0_attn_Wv"], f32)
    d["bq0"] = np.ascontiguousarray(np.asarray(inputs["L0_attn_bq"], f32)
                                    .reshape(DT, 128).T)
    d["bk0"] = np.ascontiguousarray(np.asarray(inputs["L0_attn_bk"], f32)
                                    .reshape(DT, 128).T)
    d["bv0"] = np.asarray(inputs["L0_attn_bv"], f32)[None, :]
    d["gw0"] = np.asarray(inputs["L0_ffn_gate_w"], f32)
    for k_ in ("ew1", "ew2", "ew3", "sw1", "sw2", "sw3"):
        d[k_] = np.asarray(inputs["L0_ffn_" + k_], f32)

    hs = slice(par * 2 * 256, (par * 2 + 2) * 256)
    d["wq1"] = np.ascontiguousarray(np.asarray(inputs["L1_attn_Wq"], f32).T[:, hs])
    d["wk1"] = np.ascontiguousarray(np.asarray(inputs["L1_attn_Wk"], f32).T[:, hs])
    d["wv1"] = np.ascontiguousarray(np.asarray(inputs["L1_attn_Wv"], f32).T[:, hs])
    d["wb1"] = np.ascontiguousarray(
        np.asarray(inputs["L1_attn_Wb"], f32).T[:, par * 2:par * 2 + 2])
    for nm, src in (("cvq", "L1_attn_conv_q"), ("cvk", "L1_attn_conv_k"),
                    ("cvv", "L1_attn_conv_v")):
        w = np.asarray(inputs[src], f32)[hs]
        d[nm] = np.ascontiguousarray(
            w.reshape(4, 128, 4).transpose(1, 0, 2).reshape(128, 16))
    for nm, src, K in (("firs", "L1_attn_fir_s", 3), ("firm", "L1_attn_fir_m", 15),
                       ("firl", "L1_attn_fir_l", 63)):
        fw = np.asarray(inputs[src], f32)[par * 2:par * 2 + 2]
        fw = fw.reshape(512, K).reshape(4, 128, K).transpose(1, 0, 2)
        d[nm] = np.ascontiguousarray(fw.reshape(128, 4 * K))
    gw = np.asarray(inputs["L1_attn_gate_w"], f32)[par * 2:par * 2 + 2]
    d["gw1"] = np.ascontiguousarray(gw.reshape(10, D).T)
    d["gb1"] = np.ascontiguousarray(
        np.asarray(inputs["L1_attn_gate_b"], f32)[par * 2:par * 2 + 2].reshape(1, 10))
    d["ltmp"] = np.ascontiguousarray(
        np.asarray(inputs["L1_attn_log_temp"], f32)[par * 2:par * 2 + 2].reshape(1, 2))
    d["epsp"] = np.ascontiguousarray(
        np.asarray(inputs["L1_attn_eps_param"], f32)[par * 2:par * 2 + 2].reshape(1, 2))
    d["gamm"] = np.ascontiguousarray(
        np.asarray(inputs["L1_attn_norm_w"], f32).reshape(2, 128).T)
    d["wo1"] = np.ascontiguousarray(np.asarray(inputs["L1_attn_Wo"], f32).T)
    d["w1t"] = np.ascontiguousarray(np.asarray(inputs["L1_ffn_w1"], f32).T)
    d["w2t"] = np.ascontiguousarray(np.asarray(inputs["L1_ffn_w2"], f32).T)
    d["b1f"] = np.ascontiguousarray(
        np.asarray(inputs["L1_ffn_b1"], f32).reshape(32, 128).T)
    d["b2f"] = np.ascontiguousarray(
        np.asarray(inputs["L1_ffn_b2"], f32).reshape(8, 128).T)
    return d


def kernel(**inputs):
    nc, _ = build_nc()
    in_maps = [_prep_core_inputs(inputs, c) for c in range(8)]
    res = run_bass_kernel_spmd(nc, in_maps, core_ids=list(range(8)))
    out = np.zeros((B, T, D), np.float32)
    for c in range(8):
        b, par = c // 2, c % 2
        if par == 0:
            own = np.r_[np.arange(0, 1024), np.arange(3072, 4096)]
        else:
            own = np.arange(1024, 3072)
        out[b][own] = np.asarray(res.results[c]["y"]).T
    return out


# version 03
# speedup vs baseline: 1.0852x; 1.0852x over previous
import numpy as np
import ml_dtypes
import concourse.bass as bass
import concourse.mybir as mybir
from concourse.bass_utils import run_bass_kernel_spmd

F32 = mybir.dt.float32
BF16 = mybir.dt.bfloat16
AF = mybir.ActivationFunctionType
ALU = mybir.AluOpType

B, T, D = 4, 4096, 1024
TO = 2048
DT = 8
NEGREF = -12.0
MAGIC = 12582912.0
PAIRS = [[0, 1], [2, 3], [4, 5], [6, 7]]
HMOE = 2048
CH = 128
NCH = T // CH
PAD = 64
STAGE = 10
DBG_SPILL_LIN = False

ENGS = ("sync", "gpsimd", "scalar", "vector", "tensor")


class EM:

    def __init__(self, nc):
        self.nc = nc
        self.cur = None
        self.h = None
        self.cnt = {e: 0 for e in ("pe", "dve", "act", "sp", "gp")}
        self.dmac = {q: 0 for q in ("sp", "gp", "act")}
        self.cc = 0
        self._sem_ctx = []
        self.sem = {}
        for name in ("pe", "dve", "act", "sp", "gp", "dsp", "dgp", "dact", "ccs"):
            g = nc.semaphore("sem_" + name)
            self.sem[name] = g.__enter__()
            self._sem_ctx.append(g)

    ENG2NAME = {"tensor": "pe", "vector": "dve", "scalar": "act",
                "sync": "sp", "gpsimd": "gp"}
    NORM = {"tensor": "pe", "vector": "dve", "scalar": "act", "sync": "sp",
            "gpsimd": "gp", "pe": "pe", "dve": "dve", "act": "act",
            "sp": "sp", "gp": "gp"}

    def phase(self, fn):
        snap = (dict(self.cnt), dict(self.dmac), self.cc)
        results = []
        with self.nc.Block() as blk:
            for pi, eng in enumerate(ENGS):
                self.cnt, self.dmac, self.cc = (dict(snap[0]), dict(snap[1]), snap[2])
                self.cur = self.ENG2NAME[eng]
                self._oplog_ref = getattr(self, "_oplog", None) if pi > 0 else None
                self._oplog = []
                self._opi = 0
                holder = {}

                def body(h, _fn=fn, _holder=holder):
                    self.h = h
                    _fn(self)
                    _holder["state"] = (dict(self.cnt), dict(self.dmac), self.cc)

                getattr(blk, eng)(body)
                results.append(holder["state"])
        for r in results[1:]:
            assert r == results[0], f"engine passes disagree: {results[0]} vs {r}"
        self.cnt, self.dmac, self.cc = (dict(results[0][0]), dict(results[0][1]),
                                       results[0][2])
        self.cur = None

    def _ck(self, tag):
        state = (tag, tuple(sorted(self.cnt.items())), tuple(sorted(self.dmac.items())))
        if self._oplog_ref is not None:
            assert self._opi < len(self._oplog_ref), f"extra op {state} in pass {self.cur}"
            ref = self._oplog_ref[self._opi]
            assert ref == state, f"pass {self.cur} op {self._opi}: {state} != ref {ref}"
        self._oplog.append(state)
        self._opi += 1

    def _emit(self, eng, f, inc):
        self._ck(("op", eng, inc))
        if self.cur == eng:
            inst = f(self.h)
            if inc:
                inst.then_inc(self.sem[eng], 1)
        if inc:
            self.cnt[eng] += 1

    def pe(self, f, inc=False):
        self._emit("pe", f, inc)

    def dve(self, f, inc=False):
        self._emit("dve", f, inc)

    def act(self, f, inc=False):
        self._emit("act", f, inc)

    def gp(self, f, inc=False):
        self._emit("gp", f, inc)

    def wait(self, waiter, target, lag=0):
        waiter, target = self.NORM[waiter], self.NORM[target]
        self._ck(("wait", waiter, target, lag))
        v = self.cnt[target] - lag
        if v <= 0:
            return
        if self.cur == waiter:
            self.h.wait_ge(self.sem[target], v)

    def mark(self, target):
        return self.cnt[self.NORM[target]]

    def wait_abs(self, waiter, target, value):
        waiter, target = self.NORM[waiter], self.NORM[target]
        self._ck(("wait_abs", waiter, target, value))
        if value <= 0:
            return
        if self.cur == waiter:
            self.h.wait_ge(self.sem[target], value)

    def dma(self, q, out, in_, inc=True):
        self._ck(("dma", q, inc))
        if self.cur == q:
            inst = self.h.dma_start(out=out, in_=in_)
            if inc:
                inst.then_inc(self.sem["d" + q], 16)
        if inc:
            self.dmac[q] += 16

    def mark_dma(self, q):
        return self.dmac[q]

    def wait_dma(self, waiter, q, lag=0, value=None):
        waiter = self.NORM[waiter]
        self._ck(("wait_dma", waiter, q, lag, value))
        v = (self.dmac[q] - 16 * lag) if value is None else value
        if v <= 0:
            return
        if self.cur == waiter:
            self.h.wait_ge(self.sem["d" + q], v)

    def collective(self, kind, op, ins, outs):
        if self.cur == "gp":
            self.h.collective_compute(kind, op, replica_groups=PAIRS,
                                      ins=ins, outs=outs).then_inc(self.sem["ccs"])
        self.cc += 1

    def wait_cc(self, waiter):
        waiter = self.NORM[waiter]
        if self.cur == waiter and self.cc > 0:
            self.h.wait_ge(self.sem["ccs"], self.cc)


def build_nc(debug_taps=()):
    nc = bass.Bass("TRN2", target_bir_lowering=False, debug=False,
                   detect_race_conditions=False)

    inp = {}

    def di(name, shape, dtype=F32):
        inp[name] = nc.dram_tensor(name, shape, dtype, kind="ExternalInput").ap()
        return inp[name]

    x_own = di("x_own", [D, TO])
    x_full = di("x_full", [D, T])
    row_idx = di("row_idx", [1, TO])
    col_idx = di("col_idx", [128, T // 128])
    ident_b = di("ident_b", [128, 128], BF16)
    ident_f = di("ident_f", [128, 128])
    msl_neg = di("msl_neg", [128, 128])
    msu_neg = di("msu_neg", [128, 128])
    tril_f = di("tril_f", [128, 128])
    triu_f = di("triu_f", [128, 128])
    sel01 = di("sel01", [128, 2])
    sel8 = di("sel8", [8, 8 * 128], BF16)
    sel10 = di("sel10", [10, 10 * 128])
    selh = di("selh", [2, 256], BF16)

    wq0 = di("wq0", [D, D])
    wk0 = di("wk0", [D, D])
    wv0 = di("wv0", [D, D])
    bq0 = di("bq0", [128, DT])
    bk0 = di("bk0", [128, DT])
    bv0 = di("bv0", [1, D])
    gw0 = di("gw0", [D, 8])
    ew1 = di("ew1", [8, D, HMOE])
    ew2 = di("ew2", [8, D, HMOE])
    ew3 = di("ew3", [8, HMOE, D])
    sw1 = di("sw1", [D, HMOE])
    sw2 = di("sw2", [D, HMOE])
    sw3 = di("sw3", [HMOE, D])

    wq1 = di("wq1", [D, 512])
    wk1 = di("wk1", [D, 512])
    wv1 = di("wv1", [D, 512])
    wb1 = di("wb1", [D, 2])
    cvq = di("cvq", [128, 16])
    cvk = di("cvk", [128, 16])
    cvv = di("cvv", [128, 16])
    firs = di("firs", [128, 4 * 3])
    firm = di("firm", [128, 4 * 15])
    firl = di("firl", [128, 4 * 63])
    gw1 = di("gw1", [D, 10])
    gb1 = di("gb1", [1, 10])
    ltmp = di("ltmp", [1, 2])
    epsp = di("epsp", [1, 2])
    gamm = di("gamm", [128, 2])
    wo1 = di("wo1", [D, D])
    w1t = di("w1t", [D, 4096])
    w2t = di("w2t", [4096, D])
    b1f = di("b1f", [128, 32])
    b2f = di("b2f", [128, 8])

    y = nc.dram_tensor("y", [D, TO], F32, kind="ExternalOutput").ap()

    s_xn0o = nc.dram_tensor("s_xn0o", [D, TO], BF16).ap()
    s_xn0f = nc.dram_tensor("s_xn0f", [D, T], BF16).ap()
    s_oun = nc.dram_tensor("s_oun", [D, TO], BF16).ap()
    s_x1 = nc.dram_tensor("s_x1", [D, TO], F32).ap()
    s_xq = nc.dram_tensor("s_xq", [D, TO], BF16).ap()
    s_ffn1 = nc.dram_tensor("s_ffn1", [D, TO], F32).ap()
    s_q = nc.dram_tensor("s_q", [D, TO], BF16).ap()
    s_qc = nc.dram_tensor("s_qc", [512, T], BF16).ap()
    s_kc = nc.dram_tensor("s_kc", [512, T], BF16).ap()

    cc_z_i = nc.dram_tensor("cc_z_i", [1, 128], F32).ap()
    cc_z_o = nc.dram_tensor("cc_z_o", [2, 128], F32).ap()
    cc_x_i = nc.dram_tensor("cc_x_i", [D, TO], BF16).ap()
    cc_x_o = nc.dram_tensor("cc_x_o", [2 * D, TO], BF16).ap()
    cc_o_i = nc.dram_tensor("cc_o_i", [512, T], BF16).ap()
    cc_o_o = nc.dram_tensor("cc_o_o", [1024, T], BF16).ap()

    taps = {}
    for tname in debug_taps:
        shp = {"xn0o": [D, TO], "q0": [D, TO], "attn0": [D, TO], "zz": [1, 260],
               "wdump": [128, 4096],
               "ffn0": [D, TO], "x1": [D, TO], "xn1f": [D, T], "ffn1": [D, TO],
               "qc1": [512, T], "kc1": [512, T], "vc1": [512, T], "beta": [2, T],
               "onrm": [512, T], "probs": [10, T], "dlt": [512, T],
               "dbg0": [128, 4096], "wef": [8, TO],
               "gdbg": [128, 128]}[tname]
        taps[tname] = nc.dram_tensor("tap_" + tname, shp, F32,
                                     kind="ExternalOutput").ap()

    em = EM(nc)
    fm = lambda ap, n=DT: ap.rearrange("(dt p) t -> p dt t", dt=n)

    alloc_stack = {"left": [], "right": []}

    def sb(name, shape, dtype, side="left"):
        g = nc.sbuf_tensor(name, shape, dtype, side=side)
        h = g.__enter__()
        alloc_stack[side].append((name, g))
        return h

    def sb_free(upto_name, side="left"):
        stk = alloc_stack[side]
        while stk:
            name, g = stk.pop()
            g.__exit__(None, None, None)
            if name == upto_name:
                return
        raise KeyError(upto_name)

    c_ones_b = sb("c_ones_b", [128, 1], BF16)
    c_ones_f = sb("c_ones_f", [128, 1], F32)
    c_onesr_f = sb("c_onesr_f", [1, 128], F32)
    c_bq = sb("c_bq", [128, DT], F32)
    c_bk = sb("c_bk", [128, DT], F32)
    c_sel = sb("c_sel", [128, 2], F32)
    c_small = sb("c_small", [128, 16], F32)
    c_zpad = sb("c_zpad", [1, 128], F32)
    STAT = sb("STAT", [2, 1024], F32)
    ST = sb("ST", [128, 1024], F32)
    c_ident_f = sb("c_ident_f", [128, 128], F32)
    c_rowbc = sb("c_rowbc", [128, TO], F32, side="right")
    c_colidx = sb("c_colidx", [128, 32], F32, side="right")

    ps1 = nc.psum_tensor("ps1", [128, 512], F32).__enter__()
    ps2 = nc.psum_tensor("ps2", [128, 512], F32).__enter__()
    ps3 = nc.psum_tensor("ps3", [128, 512], F32).__enter__()
    ps4 = nc.psum_tensor("ps4", [128, 512], F32).__enter__()
    ps5 = nc.psum_tensor("ps5", [128, 512], F32).__enter__()
    ps6 = nc.psum_tensor("ps6", [128, 512], F32).__enter__()
    ps7 = nc.psum_tensor("ps7", [128, 512], F32).__enter__()
    OB = [ps1, ps2, ps3, ps4]

    def p0(em):
        em.dma("sp", c_ident_f[:], ident_f)
        em.dma("sp", c_colidx[:], col_idx)
        em.dma("sp", c_bq[:], bq0)
        em.dma("sp", c_bk[:], bk0)
        em.dma("sp", c_sel[:], sel01)
        em.wait_dma("vector", "sp")
        em.dve(lambda h: h.memset(c_ones_b[:], 1.0), inc=True)
        em.dve(lambda h: h.memset(c_ones_f[:], 1.0), inc=True)
        em.dve(lambda h: h.memset(c_onesr_f[:], 1.0), inc=True)
        em.dve(lambda h: h.memset(c_zpad[:], 0.0), inc=True)
        em.dve(lambda h: h.memset(c_small[:, 0:4], 0.0), inc=True)
        em.dve(lambda h: h.memset(c_small[:, 4:5], NEGREF), inc=True)
        em.dve(lambda h: h.memset(c_small[:, 5:6], MAGIC), inc=True)
        em.dve(lambda h: h.memset(c_small[:, 6:16], 0.0), inc=True)
        em.dve(lambda h: h.tensor_single_scalar(out=c_bq[:], in_=c_bq[:],
                                                scalar=1.0 / 32.0, op=ALU.mult),
               inc=True)
        em.wait("tensor", "dve")
        for i in range(TO // 512):
            em.wait("sp", "pe")
            em.dma("sp", ST[0:1, 0:512], row_idx[0:1, i * 512:(i + 1) * 512])
            em.wait_dma("tensor", "sp")
            em.pe(lambda h, i=i: h.matmul(ps1[:, 0:512], c_onesr_f[:],
                                          ST[0:1, 0:512],
                                          start=True, stop=True), inc=True)
            em.wait("vector", "pe")
            em.dve(lambda h, i=i: h.tensor_copy(out=c_rowbc[:, i * 512:(i + 1) * 512],
                                                in_=ps1[:, 0:512]), inc=True)
            em.wait("tensor", "dve")

    def ln_chunk(em, SC, SD, cw, dst_fn):
        for dt in range(DT):
            em.pe(lambda h, dt=dt: h.matmul(
                ps5[0:1, 0:cw], c_ones_f[:], SC[:, dt * cw:(dt + 1) * cw],
                start=(dt == 0), stop=(dt == DT - 1)), inc=(dt == DT - 1))
        em.wait("scalar", "pe", lag=1)
        for dt in range(DT):
            em.act(lambda h, dt=dt: h.activation(
                SD[:, dt * cw:(dt + 1) * cw], SC[:, dt * cw:(dt + 1) * cw],
                AF.Square), inc=(dt == DT - 1))
        em.wait("tensor", "act")
        for dt in range(DT):
            em.pe(lambda h, dt=dt: h.matmul(
                ps6[0:1, 0:cw], c_ones_f[:], SD[:, dt * cw:(dt + 1) * cw],
                start=(dt == 0), stop=(dt == DT - 1)), inc=(dt == DT - 1))
        em.wait("vector", "pe")
        em.dve(lambda h: h.tensor_single_scalar(out=STAT[0:1, 0:cw], in_=ps5[0:1, 0:cw],
                                                scalar=1.0 / D, op=ALU.mult), inc=True)
        em.dve(lambda h: h.tensor_single_scalar(out=STAT[0:1, cw:2 * cw],
                                                in_=ps6[0:1, 0:cw],
                                                scalar=1.0 / D, op=ALU.mult), inc=True)
        em.dve(lambda h: h.tensor_tensor(out=ST[0:1, 0:cw], in0=STAT[0:1, 0:cw],
                                         in1=STAT[0:1, 0:cw], op=ALU.mult), inc=True)
        em.dve(lambda h: h.tensor_tensor(out=ST[0:1, 0:cw], in0=STAT[0:1, cw:2 * cw],
                                         in1=ST[0:1, 0:cw], op=ALU.subtract), inc=True)
        em.dve(lambda h: h.tensor_single_scalar(out=ST[0:1, 0:cw], in_=ST[0:1, 0:cw],
                                                scalar=1e-5, op=ALU.add), inc=True)
        em.wait("scalar", "dve")
        em.act(lambda h: h.activation(ST[0:1, cw:2 * cw], ST[0:1, 0:cw], AF.Sqrt),
               inc=True)
        em.wait("vector", "act")
        em.dve(lambda h: h.reciprocal(out=ST[0:1, 0:cw], in_=ST[0:1, cw:2 * cw]),
               inc=True)
        em.wait("tensor", "dve")
        nb = max(cw // 512, 1)
        bw = min(cw, 512)
        for i in range(nb):
            em.pe(lambda h, i=i: h.matmul(ps5[:, 0:bw], c_onesr_f[:],
                                          STAT[0:1, i * bw:(i + 1) * bw],
                                          start=True, stop=True), inc=True)
            em.pe(lambda h, i=i: h.matmul(ps6[:, 0:bw], c_onesr_f[:],
                                          ST[0:1, i * bw:(i + 1) * bw],
                                          start=True, stop=True), inc=True)
            em.wait("vector", "pe")
            for dt in range(DT):
                o0 = dt * cw + i * bw
                em.dve(lambda h, o0=o0, bw=bw: h.tensor_tensor(
                    out=SD[:, o0:o0 + bw], in0=SC[:, o0:o0 + bw],
                    in1=ps5[:, 0:bw], op=ALU.subtract), inc=True)
                em.dve(lambda h, o0=o0, bw=bw: h.tensor_tensor(
                    out=SD[:, o0:o0 + bw], in0=SD[:, o0:o0 + bw],
                    in1=ps6[:, 0:bw], op=ALU.mult), inc=True)
                dst_fn(em, dt, i * bw, bw)
            em.wait("tensor", "dve")

    CW = 512

    def mk_p1():
        Aq = sb("Aq", [128, DT * TO], BF16)
        Ax = sb("Ax", [128, DT * TO], BF16)
        Wst = sb("Wst_p1", [128, DT * 1024], BF16)
        SC = sb("SC_p1", [128, DT * CW], F32)
        SD = sb("SD_p1", [128, DT * CW], F32)

        def p1(em):
            em.dma("gp", Wst[:].rearrange("p (kt n) -> p kt n", kt=DT), fm(wq0))
            for tch in range(TO // CW):
                em.wait("sp", "pe")
                em.wait("sp", "dve")
                em.dma("sp", SC[:].rearrange("p (dt t) -> p dt t", dt=DT),
                       fm(x_own)[:, :, tch * CW:(tch + 1) * CW])
                em.wait_dma("tensor", "sp")
                em.wait_dma("scalar", "sp")
                em.wait_dma("vector", "sp")

                def dst(em, dt, off, bw, tch=tch):
                    em.dve(lambda h, dt=dt, off=off, bw=bw, tch=tch: h.tensor_copy(
                        out=Aq[:, 0:1][:, 0:0] if False else
                        Ax[:, dt * TO + tch * CW + off:dt * TO + tch * CW + off + bw],
                        in_=SD[:, dt * CW + off:dt * CW + off + bw]), inc=True)

                ln_chunk(em, SC, SD, CW, dst)
            em.wait("sync", "dve")
            em.dma("sp", fm(s_xn0o), Ax[:].rearrange("p (dt t) -> p dt t", dt=DT))
            em.wait_dma("tensor", "gp")
            for dqt in range(DT):
                for tch in range(TO // 512):
                    bank = ps1 if (dqt * 4 + tch) % 2 == 0 else ps2
                    em.wait("tensor", "act", lag=1)
                    for kt in range(DT):
                        em.pe(lambda h, dqt=dqt, tch=tch, kt=kt, bank=bank: h.matmul(
                            bank[:, 0:512],
                            Wst[:, kt * 1024 + dqt * 128:kt * 1024 + dqt * 128 + 128],
                            Ax[:, kt * TO + tch * 512:kt * TO + tch * 512 + 512],
                            start=(kt == 0), stop=(kt == DT - 1)), inc=(kt == DT - 1))
                    em.wait("scalar", "pe")
                    em.act(lambda h, dqt=dqt, tch=tch, bank=bank: h.activation(
                        Aq[:, dqt * TO + tch * 512:dqt * TO + tch * 512 + 512],
                        bank[:, 0:512], AF.Identity, scale=1.0 / 32.0,
                        bias=c_bq[:, dqt:dqt + 1]), inc=True)
            em.wait("sync", "act")
            em.dma("sp", fm(s_q), Aq[:].rearrange("p (dt t) -> p dt t", dt=DT))
            if "xn0o" in taps:
                em.wait("gpsimd", "act")
                em.dma("gp", fm(taps["xn0o"]),
                       Ax[:].rearrange("p (dt t) -> p dt t", dt=DT))
            if "q0" in taps:
                em.wait("gpsimd", "act")
                em.dma("gp", fm(taps["q0"]),
                       Aq[:].rearrange("p (dt t) -> p dt t", dt=DT))
            em.wait_dma("sync", "sp")
            em.wait_dma("gpsimd", "gp")

        em.phase(p1)
        sb_free("Aq")
        return None

    def mk_p2():
        Ak = sb("Ak", [128, DT * T], BF16)
        Wst = sb("Wst_p2", [128, DT * 1024], BF16)
        SC = sb("SC_p2", [128, DT * CW], F32)
        SD = sb("SD_p2", [128, DT * CW], F32)

        def p2(em):
            em.dma("gp", Wst[:].rearrange("p (kt n) -> p kt n", kt=DT), fm(wk0))
            em.wait_dma("tensor", "gp")
            for tch in range(T // CW):
                em.wait("sp", "pe")
                em.wait("sp", "dve")
                em.dma("sp", SC[:].rearrange("p (dt t) -> p dt t", dt=DT),
                       fm(x_full)[:, :, tch * CW:(tch + 1) * CW])
                em.wait_dma("tensor", "sp")
                em.wait_dma("scalar", "sp")
                em.wait_dma("vector", "sp")

                def dst(em, dt, off, bw, tch=tch):
                    pass

                ln_chunk(em, SC, SD, CW, dst)
                SCb = SC.bitcast(BF16)
                em.wait("vector", "pe")
                em.wait_dma("vector", "sp")
                em.dve(lambda h: h.tensor_copy(out=SCb[:, 0:DT * CW],
                                               in_=SD[:, 0:DT * CW]), inc=True)
                em.wait("sync", "dve")
                em.dma("sp", fm(s_xn0f)[:, :, tch * CW:(tch + 1) * CW],
                       SCb[:, 0:DT * CW].rearrange("p (dt t) -> p dt t", dt=DT))
                for dqt in range(DT):
                    bank = ps1 if dqt % 2 == 0 else ps2
                    em.wait("tensor", "act", lag=1)
                    em.wait("tensor", "dve")
                    for kt in range(DT):
                        em.pe(lambda h, dqt=dqt, kt=kt, bank=bank, tch=tch: h.matmul(
                            bank[:, 0:CW],
                            Wst[:, kt * 1024 + dqt * 128:kt * 1024 + dqt * 128 + 128],
                            SCb[:, kt * CW:(kt + 1) * CW],
                            start=(kt == 0), stop=(kt == DT - 1)), inc=(kt == DT - 1))
                    em.wait("scalar", "pe")
                    em.act(lambda h, dqt=dqt, tch=tch, bank=bank: h.activation(
                        Ak[:, dqt * T + tch * CW:dqt * T + tch * CW + CW],
                        bank[:, 0:CW], AF.Identity, bias=c_bk[:, dqt:dqt + 1]),
                        inc=True)
                em.wait("sp", "act")
            em.wait_dma("gpsimd", "gp")
            em.wait_dma("sync", "sp")

        em.phase(p2)
        sb_free("Wst_p2")
        return Ak

    def mk_p3():
        Av = sb("Av", [128, 32 * 1024], BF16)
        Wst = sb("Wst_p3", [128, DT * 1024], BF16)
        Xc = sb("Xc_p3", [128, DT * CW], BF16)
        bvb = ST

        def p3(em):
            em.dma("gp", Wst[:].rearrange("p (kt n) -> p kt n", kt=DT), fm(wv0))
            em.dma("sp", STAT[0:1, 0:D], bv0)
            em.wait_dma("tensor", "sp")
            for i in range(D // 512):
                em.pe(lambda h, i=i: h.matmul(ps5[:, 0:512], c_onesr_f[:],
                                              STAT[0:1, i * 512:(i + 1) * 512],
                                              start=True, stop=True), inc=True)
                em.wait("vector", "pe")
                em.dve(lambda h, i=i: h.tensor_copy(out=bvb[:, i * 512:(i + 1) * 512],
                                                    in_=ps5[:, 0:512]), inc=True)
                em.wait("tensor", "dve")
            em.wait_dma("tensor", "gp")
            for g in range(T // CW):
                em.wait("sp", "pe")
                em.dma("sp", Xc[:].rearrange("p (dt t) -> p dt t", dt=DT),
                       fm(s_xn0f)[:, :, g * CW:(g + 1) * CW])
                em.wait_dma("tensor", "sp")
                for tt in range(4):
                    for n in range(2):
                        bank = ps1 if (tt * 2 + n) % 2 == 0 else ps2
                        em.wait("tensor", "dve", lag=1)
                        for kt in range(DT):
                            em.pe(lambda h, tt=tt, n=n, kt=kt, bank=bank, g=g:
                                  h.matmul(
                                      bank[:, 0:512],
                                      Xc[:, kt * CW + tt * 128:
                                         kt * CW + tt * 128 + 128],
                                      Wst[:, kt * 1024 + n * 512:
                                          kt * 1024 + n * 512 + 512],
                                      start=(kt == 0), stop=(kt == DT - 1)),
                                  inc=(kt == DT - 1))
                        em.wait("vector", "pe")
                        gt = g * 4 + tt
                        em.dve(lambda h, gt=gt, n=n, bank=bank:
                               h.scalar_tensor_tensor(
                                   out=Av[:, gt * 1024 + n * 512:
                                          gt * 1024 + n * 512 + 512],
                                   in0=bank[:, 0:512], scalar=1.0,
                                   in1=bvb[:, n * 512:(n + 1) * 512],
                                   op0=ALU.mult, op1=ALU.add), inc=True)
            em.wait("sync", "dve")
            em.wait_dma("gpsimd", "gp")

        em.phase(p3)
        sb_free("Wst_p3")
        return Av

    def mk_p4(Ak, Av):
        Eall = sb("Eall_p4", [128, 32 * 256], BF16)
        Qrb = sb("Qrb_p4", [128, DT * 256], BF16)
        Wsc = sb("Wsc_p4", [128, 1536], BF16)

        def p4(em):
            W32 = Wsc.bitcast(F32)
            em.dve(lambda h: h.memset(c_small[:, 0:4], 0.0), inc=True)
            qld = [0]
            for rb in range(8):
                ncols = 2048 if rb < 4 else T
                ncts = ncols // 128
                em.wait("sp", "pe")
                em.dma("sp", Qrb[:].rearrange("p (dt t) -> p dt t", dt=DT),
                       fm(s_q)[:, :, rb * 256:(rb + 1) * 256])
                qld[0] = em.mark_dma("sp")
                s_done = [0] * ncts
                exp_done = [0] * ncts
                stt_done = [0] * ncts

                def emit_sgroup(ct, rb=rb, s_done=s_done, exp_done=exp_done):
                    sbank = ps5 if ct % 2 == 0 else ps6
                    if ct >= 2:
                        em.wait_abs("tensor", "act", exp_done[ct - 2])
                    if ct == 0:
                        em.wait_dma("tensor", "sp", value=qld[0])
                    for kt in range(DT):
                        em.pe(lambda h, ct=ct, kt=kt, sbank=sbank, rb=rb: h.matmul(
                            sbank[:, 0:256],
                            Ak[:, kt * T + ct * 128:kt * T + ct * 128 + 128],
                            Qrb[:, kt * 256:kt * 256 + 256],
                            start=(kt == 0), stop=(kt == DT - 1)),
                            inc=(kt == DT - 1))
                    s_done[ct] = em.mark("pe")

                emit_sgroup(0)
                for ct in range(ncts):
                    if ct + 1 < ncts:
                        emit_sgroup(ct + 1)
                    sbank = ps5 if ct % 2 == 0 else ps6
                    em.dve(lambda h, rb=rb, ct=ct: h.tensor_single_scalar(
                        out=W32[:, 0:256],
                        in_=c_rowbc[:, rb * 256:rb * 256 + 256],
                        scalar=c_colidx[:, ct:ct + 1], op=ALU.is_ge), inc=True)
                    em.wait_abs("scalar", "pe", s_done[ct])
                    if ct >= 1:
                        em.wait_abs("scalar", "dve", stt_done[ct - 1])
                    em.act(lambda h, sbank=sbank: h.activation(
                        Wsc[:, 1024:1280], sbank[:, 0:256], AF.Exp,
                        bias=c_small[:, 4:5]), inc=True)
                    exp_done[ct] = em.mark("act")
                    em.wait("vector", "act")
                    em.dve(lambda h, ct=ct: h.scalar_tensor_tensor(
                        out=Eall[:, ct * 256:ct * 256 + 256], in0=Wsc[:, 1024:1280],
                        scalar=1.0, in1=W32[:, 0:256], op0=ALU.mult, op1=ALU.mult,
                        accum_out=c_small[:, 2:3]), inc=True)
                    em.dve(lambda h: h.tensor_tensor(
                        out=c_small[:, 0:1], in0=c_small[:, 0:1],
                        in1=c_small[:, 2:3], op=ALU.add), inc=True)
                    stt_done[ct] = em.mark("dve")
                em.wait_abs("tensor", "dve", stt_done[ncts - 1])
                em.wait("tensor", "dve")
                for dvh in range(2):
                    if dvh == 1:
                        em.wait("tensor", "dve")
                    for ct in range(ncts):
                        for i in range(4):
                            dvt = dvh * 4 + i
                            em.pe(lambda h, ct=ct, dvt=dvt, i=i: h.matmul(
                                OB[i][:, 0:256],
                                Av[:, ct * 1024 + dvt * 128:
                                   ct * 1024 + dvt * 128 + 128],
                                Eall[:, ct * 256:ct * 256 + 256],
                                start=(ct == 0), stop=(ct == ncts - 1)),
                                inc=(ct == ncts - 1 and i == 3))
                    em.wait("vector", "pe")
                    em.wait_dma("vector", "sp")
                    for i in range(4):
                        em.dve(lambda h, i=i: h.tensor_copy(
                            out=ST[:, i * 256:(i + 1) * 256],
                            in_=OB[i][:, 0:256]), inc=(i == 3))
                    em.wait("gpsimd", "dve")
                    em.dma("gp", fm(s_oun)[:, dvh * 4:dvh * 4 + 4,
                                           rb * 256:(rb + 1) * 256],
                           ST[:, 0:1024].rearrange("p (dt t) -> p dt t", dt=4))
                    em.wait_dma("vector", "gp")
            em.wait("tensor", "dve")
            em.pe(lambda h: h.matmul(ps5[0:1, 0:1], c_small[:, 0:1],
                                     c_ones_f[:], start=True, stop=True), inc=True)
            em.wait("vector", "pe")
            em.dve(lambda h: h.tensor_copy(out=c_zpad[0:1, 0:1], in_=ps5[0:1, 0:1]),
                   inc=True)
            em.wait("sync", "dve")
            em.dma("sp", cc_z_i, c_zpad[:])
            em.wait_dma("gpsimd", "sp")
            em.collective("AllGather", ALU.bypass, [cc_z_i.opt()], [cc_z_o.opt()])

        em.phase(p4)
        sb_free("Eall_p4")

    def mk_p5():
        ACC = sb("ACC_p5", [128, DT * TO], F32)
        WEF = sb("WEF_p5", [8, TO], BF16)
        S8B = ST.bitcast(BF16)
        W1s = sb("W1s_p5", [128, DT * HMOE], BF16)
        W2s = sb("W2s_p5", [128, DT * HMOE], BF16)
        W3s = sb("W3s_p5", [128, 16 * 1024], BF16)
        Xt2 = sb("Xt_p5", [128, 2 * DT * 512], BF16)
        Gt = sb("Gt_p5", [128, 16 * 512], BF16)
        WB = sb("WB_p5", [128, 512], BF16)
        S8 = ST.ap()[0:8, 0:1024] if False else ST

        def p5(em):
            Tm = Xt2.bitcast(F32)[:, 2048:2560]
            GWB = ST.bitcast(BF16)
            Xt = Xt2[:, 0:DT * 512]
            for tch in range(4):
                em.wait("sp", "pe")
                em.wait("sp", "dve")
                em.dma("sp", Xt[:].rearrange("p (dt t) -> p dt t", dt=DT),
                       fm(s_xn0o)[:, :, tch * 512:(tch + 1) * 512])
                if tch == 0:
                    em.dma("sp", S8B[0:8, 0:1024], sel8)
                    em.dma("gp",
                           GWB[:, 1024:1088].rearrange("p (kt e) -> p kt e", kt=DT),
                           gw0.rearrange("(kt p) e -> p kt e", kt=DT))
                em.wait_dma("tensor", "sp")
                em.wait_dma("tensor", "gp")
                sb_ = lambda st: st * 128
                g_ = lambda st: Tm[:, sb_(st) + 0:sb_(st) + 8]
                m1_ = lambda st: Tm[:, sb_(st) + 8:sb_(st) + 9]
                m2_ = lambda st: Tm[:, sb_(st) + 9:sb_(st) + 10]
                den_ = lambda st: Tm[:, sb_(st) + 10:sb_(st) + 11]
                rde_ = lambda st: Tm[:, sb_(st) + 11:sb_(st) + 12]
                eq1_ = lambda st: Tm[:, sb_(st) + 16:sb_(st) + 24]
                eq2_ = lambda st: Tm[:, sb_(st) + 24:sb_(st) + 32]
                w_ = lambda st: Tm[:, sb_(st) + 32:sb_(st) + 40]
                gp2_ = lambda st: Tm[:, sb_(st) + 40:sb_(st) + 48]
                nm1_ = lambda st: Tm[:, sb_(st) + 48:sb_(st) + 49]
                ex_ = lambda st: Tm[:, sb_(st) + 56:sb_(st) + 64]
                for tt in range(4):
                    em.wait("tensor", "dve")
                    for kt in range(DT):
                        em.pe(lambda h, tch=tch, tt=tt, kt=kt: h.matmul(
                            ps7[:, 0:8],
                            Xt[:, kt * 512 + tt * 128:kt * 512 + tt * 128 + 128],
                            GWB[:, 1024 + kt * 8:1024 + kt * 8 + 8],
                            start=(kt == 0), stop=(kt == DT - 1)),
                            inc=(kt == DT - 1))
                    em.wait("vector", "pe")
                    em.dve(lambda h, tt=tt: h.tensor_copy(out=g_(tt),
                                                          in_=ps7[:, 0:8]),
                           inc=True)
                for st in range(4):
                    em.dve(lambda h, st=st: h.tensor_reduce(
                        out=m1_(st), in_=g_(st), axis=mybir.AxisListType.X,
                        op=ALU.max), inc=(st == 3))
                for st in range(4):
                    em.dve(lambda h, st=st: h.tensor_single_scalar(
                        out=eq1_(st), in_=g_(st), scalar=m1_(st),
                        op=ALU.is_ge), inc=(st == 3))
                for st in range(4):
                    em.dve(lambda h, st=st: h.scalar_tensor_tensor(
                        out=gp2_(st), in0=eq1_(st), scalar=-1e30, in1=g_(st),
                        op0=ALU.mult, op1=ALU.add), inc=(st == 3))
                for st in range(4):
                    em.dve(lambda h, st=st: h.tensor_reduce(
                        out=m2_(st), in_=gp2_(st), axis=mybir.AxisListType.X,
                        op=ALU.max), inc=(st == 3))
                for st in range(4):
                    em.dve(lambda h, st=st: h.tensor_single_scalar(
                        out=eq2_(st), in_=gp2_(st), scalar=m2_(st),
                        op=ALU.is_ge), inc=(st == 3))
                for st in range(4):
                    em.dve(lambda h, st=st: h.tensor_single_scalar(
                        out=nm1_(st), in_=m1_(st), scalar=-1.0,
                        op=ALU.mult), inc=(st == 3))
                em.wait("scalar", "dve")
                for st in range(4):
                    em.act(lambda h, st=st: h.activation(
                        ex_(st), g_(st), AF.Exp, bias=nm1_(st)),
                        inc=(st == 3))
                em.wait("vector", "act")
                for st in range(4):
                    em.dve(lambda h, st=st: h.tensor_tensor(
                        out=w_(st), in0=eq1_(st), in1=eq2_(st),
                        op=ALU.add), inc=(st == 3))
                for st in range(4):
                    em.dve(lambda h, st=st: h.tensor_tensor(
                        out=w_(st), in0=w_(st), in1=ex_(st),
                        op=ALU.mult), inc=(st == 3))
                for st in range(4):
                    em.dve(lambda h, st=st: h.tensor_reduce(
                        out=den_(st), in_=w_(st), axis=mybir.AxisListType.X,
                        op=ALU.add), inc=(st == 3))
                for st in range(4):
                    em.dve(lambda h, st=st: h.reciprocal(
                        out=rde_(st), in_=den_(st)), inc=(st == 3))
                for st in range(4):
                    em.dve(lambda h, st=st: h.tensor_single_scalar(
                        out=w_(st), in_=w_(st), scalar=rde_(st),
                        op=ALU.mult), inc=(st == 3))
                for st in range(4):
                    em.wait("tensor", "dve")
                    em.pe(lambda h, st=st: h.transpose(
                        ps6[0:8, 0:128], w_(st), c_ident_f[:]), inc=True)
                    em.wait("vector", "pe")
                    em.dve(lambda h, tch=tch, st=st: h.tensor_copy(
                        out=WEF[0:8, tch * 512 + st * 128:
                                tch * 512 + st * 128 + 128],
                        in_=ps6[0:8, 0:128]), inc=True)
            if "wef" in taps:
                em.wait("gp", "dve")
                em.dma("gp", taps["wef"], WEF[0:8, :])
                em.wait_dma("tensor", "gp")
            wdma12 = [0]
            wdma3 = [0]

            def load_w12(e):
                s1, s2 = (ew1[e], ew2[e]) if e < 8 else (sw1, sw2)
                em.dma("gp", W1s[:].rearrange("p (kt n) -> p kt n", kt=DT),
                       s1.rearrange("(kt p) n -> p kt n", kt=DT))
                em.dma("gp", W2s[:].rearrange("p (kt n) -> p kt n", kt=DT),
                       s2.rearrange("(kt p) n -> p kt n", kt=DT))
                wdma12[0] = em.mark_dma("gp")

            def load_w3(e):
                s3 = ew3[e] if e < 8 else sw3
                em.dma("gp", W3s[:].rearrange("p (kt n) -> p kt n", kt=16),
                       s3.rearrange("(kt p) n -> p kt n", kt=16))
                wdma3[0] = em.mark_dma("gp")

            load_w12(0)
            load_w3(0)
            xmark = [0, 0]
            em.wait("sp", "pe")
            em.dma("sp", Xt2[:, 0:DT * 512].rearrange("p (dt t) -> p dt t", dt=DT),
                   fm(s_xn0o)[:, :, 0:512])
            xmark[0] = em.mark_dma("sp")
            for e in range(9):
                em.wait_dma("tensor", "gp", value=wdma12[0])
                for tch in range(4):
                    g36 = e * 4 + tch
                    Xt = Xt2[:, (g36 % 2) * DT * 512:(g36 % 2) * DT * 512
                             + DT * 512]
                    if g36 + 1 < 36:
                        ntch = (g36 + 1) % 4
                        nb = (g36 + 1) % 2
                        em.wait("sp", "pe")
                        em.dma("sp", Xt2[:, nb * DT * 512:(nb + 1) * DT * 512]
                               .rearrange("p (dt t) -> p dt t", dt=DT),
                               fm(s_xn0o)[:, :, ntch * 512:(ntch + 1) * 512])
                        xmark[nb] = em.mark_dma("sp")
                    em.wait_dma("tensor", "sp", value=xmark[g36 % 2])
                    hdone = [0] * 16
                    for ht in range(16):
                        hb = ps5 if ht % 2 == 0 else ps6
                        h2b = OB[ht % 2]
                        em.wait("tensor", "dve", lag=1)
                        for kt in range(DT):
                            em.pe(lambda h, ht=ht, kt=kt, hb=hb: h.matmul(
                                hb[:, 0:512],
                                W1s[:, kt * HMOE + ht * 128:kt * HMOE + ht * 128 + 128],
                                Xt[:, kt * 512:kt * 512 + 512],
                                start=(kt == 0), stop=(kt == DT - 1)),
                                inc=(kt == DT - 1))
                        for kt in range(DT):
                            em.pe(lambda h, ht=ht, kt=kt, h2b=h2b: h.matmul(
                                h2b[:, 0:512],
                                W2s[:, kt * HMOE + ht * 128:kt * HMOE + ht * 128 + 128],
                                Xt[:, kt * 512:kt * 512 + 512],
                                start=(kt == 0), stop=(kt == DT - 1)),
                                inc=(kt == DT - 1))
                        em.wait("scalar", "pe", lag=1)
                        em.act(lambda h, hb=hb, ht=ht: h.activation(
                            Gt[:, ht * 512:(ht + 1) * 512],
                            hb[:, 0:512], AF.Silu), inc=True)
                        em.wait("vector", "act")
                        em.wait("vector", "pe")
                        em.dve(lambda h, ht=ht, h2b=h2b: h.tensor_tensor(
                            out=Gt[:, ht * 512:(ht + 1) * 512],
                            in0=Gt[:, ht * 512:(ht + 1) * 512],
                            in1=h2b[:, 0:512],
                            op=ALU.mult), inc=True)
                        hdone[ht] = em.mark("dve")
                    if tch == 3 and e < 8:
                        em.wait("gp", "pe")
                        load_w12(e + 1)
                    if tch == 0:
                        em.wait_dma("tensor", "gp", value=wdma3[0])
                    em.wait("tensor", "dve")
                    if e < 8:
                        em.pe(lambda h, e=e, tch=tch: h.matmul(
                            ps7[:, 0:512], S8B[0:8, e * 128:(e + 1) * 128],
                            WEF[0:8, tch * 512:(tch + 1) * 512],
                            start=True, stop=True), inc=True)
                        em.wait("vector", "pe")
                        em.dve(lambda h: h.tensor_copy(out=WB[:, 0:512],
                                                       in_=ps7[:, 0:512]), inc=True)
                        for ht in range(16):
                            em.dve(lambda h, ht=ht: h.tensor_tensor(
                                out=Gt[:, ht * 512:(ht + 1) * 512],
                                in0=Gt[:, ht * 512:(ht + 1) * 512],
                                in1=WB[:, 0:512], op=ALU.mult),
                                inc=(ht == 15))
                        em.wait("tensor", "dve")
                    for dth in range(2):
                        if dth == 1:
                            em.wait("tensor", "dve")
                        for ht in range(16):
                            for i in range(4):
                                dt = dth * 4 + i
                                em.pe(lambda h, ht=ht, i=i, dt=dt: h.matmul(
                                    OB[i][:, 0:512],
                                    W3s[:, ht * 1024 + dt * 128:
                                        ht * 1024 + dt * 128 + 128],
                                    Gt[:, ht * 512:(ht + 1) * 512],
                                    start=(ht == 0), stop=(ht == 15)),
                                    inc=(ht == 15 and i == 3))
                        em.wait("vector", "pe")
                        for i in range(4):
                            dt = dth * 4 + i
                            dst = ACC[:, dt * TO + tch * 512:dt * TO + tch * 512 + 512]
                            if e == 0:
                                em.dve(lambda h, i=i, dst=dst:
                                       h.tensor_copy(out=dst, in_=OB[i][:, 0:512]),
                                       inc=(i == 3))
                            else:
                                em.dve(lambda h, i=i, dst=dst: h.tensor_tensor(
                                    out=dst, in0=dst, in1=OB[i][:, 0:512],
                                    op=ALU.add), inc=(i == 3))
                    if tch == 3 and e < 8:
                        em.wait("gp", "pe")
                        load_w3(e + 1)
            if "ffn0" in taps:
                em.wait("gpsimd", "dve")
                em.dma("gp", fm(taps["ffn0"]),
                       ACC[:].rearrange("p (dt t) -> p dt t", dt=DT))
                em.wait_dma("gpsimd", "gp")

        em.phase(p5)
        sb_free("WEF_p5")
        return ACC

    def mk_p6(ACC):
        SC = sb("SC_p6", [128, DT * 512], F32)
        SD = sb("SD_p6", [128, DT * 512], F32)
        OC = sb("OC_p6", [128, DT * 512], BF16)

        def p6(em):
            em.wait_cc("gpsimd")
            em.dma("gp", ST[0:1, 0:256].rearrange("o (a n) -> o a n", a=2),
                   cc_z_o.rearrange("(a o) n -> o a n", a=2))
            em.wait_dma("vector", "gp")
            em.dve(lambda h: h.tensor_tensor(out=ST[0:1, 512:640],
                                             in0=ST[0:1, 0:128],
                                             in1=ST[0:1, 128:256], op=ALU.add),
                   inc=True)
            em.dve(lambda h: h.tensor_single_scalar(out=ST[0:1, 512:640],
                                                    in_=ST[0:1, 512:640],
                                                    scalar=1e-20, op=ALU.max),
                   inc=True)
            em.dve(lambda h: h.reciprocal(out=ST[0:1, 640:768],
                                          in_=ST[0:1, 512:640]), inc=True)
            em.wait("tensor", "dve")
            em.pe(lambda h: h.matmul(ps5[:, 0:128], c_onesr_f[:], ST[0:1, 640:768],
                                     start=True, stop=True), inc=True)
            em.wait("vector", "pe")
            em.dve(lambda h: h.tensor_copy(out=c_small[:, 3:4], in_=ps5[:, 0:1]),
                   inc=True)
            for tch in range(4):
                em.wait("sp", "pe")
                em.wait("sp", "dve")
                em.wait("sp", "act")
                em.dma("sp", SC[:].rearrange("p (dt t) -> p dt t", dt=DT),
                       fm(x_own)[:, :, tch * 512:(tch + 1) * 512])
                em.dma("sp", OC[:].rearrange("p (dt t) -> p dt t", dt=DT),
                       fm(s_oun)[:, :, tch * 512:(tch + 1) * 512])
                em.wait_dma("vector", "sp")
                em.wait_dma("tensor", "sp")
                em.wait_dma("scalar", "sp")
                for dt in range(DT):
                    em.dve(lambda h, dt=dt: h.scalar_tensor_tensor(
                        out=SC[:, dt * 512:(dt + 1) * 512],
                        in0=OC[:, dt * 512:(dt + 1) * 512],
                        scalar=c_small[:, 3:4],
                        in1=SC[:, dt * 512:(dt + 1) * 512],
                        op0=ALU.mult, op1=ALU.add), inc=True)
                    em.dve(lambda h, dt=dt, tch=tch: h.tensor_tensor(
                        out=SC[:, dt * 512:(dt + 1) * 512],
                        in0=SC[:, dt * 512:(dt + 1) * 512],
                        in1=ACC[:, dt * TO + tch * 512:dt * TO + tch * 512 + 512],
                        op=ALU.add), inc=True)
                em.wait("sync", "dve")
                em.dma("sp", fm(s_x1)[:, :, tch * 512:(tch + 1) * 512],
                       SC[:].rearrange("p (dt t) -> p dt t", dt=DT))

                def dst(em, dt, off, bw, tch=tch):
                    em.dve(lambda h, dt=dt, off=off, bw=bw: h.tensor_copy(
                        out=OC[:, dt * 512 + off:dt * 512 + off + bw],
                        in_=SD[:, dt * 512 + off:dt * 512 + off + bw]), inc=True)

                em.wait("tensor", "dve")
                em.wait("scalar", "dve")
                ln_chunk(em, SC, SD, 512, dst)
                em.wait("sync", "dve")
                em.dma("sp", fm(cc_x_i)[:, :, tch * 512:(tch + 1) * 512],
                       OC[:].rearrange("p (dt t) -> p dt t", dt=DT))
                em.wait("scalar", "dve")
                for dt in range(DT):
                    sl = slice(dt * 512, (dt + 1) * 512)
                    em.act(lambda h, sl=sl: h.activation(
                        SC[:, sl], SD[:, sl], AF.Identity, scale=16.0,
                        bias=c_small[:, 5:6]), inc=True)
                em.wait("vector", "act")
                em.wait_dma("vector", "sp")
                for dt in range(DT):
                    sl = slice(dt * 512, (dt + 1) * 512)
                    em.dve(lambda h, sl=sl: h.tensor_scalar(
                        out=SC[:, sl], in0=SC[:, sl], scalar1=MAGIC, scalar2=127.0,
                        op0=ALU.subtract, op1=ALU.min), inc=True)
                    em.dve(lambda h, sl=sl: h.tensor_scalar(
                        out=OC[:, sl], in0=SC[:, sl], scalar1=-128.0, scalar2=0.0625,
                        op0=ALU.max, op1=ALU.mult), inc=True)
                em.wait("sync", "dve")
                em.dma("sp", fm(s_xq)[:, :, tch * 512:(tch + 1) * 512],
                       OC[:].rearrange("p (dt t) -> p dt t", dt=DT))
                if "x1" in taps:
                    em.dma("sp", fm(taps["x1"])[:, :, tch * 512:(tch + 1) * 512],
                           SC[:, 0:0] if False else
                           fm(s_x1)[:, :, tch * 512:(tch + 1) * 512])
            em.wait_dma("gpsimd", "sp")
            em.collective("AllGather", ALU.bypass, [cc_x_i.opt()], [cc_x_o.opt()])

        em.phase(p6)
        sb_free("SC_p6")

    def mk_p7():
        W1q = sb("W1q_p7", [128, DT * 4096], BF16)
        W2q = sb("W2q_p7", [128, 32 * 1024], BF16)
        CB = sb("CB_p7", [128, 48], F32)

        def quant_weights(em):
            em.dma("sp", CB[:, 0:32], b1f)
            em.dma("sp", CB[:, 32:40], b2f)
            em.wait_dma("scalar", "sp")
            em.wait_dma("vector", "sp")
            em.act(lambda h: h.activation(CB[:, 0:32], CB[:, 0:32], AF.Identity,
                                          scale=64.0, bias=c_small[:, 5:6]),
                   inc=True)
            em.act(lambda h: h.activation(CB[:, 32:40], CB[:, 32:40], AF.Identity,
                                          scale=64.0, bias=c_small[:, 5:6]),
                   inc=True)
            em.wait("vector", "act")
            em.dve(lambda h: h.tensor_scalar(out=CB[:, 0:32], in0=CB[:, 0:32],
                                             scalar1=MAGIC, scalar2=127.0,
                                             op0=ALU.subtract, op1=ALU.min),
                   inc=True)
            em.dve(lambda h: h.tensor_scalar(out=CB[:, 32:40], in0=CB[:, 32:40],
                                             scalar1=MAGIC, scalar2=127.0,
                                             op0=ALU.subtract, op1=ALU.min), inc=True)
            em.dve(lambda h: h.tensor_single_scalar(out=CB[:, 0:32], in_=CB[:, 0:32],
                                                    scalar=-128.0, op=ALU.max),
                   inc=True)
            em.dve(lambda h: h.tensor_single_scalar(out=CB[:, 32:40],
                                                    in_=CB[:, 32:40],
                                                    scalar=-128.0, op=ALU.max),
                   inc=True)
            em.wait("scalar", "dve")
            em.act(lambda h: h.activation(CB[:, 0:32], CB[:, 0:32], AF.Identity,
                                          scale=0.25, bias=c_small[:, 5:6]),
                   inc=True)
            em.act(lambda h: h.activation(CB[:, 32:40], CB[:, 32:40], AF.Identity,
                                          scale=1.0 / 64.0), inc=True)
            plan = []
            for src, dstW, nkt, ncols in ((w1t, W1q, DT, 4096), (w2t, W2q, 32, 1024)):
                for ch in range((nkt * ncols) // 4096):
                    plan.append((src, dstW, nkt, ch, 4096 // nkt))

            def qload(i):
                src, dstW, nkt, ch, cpk = plan[i]
                buf = SC if i % 2 == 0 else SD
                q = "sp" if i % 2 == 0 else "gp"
                em.wait(q, "dve")
                em.dma(q, buf[:].rearrange("p (kt t) -> p kt t", kt=nkt),
                       src.rearrange("(kt p) n -> p kt n", kt=nkt)
                       [:, :, ch * cpk:(ch + 1) * cpk])
                qmarks[i] = em.mark_dma(q)

            qmarks = [0] * len(plan)
            qload(0)
            for i, (src, dstW, nkt, ch, cpk) in enumerate(plan):
                if i + 1 < len(plan):
                    qload(i + 1)
                buf = SC if i % 2 == 0 else SD
                q = "sp" if i % 2 == 0 else "gp"
                em.wait_dma("scalar", q, value=qmarks[i])
                em.act(lambda h, buf=buf: h.activation(
                    buf[:], buf[:], AF.Identity,
                    scale=64.0, bias=c_small[:, 5:6]), inc=True)
                em.wait("vector", "act")
                em.dve(lambda h, buf=buf: h.tensor_scalar(
                    out=buf[:], in0=buf[:],
                    scalar1=MAGIC, scalar2=127.0,
                    op0=ALU.subtract, op1=ALU.min), inc=True)
                em.dve(lambda h, buf=buf, dstW=dstW, nkt=nkt, ch=ch, cpk=cpk:
                       h.tensor_scalar(
                           out=dstW[:].rearrange("p (kt t) -> p kt t", kt=nkt)
                           [:, :, ch * cpk:(ch + 1) * cpk],
                           in0=buf[:].rearrange("p (kt t) -> p kt t", kt=nkt),
                           scalar1=-128.0, scalar2=1.0 / 64.0,
                           op0=ALU.max, op1=ALU.mult), inc=True)

        def p7main(em):
            Xq = SD.bitcast(BF16)
            Hq = SC.bitcast(BF16)
            em.wait("tensor", "dve")
            for tch in range(4):
                em.wait("sp", "pe")
                em.wait("sp", "dve")
                em.dma("sp", Xq[:, 0:DT * 512].rearrange("p (dt t) -> p dt t", dt=DT),
                       fm(s_xq)[:, :, tch * 512:(tch + 1) * 512])
                em.wait_dma("tensor", "sp")
                em.wait("vector", "pe")
                for ht in range(32):
                    hb = ps5 if ht % 2 == 0 else ps6
                    em.wait("tensor", "dve", lag=1)
                    for kt in range(DT):
                        em.pe(lambda h, ht=ht, kt=kt, hb=hb: h.matmul(
                            hb[:, 0:512],
                            W1q[:, kt * 4096 + ht * 128:kt * 4096 + ht * 128 + 128],
                            Xq[:, kt * 512:kt * 512 + 512],
                            start=(kt == 0), stop=(kt == DT - 1)),
                            inc=(kt == DT - 1))
                    em.wait("scalar", "pe")
                    em.act(lambda h, hb=hb, ht=ht: h.activation(
                        ST[:, 0:512], hb[:, 0:512], AF.Identity,
                        scale=16.0, bias=CB[:, ht:ht + 1]), inc=True)
                    em.wait("vector", "act")
                    em.dve(lambda h: h.tensor_scalar(
                        out=ST[:, 512:1024], in0=ST[:, 0:512],
                        scalar1=MAGIC, scalar2=127.0,
                        op0=ALU.subtract, op1=ALU.min), inc=True)
                    em.dve(lambda h, ht=ht: h.tensor_scalar(
                        out=HQ[:, ht * 512:(ht + 1) * 512], in0=ST[:, 512:1024],
                        scalar1=0.0, scalar2=0.0625,
                        op0=ALU.max, op1=ALU.mult), inc=True)
                em.wait("tensor", "dve")
                for dth in range(2):
                    if dth == 1:
                        em.wait("tensor", "act")
                    for ht in range(32):
                        for i in range(4):
                            dt = dth * 4 + i
                            em.pe(lambda h, ht=ht, i=i, dt=dt: h.matmul(
                                OB[i][:, 0:512],
                                W2q[:, ht * 1024 + dt * 128:ht * 1024 + dt * 128 + 128],
                                HQ[:, ht * 512:(ht + 1) * 512],
                                start=(ht == 0), stop=(ht == 31)),
                                inc=(ht == 31 and i == 3))
                    em.wait("scalar", "pe")
                    for i in range(4):
                        dt = dth * 4 + i
                        em.act(lambda h, i=i, dt=dt: h.activation(
                            SC[:, i * 512:(i + 1) * 512], OB[i][:, 0:512],
                            AF.Identity, scale=1.0,
                            bias=CB[:, 32 + dt:32 + dt + 1]), inc=(i == 3))
                    em.wait("sync", "act")
                    em.dma("sp", fm(s_ffn1)[:, dth * 4:dth * 4 + 4,
                                            tch * 512:(tch + 1) * 512],
                           SC[:, 0:2048].rearrange("p (dt t) -> p dt t", dt=4))
                    em.wait_dma("scalar", "sp")
            if "ffn1" in taps:
                em.dma("sp", taps["ffn1"], s_ffn1)
                em.wait_dma("sync", "sp")

        SC = sb("SC_p7", [128, 4096], F32)
        SD = sb("SD_p7", [128, 4096], F32)

        def p7(em):
            quant_weights(em)
            p7main(em)

        HQ = sb("HQ_p7", [128, 32 * 512], BF16)
        em.phase(p7)
        sb_free("W1q_p7")

    LPAD = PAD
    CTW = LPAD + T

    def mk_p8():
        Vc = sb("Vc_p8", [128, 4 * CTW], BF16)
        PFM = sb("PFM_p8", [10, T], F32)
        PB = sb("PB_p8", [2, T], BF16)
        CGT = sb("CGT_p8", [128, 8], F32)
        Xf = sb("Xf_p8", [128, DT * T], BF16)
        Wp = sb("Wp_p8", [128, DT * 512], BF16)
        DG = sb("DG_p8", [128, 4 * 128], BF16)
        CVW = sb("CVW_p8", [128, 16], F32)
        IDB = sb("IDB_p8", [128, 128], BF16)
        LIN = sb("LIN_p8", [128, 4 * CTW], BF16)
        OUT = sb("OUT_p8", [128, 4 * CTW], BF16)

        def p8(em):
            em.wait_cc("gpsimd")
            srcmap = {0: (0, 0), 1: (1, 0), 2: (1, 1), 3: (0, 1)}
            for tq in range(4):
                r, pos = srcmap[tq]
                em.dma("gp", Xf[:].rearrange("p (dt t) -> p dt t", dt=DT)
                       [:, :, tq * 1024:(tq + 1) * 1024],
                       cc_x_o[r * D:(r + 1) * D, pos * 1024:(pos + 1) * 1024]
                       .rearrange("(dt p) t -> p dt t", dt=DT))
            em.dma("sp", IDB[:], ident_b)
            em.wait_dma("vector", "gp")
            em.wait_dma("vector", "sp")

            if "xn1f" in taps:
                em.wait_dma("gp", "gp")
                em.dma("gp", fm(taps["xn1f"]),
                       Xf[:].rearrange("p (dt t) -> p dt t", dt=DT))
                em.wait_dma("gp", "gp")
            def proj_conv(wten, cvten, dst, do_norm, spill=None):
                em.wait("gp", "pe")
                em.dma("gp", Wp[:].rearrange("p (kt n) -> p kt n", kt=DT),
                       wten.rearrange("(kt p) n -> p kt n", kt=DT))
                em.dma("sp", CVW[:], cvten)
                em.wait("vector", "pe")
                em.dve(lambda h: h.memset(LIN[:].rearrange(
                    "p (ct t) -> p ct t", ct=4)[:, :, 0:LPAD], 0.0), inc=True)
                em.dve(lambda h, dst=dst: h.memset(dst[:].rearrange(
                    "p (ct t) -> p ct t", ct=4)[:, :, 0:LPAD], 0.0), inc=True)
                em.wait_dma("tensor", "gp")
                em.wait_dma("tensor", "sp")
                em.wait("tensor", "dve")
                for ct in range(4):
                    for tch in range(8):
                        bank = ps5 if (ct * 8 + tch) % 2 == 0 else ps6
                        em.wait("tensor", "act", lag=1)
                        for kt in range(DT):
                            em.pe(lambda h, ct=ct, tch=tch, kt=kt, bank=bank:
                                  h.matmul(
                                      bank[:, 0:512],
                                      Wp[:, kt * 512 + ct * 128:
                                         kt * 512 + ct * 128 + 128],
                                      Xf[:, kt * T + tch * 512:
                                         kt * T + tch * 512 + 512],
                                      start=(kt == 0), stop=(kt == DT - 1)),
                                  inc=(kt == DT - 1))
                        em.wait("scalar", "pe")
                        em.act(lambda h, ct=ct, tch=tch, bank=bank: h.activation(
                            LIN[:, ct * CTW + LPAD + tch * 512:
                                ct * CTW + LPAD + tch * 512 + 512],
                            bank[:, 0:512], AF.Identity), inc=True)
                if DBG_SPILL_LIN and spill is not None and "wdump" in taps:
                    em.wait("gpsimd", "pe")
                    em.dma("gp", taps["wdump"], Wp[:, 0:4096])
                if DBG_SPILL_LIN and spill is not None:
                    em.wait("sync", "act")
                    em.dma("sp", spill.rearrange("(ct p) t -> p ct t", ct=4),
                           LIN[:].rearrange("p (ct t) -> p ct t", ct=4)
                           [:, :, LPAD:LPAD + T])
                    em.wait_dma("tensor", "sp")
                    em.wait_dma("vector", "sp")
                em.wait("vector", "act")
                for ct in range(4):
                    em.wait("vector", "pe")
                    for j in range(4):
                        em.dve(lambda h, ct=ct, j=j: h.tensor_single_scalar(
                            out=DG[:, j * 128:j * 128 + 128],
                            in_=IDB[:], scalar=CVW[:, ct * 4 + j:ct * 4 + j + 1],
                            op=ALU.mult), inc=(j == 3))
                    for tch in range(8):
                        bank = ps5 if tch % 2 == 0 else ps6
                        em.wait("tensor", "act", lag=1)
                        em.wait("tensor", "dve")
                        for j in range(4):
                            em.pe(lambda h, ct=ct, tch=tch, j=j, bank=bank:
                                  h.matmul(
                                      bank[:, 0:512],
                                      DG[:, j * 128:j * 128 + 128],
                                      LIN[:, ct * CTW + LPAD + tch * 512 - 3 + j:
                                          ct * CTW + LPAD + tch * 512 - 3 + j + 512],
                                      start=(j == 0), stop=(j == 3)),
                                  inc=(j == 3))
                        em.wait("scalar", "pe")
                        em.act(lambda h, ct=ct, tch=tch, bank=bank, dst=dst:
                               h.activation(
                                   dst[:, ct * CTW + LPAD + tch * 512:
                                       ct * CTW + LPAD + tch * 512 + 512],
                                   bank[:, 0:512], AF.Silu), inc=True)
                em.wait("vector", "act")
                if do_norm:
                    for hh in range(2):
                        for tch in range(8):
                            em.wait("scalar", "pe")
                            for k in range(2):
                                ct = hh * 2 + k
                                em.act(lambda h, ct=ct, tch=tch, k=k: h.activation(
                                    LIN[:, k * 512:(k + 1) * 512],
                                    dst[:, ct * CTW + LPAD + tch * 512:
                                        ct * CTW + LPAD + tch * 512 + 512],
                                    AF.Square), inc=(k == 1))
                            em.wait("tensor", "act")
                            for k in range(2):
                                em.pe(lambda h, k=k: h.matmul(
                                    ps7[0:1, 0:512], c_ones_b[:],
                                    LIN[:, k * 512:(k + 1) * 512],
                                    start=(k == 0), stop=(k == 1)), inc=(k == 1))
                            em.wait("vector", "pe")
                            em.dve(lambda h: h.tensor_single_scalar(
                                out=ST[0:1, 0:512], in_=ps7[0:1, 0:512],
                                scalar=1e-6, op=ALU.add), inc=True)
                            em.wait("scalar", "dve")
                            em.act(lambda h: h.activation(ST[0:1, 512:1024],
                                                          ST[0:1, 0:512], AF.Sqrt),
                                   inc=True)
                            em.wait("vector", "act")
                            em.dve(lambda h: h.reciprocal(out=ST[0:1, 0:512],
                                                          in_=ST[0:1, 512:1024]),
                                   inc=True)
                            em.wait("tensor", "dve")
                            em.pe(lambda h: h.matmul(ps7[:, 0:512], c_onesr_f[:],
                                                     ST[0:1, 0:512],
                                                     start=True, stop=True),
                                  inc=True)
                            em.wait("vector", "pe")
                            for k in range(2):
                                ct = hh * 2 + k
                                sl = slice(ct * CTW + LPAD + tch * 512,
                                           ct * CTW + LPAD + tch * 512 + 512)
                                em.dve(lambda h, sl=sl: h.tensor_tensor(
                                    out=dst[:, sl], in0=dst[:, sl],
                                    in1=ps7[:, 0:512], op=ALU.mult), inc=(k == 1))
                            em.wait("tensor", "dve")
                            em.wait("scalar", "dve")
                if spill is not None and not DBG_SPILL_LIN:
                    em.wait("sync", "act")
                    em.wait("sync", "dve")
                    em.dma("sp", spill.rearrange("(ct p) t -> p ct t", ct=4),
                           dst[:].rearrange("p (ct t) -> p ct t", ct=4)
                           [:, :, LPAD:LPAD + T])

            proj_conv(wq1, cvq, OUT, True, spill=s_qc)
            em.wait_dma("tensor", "sp")
            proj_conv(wk1, cvk, OUT, True, spill=s_kc)
            em.wait_dma("tensor", "sp")
            proj_conv(wv1, cvv, Vc, False, spill=None)

            em.wait("gp", "pe")
            em.dma("gp", Wp[:, 0:DT * 2].rearrange("p (kt n) -> p kt n", kt=DT),
                   wb1.rearrange("(kt p) n -> p kt n", kt=DT))
            em.wait_dma("tensor", "gp")
            for tt8 in range(8):
                em.wait("tensor", "act")
                for kt in range(DT):
                    em.pe(lambda h, tt8=tt8, kt=kt: h.matmul(
                        ps5[0:2, 0:512],
                        Wp[:, kt * 2:kt * 2 + 2],
                        Xf[:, kt * T + tt8 * 512:kt * T + tt8 * 512 + 512],
                        start=(kt == 0), stop=(kt == DT - 1)),
                        inc=(kt == DT - 1))
                em.wait("scalar", "pe")
                em.act(lambda h, tt8=tt8: h.activation(
                    PB[0:2, tt8 * 512:tt8 * 512 + 512], ps5[0:2, 0:512],
                    AF.Sigmoid), inc=True)
            if "vc1" in taps:
                em.wait("gpsimd", "act")
                em.wait("gpsimd", "dve")
                em.dma("gp", taps["vc1"].rearrange("(ct p) t -> p ct t", ct=4),
                       Vc[:].rearrange("p (ct t) -> p ct t", ct=4)
                       [:, :, LPAD:LPAD + T])
            if "qc1" in taps:
                em.wait_dma("gp", "sp")
                em.dma("gp", taps["qc1"], s_qc)
            if "kc1" in taps:
                em.wait_dma("gp", "sp")
                em.dma("gp", taps["kc1"], s_kc)

            em.wait("vector", "act")
            em.dve(lambda h: h.memset(ST[0:1, 0:128], 0.0), inc=True)
            em.wait("sp", "dve")
            em.dma("sp", ST[0:1, 0:2], ltmp)
            em.dma("sp", ST[0:1, 128:130], epsp)
            em.wait_dma("scalar", "sp")
            em.act(lambda h: h.activation(ST[0:1, 256:384], ST[0:1, 0:128],
                                          AF.Exp), inc=True)
            em.wait("vector", "act")
            em.dve(lambda h: h.reciprocal(out=ST[0:1, 0:128], in_=ST[0:1, 256:384]),
                   inc=True)
            em.dve(lambda h: h.tensor_scalar(out=ST[0:1, 128:256],
                                             in0=ST[0:1, 128:256],
                                             scalar1=0.0, scalar2=0.2,
                                             op0=ALU.max, op1=ALU.min), inc=True)
            em.dve(lambda h: h.tensor_scalar(out=ST[0:1, 384:512],
                                             in0=ST[0:1, 128:256],
                                             scalar1=-5.0, scalar2=1.0,
                                             op0=ALU.mult, op1=ALU.add), inc=True)
            em.wait("tensor", "dve")
            for src_off, dst_col in ((0, 0), (128, 4), (384, 2)):
                em.pe(lambda h, src_off=src_off: h.matmul(
                    ps7[:, 0:128], c_onesr_f[:], ST[0:1, src_off:src_off + 128],
                    start=True, stop=True), inc=True)
                em.wait("vector", "pe")
                em.dve(lambda h, dst_col=dst_col: h.tensor_copy(
                    out=CGT[:, dst_col:dst_col + 2], in_=ps7[:, 0:2]), inc=True)
                em.wait("tensor", "dve")

            em.wait("gp", "pe")
            em.dma("gp", Wp[:, 0:DT * 16].rearrange("p (kt n) -> p kt n", kt=DT)
                   [:, :, 0:10], gw1.rearrange("(kt p) n -> p kt n", kt=DT))
            em.dma("sp", ST[0:1, 0:10], gb1)
            em.wait_dma("tensor", "sp")
            em.pe(lambda h: h.matmul(ps7[:, 0:128], c_onesr_f[:],
                                     ST[0:1, 0:128], start=True, stop=True),
                  inc=True)
            em.wait("vector", "pe")
            em.dve(lambda h: h.tensor_copy(out=c_small[:, 6:16], in_=ps7[:, 0:10]),
                   inc=True)
            em.wait_dma("tensor", "gp")
            Tg = ST
            bs_ = lambda sid: 64 * sid
            lg_ = lambda t: Tg[:, bs_(t * 2):bs_(t * 2) + 10]
            g5_ = lambda sid: Tg[:, bs_(sid) + 32:bs_(sid) + 37]
            mx_ = lambda sid: Tg[:, bs_(sid) + 16:bs_(sid) + 17]
            nmx_ = lambda sid: Tg[:, bs_(sid) + 17:bs_(sid) + 18]
            sm_ = lambda sid: Tg[:, bs_(sid) + 18:bs_(sid) + 19]
            rs_ = lambda sid: Tg[:, bs_(sid) + 19:bs_(sid) + 20]
            ex_ = lambda sid: Tg[:, bs_(sid) + 24:bs_(sid) + 29]
            ou_ = lambda t, hh: Tg[:, 512 + t * 16 + hh * 5:
                                   512 + t * 16 + hh * 5 + 5]
            ou10_ = lambda t: Tg[:, 512 + t * 16:512 + t * 16 + 10]
            pbank = [ps7, ps5]
            for tp in range(16):
                em.wait("tensor", "dve")
                for t in range(2):
                    for kt in range(DT):
                        em.pe(lambda h, tp=tp, t=t, kt=kt: h.matmul(
                            pbank[t][:, 0:10],
                            Xf[:, kt * T + (tp * 2 + t) * 128:
                               kt * T + (tp * 2 + t) * 128 + 128],
                            Wp[:, kt * 16:kt * 16 + 10],
                            start=(kt == 0), stop=(kt == DT - 1)),
                            inc=(kt == DT - 1))
                em.wait("vector", "pe")
                for t in range(2):
                    em.dve(lambda h, t=t: h.tensor_tensor(
                        out=lg_(t), in0=pbank[t][:, 0:10],
                        in1=c_small[:, 6:16], op=ALU.add), inc=(t == 1))
                ss = [(0, 0), (0, 1), (1, 0), (1, 1)]
                for t, hh in ss:
                    sid = t * 2 + hh
                    em.dve(lambda h, t=t, hh=hh, sid=sid: h.tensor_single_scalar(
                        out=g5_(sid), in_=lg_(t)[:, hh * 5:hh * 5 + 5],
                        scalar=CGT[:, hh:hh + 1], op=ALU.mult),
                        inc=(sid == 3))
                for t, hh in ss:
                    sid = t * 2 + hh
                    em.dve(lambda h, sid=sid: h.tensor_reduce(
                        out=mx_(sid), in_=g5_(sid), axis=mybir.AxisListType.X,
                        op=ALU.max), inc=(sid == 3))
                for t, hh in ss:
                    sid = t * 2 + hh
                    em.dve(lambda h, sid=sid: h.tensor_single_scalar(
                        out=nmx_(sid), in_=mx_(sid), scalar=-1.0,
                        op=ALU.mult), inc=(sid == 3))
                em.wait("scalar", "dve")
                for t, hh in ss:
                    sid = t * 2 + hh
                    em.act(lambda h, sid=sid: h.activation(
                        ex_(sid), g5_(sid), AF.Exp, bias=nmx_(sid)),
                        inc=(sid == 3))
                em.wait("vector", "act")
                for t, hh in ss:
                    sid = t * 2 + hh
                    em.dve(lambda h, sid=sid: h.tensor_reduce(
                        out=sm_(sid), in_=ex_(sid), axis=mybir.AxisListType.X,
                        op=ALU.add), inc=(sid == 3))
                for t, hh in ss:
                    sid = t * 2 + hh
                    em.dve(lambda h, sid=sid: h.reciprocal(
                        out=rs_(sid), in_=sm_(sid)), inc=(sid == 3))
                for t, hh in ss:
                    sid = t * 2 + hh
                    em.dve(lambda h, sid=sid: h.tensor_single_scalar(
                        out=ex_(sid), in_=ex_(sid), scalar=rs_(sid),
                        op=ALU.mult), inc=(sid == 3))
                for t, hh in ss:
                    sid = t * 2 + hh
                    em.dve(lambda h, sid=sid, hh=hh: h.tensor_single_scalar(
                        out=ex_(sid), in_=ex_(sid),
                        scalar=CGT[:, 2 + hh:3 + hh], op=ALU.mult),
                        inc=(sid == 3))
                for t, hh in ss:
                    sid = t * 2 + hh
                    em.dve(lambda h, sid=sid, t=t, hh=hh: h.tensor_single_scalar(
                        out=ou_(t, hh), in_=ex_(sid),
                        scalar=CGT[:, 4 + hh:5 + hh], op=ALU.add),
                        inc=(sid == 3))
                em.wait("tensor", "dve")
                for t in range(2):
                    em.pe(lambda h, t=t: h.transpose(
                        pbank[t][0:10, 128:256], ou10_(t), c_ident_f[:]),
                        inc=(t == 1))
                em.wait("vector", "pe")
                for t in range(2):
                    em.dve(lambda h, tp=tp, t=t: h.tensor_copy(
                        out=PFM[0:10, (tp * 2 + t) * 128:(tp * 2 + t) * 128 + 128],
                        in_=pbank[t][0:10, 128:256]), inc=(t == 1))
            if "probs" in taps:
                em.wait("gpsimd", "dve")
                em.dma("gp", taps["probs"], PFM[0:10, 0:T])
                em.wait_dma("gpsimd", "gp")
            em.wait_dma("gpsimd", "gp")
            em.wait_dma("sync", "sp")

        em.phase(p8)
        sb_free("Xf_p8")
        return Vc, PFM, PB, CGT

    def mk_p9(Vc, PFM, PB, CGT):
        c_selh = sb("c_selh9", [2, 256], BF16)
        c_msl = sb("c_msl9", [128, 128], F32)
        c_msu = sb("c_msu9", [128, 128], F32)
        c_tril = sb("c_tril9", [128, 128], F32)
        c_triu = sb("c_triu9", [128, 128], F32)
        FIRW = sb("FIRW_p9", [128, 4 * 81], F32)
        OACC = sb("OACC_p9", [128, 4 * TO], F32)
        ONRM = sb("ONRM_p9", [128, 4 * T], BF16)
        DGF = sb("DGF_p9", [128, 63 * 128], BF16)
        QS = sb("QS_p9", [128, 2 * 512], F32)
        KS = sb("KS_p9", [128, 2 * 512], F32)
        GAM = sb("GAM_p9", [128, 2], F32)
        CH = sb("CH_p9", [128, 16 * 128], F32)
        UW = sb("UW_p9", [128, 6 * 256], F32)
        SF = sb("SF_p9", [128, 2 * 512], F32)
        TMPF = sb("TMPF_p9", [128, 512], F32)
        PBS = sb("PBS_p9", [128, 512], F32)
        SEL10 = sb("SEL10_p9", [10, 10 * 128], F32)

        def cht(i):
            return CH[:, i * 128:(i + 1) * 128]

        UWr = UW[:, 0:512]
        UT = UW[:, 512:768]
        WT = UW[:, 768:1024]
        WF = UW[:, 1024:1280]
        KT = UW[:, 1280:1536]

        def p9(em):
            em.dma("sp", c_selh[:], selh)
            em.dma("sp", c_msl[:], msl_neg)
            em.dma("sp", c_msu[:], msu_neg)
            em.dma("sp", c_tril[:], tril_f)
            em.dma("sp", c_triu[:], triu_f)
            em.dma("sp", FIRW[:, 0:12], firs)
            em.dma("sp", FIRW[:, 12:72], firm)
            em.dma("sp", FIRW[:, 72:324], firl)
            em.dma("sp", GAM[:], gamm)
            em.dma("sp", SEL10[:], sel10)
            em.wait_dma("vector", "sp")
            em.dve(lambda h: h.memset(SF[:], 0.0), inc=True)

            def pbc_load(brow, toff, width):
                em.wait("tensor", "dve")
                em.pe(lambda h, brow=brow, toff=toff, width=width: h.matmul(
                    ps7[:, 0:width], SEL10[0:10, brow * 128:(brow + 1) * 128],
                    PFM[0:10, toff:toff + width],
                    start=True, stop=True), inc=True)
                em.wait("vector", "pe")
                em.dve(lambda h, width=width: h.tensor_copy(
                    out=PBS[:, 0:width], in_=ps7[:, 0:width]), inc=True)

            for half in range(2):
                hb = half * TO
                em.wait("vector", "pe")
                em.dve(lambda h: h.memset(OACC[:], 0.0), inc=True)
                for fi, (K, fw0) in enumerate(((3, 0), (15, 12), (63, 72))):
                    for ct in range(4):
                        em.wait("vector", "pe")
                        for j in range(K):
                            em.dve(lambda h, ct=ct, j=j, fw0=fw0, K=K:
                                   h.tensor_single_scalar(
                                       out=DGF[:, j * 128:j * 128 + 128],
                                       in_=IDB9[:],
                                       scalar=FIRW[:, fw0 + ct * K + j:
                                                   fw0 + ct * K + j + 1],
                                       op=ALU.mult), inc=(j == K - 1))
                        hh = ct // 2
                        for tch in range(4):
                            bank = OB[tch % 2]
                            pbc_load(hh * 5 + fi, hb + tch * 512, 512)
                            em.wait("tensor", "dve")
                            for j in range(K):
                                base = ct * CTW + LPAD + hb + tch * 512 - (K - 1) + j
                                em.pe(lambda h, j=j, base=base, bank=bank: h.matmul(
                                    bank[:, 0:512],
                                    DGF[:, j * 128:j * 128 + 128],
                                    Vc[:, base:base + 512],
                                    start=(j == 0), stop=(j == K - 1)),
                                    inc=(j == K - 1))
                            em.wait("vector", "pe")
                            em.dve(lambda h, bank=bank: h.tensor_tensor(
                                out=TMPF[:], in0=bank[:, 0:512], in1=PBS[:, 0:512],
                                op=ALU.mult), inc=True)
                            osl = slice(ct * TO + tch * 512, ct * TO + tch * 512 + 512)
                            em.dve(lambda h, osl=osl: h.tensor_tensor(
                                out=OACC[:, osl], in0=OACC[:, osl], in1=TMPF[:],
                                op=ALU.add), inc=True)
                for ct in range(4):
                    hh = ct // 2
                    for tch in range(4):
                        pbc_load(hh * 5 + 4, hb + tch * 512, 512)
                        em.dve(lambda h, ct=ct, tch=tch, hb=hb: h.tensor_tensor(
                            out=TMPF[:],
                            in0=Vc[:, ct * CTW + LPAD + hb + tch * 512:
                                   ct * CTW + LPAD + hb + tch * 512 + 512],
                            in1=PBS[:, 0:512], op=ALU.mult), inc=True)
                        osl = slice(ct * TO + tch * 512, ct * TO + tch * 512 + 512)
                        em.dve(lambda h, osl=osl: h.tensor_tensor(
                            out=OACC[:, osl], in0=OACC[:, osl], in1=TMPF[:],
                            op=ALU.add), inc=True)
                for hh in range(2):
                    for g in range(4):
                        gi = half * 16 + g * 4
                        em.wait("gp", "pe")
                        em.wait("gp", "dve")
                        em.dma("gp", QS[:].rearrange("p (k t) -> p k t", k=2),
                               s_qc[hh * 2 * 128:(hh * 2 + 2) * 128,
                                    gi * 128:gi * 128 + 512]
                               .rearrange("(k p) t -> p k t", k=2))
                        em.dma("gp", KS[:].rearrange("p (k t) -> p k t", k=2),
                               s_kc[hh * 2 * 128:(hh * 2 + 2) * 128,
                                    gi * 128:gi * 128 + 512]
                               .rearrange("(k p) t -> p k t", k=2))
                        em.wait_dma("tensor", "gp")
                        em.wait_dma("vector", "gp")
                        for i4 in range(4):
                            ci = gi + i4
                            qn = [QS[:, k * 512 + i4 * 128:k * 512 + i4 * 128 + 128]
                                  for k in range(2)]
                            kn = [KS[:, k * 512 + i4 * 128:k * 512 + i4 * 128 + 128]
                                  for k in range(2)]
                            vsl = [Vc[:, (hh * 2 + k) * CTW + LPAD + ci * 128:
                                      (hh * 2 + k) * CTW + LPAD + ci * 128 + 128]
                                   for k in range(2)]
                            em.wait("tensor", "dve")
                            em.pe(lambda h, ci=ci, hh=hh: h.matmul(
                                ps7[:, 0:128],
                                c_selh[0:2, hh * 128:hh * 128 + 128],
                                PB[0:2, ci * 128:ci * 128 + 128],
                                start=True, stop=True), inc=True)
                            em.wait("vector", "pe")
                            for k in range(2):
                                em.dve(lambda h, k=k, kn=kn: h.tensor_tensor(
                                    out=CH[:, (12 + k) * 128:(12 + k) * 128 + 128],
                                    in0=kn[k], in1=ps7[:, 0:128], op=ALU.mult),
                                    inc=True)
                                em.dve(lambda h, k=k, vsl=vsl: h.tensor_tensor(
                                    out=CH[:, (14 + k) * 128:(14 + k) * 128 + 128],
                                    in0=vsl[k], in1=ps7[:, 0:128], op=ALU.mult),
                                    inc=(k == 1))
                            if "dbg0" in taps and half == 0 and hh == 0 \
                                    and ci == 0:
                                em.dve(lambda h: h.tensor_copy(
                                    out=CH[:, 0:128], in_=ps7[:, 0:128]),
                                    inc=True)
                                em.dve(lambda h, kn=kn: h.tensor_copy(
                                    out=CH[:, 128:256], in_=kn[0]), inc=True)
                            kb = [CH[:, (12 + k) * 128:(12 + k) * 128 + 128]
                                  for k in range(2)]
                            vb = [CH[:, (14 + k) * 128:(14 + k) * 128 + 128]
                                  for k in range(2)]
                            em.wait("tensor", "dve")
                            for k in range(2):
                                em.pe(lambda h, k=k, kb=kb, kn=kn: h.matmul(
                                    ps5[:, 0:128], kb[k], kn[k],
                                    start=(k == 0), stop=(k == 1)), inc=(k == 1))
                            em.wait("vector", "pe")
                            em.dve(lambda h: h.tensor_tensor(
                                out=cht(2), in0=ps5[:, 0:128], in1=c_msl[:],
                                op=ALU.mult), inc=True)
                            em.wait("tensor", "dve")
                            for k in range(2):
                                em.pe(lambda h, k=k, kb=kb, kn=kn: h.matmul(
                                    ps5[:, 0:128], kn[k], kb[k],
                                    start=(k == 0), stop=(k == 1)), inc=(k == 1))
                            em.wait("vector", "pe")
                            em.dve(lambda h: h.tensor_tensor(
                                out=TMPF[:, 0:128], in0=ps5[:, 0:128],
                                in1=c_msu[:], op=ALU.mult), inc=True)
                            em.dve(lambda h: h.tensor_copy(out=cht(3),
                                                           in_=TMPF[:, 0:128]),
                                   inc=True)
                            em.dve(lambda h: h.tensor_tensor(
                                out=cht(7), in0=TMPF[:, 0:128], in1=c_ident_f[:],
                                op=ALU.add), inc=True)
                            for lev in range(1, 7):
                                em.wait("tensor", "dve")
                                em.pe(lambda h: h.matmul(ps6[:, 0:128], cht(3),
                                                         cht(2), start=True,
                                                         stop=True), inc=True)
                                em.wait("vector", "pe")
                                em.dve(lambda h: h.tensor_copy(out=cht(4),
                                                               in_=ps6[:, 0:128]),
                                       inc=True)
                                em.dve(lambda h: h.tensor_tensor(
                                    out=cht(6), in0=ps6[:, 0:128], in1=c_ident_f[:],
                                    op=ALU.add), inc=True)
                                em.wait("tensor", "dve")
                                em.pe(lambda h: h.matmul(ps6[:, 0:128], cht(2),
                                                         cht(3), start=True,
                                                         stop=True), inc=True)
                                em.wait("vector", "pe")
                                em.dve(lambda h: h.tensor_copy(out=cht(5),
                                                               in_=ps6[:, 0:128]),
                                       inc=True)
                                em.wait("tensor", "dve")
                                em.pe(lambda h: h.matmul(ps6[:, 0:128], cht(6),
                                                         cht(7), start=True,
                                                         stop=True), inc=True)
                                em.wait("vector", "pe")
                                em.dve(lambda h: h.tensor_copy(out=cht(7),
                                                               in_=ps6[:, 0:128]),
                                       inc=True)
                                em.dve(lambda h: h.tensor_copy(out=cht(2),
                                                               in_=cht(4)), inc=True)
                                em.dve(lambda h: h.tensor_copy(out=cht(3),
                                                               in_=cht(5)), inc=True)
                            for k in range(2):
                                em.wait("tensor", "dve")
                                em.pe(lambda h, k=k, vb=vb: h.transpose(
                                    ps6[:, 0:128], vb[k], c_ident_f[:]), inc=True)
                                em.wait("vector", "pe")
                                em.dve(lambda h, k=k: h.tensor_copy(
                                    out=UWr[:, k * 128:k * 128 + 128],
                                    in_=ps6[:, 0:128]), inc=True)
                                em.wait("tensor", "dve")
                                em.pe(lambda h, k=k, kb=kb: h.transpose(
                                    ps6[:, 0:128], kb[k], c_ident_f[:]), inc=True)
                                em.wait("vector", "pe")
                                em.dve(lambda h, k=k: h.tensor_copy(
                                    out=UWr[:, 256 + k * 128:256 + k * 128 + 128],
                                    in_=ps6[:, 0:128]), inc=True)
                            em.wait("tensor", "dve")
                            em.pe(lambda h: h.matmul(ps6[:, 0:512], cht(7), UWr,
                                                     start=True, stop=True),
                                  inc=True)
                            em.wait("vector", "pe")
                            em.dve(lambda h: h.tensor_copy(out=UT, in_=ps6[:, 0:256]),
                                   inc=True)
                            em.dve(lambda h: h.tensor_copy(out=WT,
                                                           in_=ps6[:, 256:512]),
                                   inc=True)
                            for k in range(2):
                                em.wait("tensor", "dve")
                                em.pe(lambda h, k=k: h.transpose(
                                    ps6[:, 0:128], WT[:, k * 128:k * 128 + 128],
                                    c_ident_f[:]), inc=True)
                                em.wait("vector", "pe")
                                em.dve(lambda h, k=k: h.tensor_copy(
                                    out=WF[:, k * 128:k * 128 + 128],
                                    in_=ps6[:, 0:128]), inc=True)
                                em.wait("tensor", "dve")
                                em.pe(lambda h, k=k, kn=kn: h.transpose(
                                    ps6[:, 0:128], kn[k], c_ident_f[:]), inc=True)
                                em.wait("vector", "pe")
                                em.dve(lambda h, k=k: h.tensor_copy(
                                    out=KT[:, k * 128:k * 128 + 128],
                                    in_=ps6[:, 0:128]), inc=True)
                            em.wait("tensor", "dve")
                            for k in range(2):
                                em.pe(lambda h, k=k, qn=qn, kn=kn: h.matmul(
                                    ps5[:, 0:128], kn[k], qn[k],
                                    start=(k == 0), stop=(k == 1)), inc=(k == 1))
                            em.wait("vector", "pe")
                            em.dve(lambda h: h.tensor_tensor(
                                out=cht(11), in0=ps5[:, 0:128], in1=c_triu[:],
                                op=ALU.mult), inc=True)
                            em.wait("tensor", "dve")
                            for k in range(2):
                                em.pe(lambda h, k=k, hh=hh: h.matmul(
                                    ps6[:, 0:256],
                                    WF[:, k * 128:k * 128 + 128],
                                    SF[:, hh * 512 + k * 256:hh * 512 + k * 256
                                       + 256],
                                    start=(k == 0), stop=(k == 1)), inc=(k == 1))
                            em.wait("vector", "pe")
                            em.dve(lambda h: h.tensor_tensor(
                                out=TMPF[:, 0:256], in0=UT, in1=ps6[:, 0:256],
                                op=ALU.subtract), inc=True)
                            em.dve(lambda h: h.tensor_copy(
                                out=CH[:, 1792:2048], in_=TMPF[:, 0:256]), inc=True)
                            UADJ = CH[:, 1792:2048]
                            em.wait("tensor", "dve")
                            for dvt in range(2):
                                for k in range(2):
                                    em.pe(lambda h, dvt=dvt, k=k, hh=hh, qn=qn:
                                          h.matmul(
                                              OB[dvt][:, 0:128],
                                              SF[:, hh * 512 + k * 256 + dvt * 128:
                                                 hh * 512 + k * 256 + dvt * 128
                                                 + 128],
                                              qn[k], start=(k == 0), stop=False),
                                          inc=False)
                                em.pe(lambda h, dvt=dvt: h.matmul(
                                    OB[dvt][:, 0:128],
                                    UADJ[:, dvt * 128:dvt * 128 + 128],
                                    cht(11), start=False, stop=True), inc=True)
                            for k in range(2):
                                em.pe(lambda h, k=k: h.matmul(
                                    ps6[:, 0:256],
                                    KT[:, k * 128:k * 128 + 128],
                                    UADJ[:, 0:256],
                                    start=True, stop=True), inc=True)
                                em.wait("vector", "pe")
                                ssl = slice(hh * 512 + k * 256, hh * 512 + k * 256
                                            + 256)
                                em.dve(lambda h, ssl=ssl: h.tensor_tensor(
                                    out=SF[:, ssl], in0=SF[:, ssl],
                                    in1=ps6[:, 0:256], op=ALU.add), inc=True)
                                em.wait("tensor", "dve")
                            if "dbg0" in taps and half == 0 and hh == 0 \
                                    and ci == 0:
                                em.wait("gp", "dve")
                                em.wait("gp", "pe")
                                em.dma("gp", taps["dbg0"][0:2, 0:256],
                                       PB[0:2, 0:256])
                                em.dma("gp", taps["dbg0"][:, 1024:1280],
                                       CH[:, 0:256])
                                em.dma("gp", taps["dbg0"][:, 256:1024],
                                       CH[:, 256:1024])
                                em.dma("gp", taps["dbg0"][:, 1280:2048],
                                       CH[:, 1280:2048])
                                em.dma("gp", taps["dbg0"][:, 2048:3584],
                                       UW[:, 0:1536])
                                em.dma("gp", taps["dbg0"][:, 3584:4096],
                                       SF[:, 0:512])
                                em.wait_dma("vector", "gp")
                                em.wait_dma("tensor", "gp")
                            pbc_load(hh * 5 + 3, ci * 128, 128)
                            for dvt in range(2):
                                em.dve(lambda h, dvt=dvt: h.tensor_tensor(
                                    out=TMPF[:, 256:384], in0=OB[dvt][:, 0:128],
                                    in1=PBS[:, 0:128], op=ALU.mult), inc=True)
                                osl = slice((hh * 2 + dvt) * TO + (ci - half * 16)
                                            * 128,
                                            (hh * 2 + dvt) * TO + (ci - half * 16)
                                            * 128 + 128)
                                em.dve(lambda h, osl=osl: h.tensor_tensor(
                                    out=OACC[:, osl], in0=OACC[:, osl],
                                    in1=TMPF[:, 256:384], op=ALU.add), inc=True)
                                if "dlt" in taps:
                                    em.wait("gp", "dve")
                                    em.dma("gp", taps["dlt"]
                                           [(hh * 2 + dvt) * 128:
                                            (hh * 2 + dvt + 1) * 128,
                                            ci * 128:ci * 128 + 128],
                                           TMPF[:, 256:384])
                                    em.wait_dma("vector", "gp")
                for hh in range(2):
                    for tch in range(4):
                        em.wait("scalar", "dve")
                        em.wait("scalar", "pe")
                        for k in range(2):
                            em.act(lambda h, k=k, hh=hh, tch=tch: h.activation(
                                QS[:, k * 512:(k + 1) * 512],
                                OACC[:, (hh * 2 + k) * TO + tch * 512:
                                     (hh * 2 + k) * TO + tch * 512 + 512],
                                AF.Square), inc=(k == 1))
                        em.wait("tensor", "act")
                        for k in range(2):
                            em.pe(lambda h, k=k: h.matmul(
                                ps7[0:1, 0:512], c_ones_f[:],
                                QS[:, k * 512:(k + 1) * 512],
                                start=(k == 0), stop=(k == 1)), inc=(k == 1))
                        em.wait("vector", "pe")
                        em.dve(lambda h: h.tensor_scalar(
                            out=ST[0:1, 0:512], in0=ps7[0:1, 0:512],
                            scalar1=1.0 / 256.0, scalar2=1e-5,
                            op0=ALU.mult, op1=ALU.add), inc=True)
                        em.wait("scalar", "dve")
                        em.act(lambda h: h.activation(ST[0:1, 512:1024],
                                                      ST[0:1, 0:512], AF.Sqrt),
                               inc=True)
                        em.wait("vector", "act")
                        em.dve(lambda h: h.reciprocal(out=ST[0:1, 0:512],
                                                      in_=ST[0:1, 512:1024]),
                               inc=True)
                        em.wait("tensor", "dve")
                        em.pe(lambda h: h.matmul(ps7[:, 0:512], c_onesr_f[:],
                                                 ST[0:1, 0:512],
                                                 start=True, stop=True), inc=True)
                        em.wait("vector", "pe")
                        for k in range(2):
                            ct = hh * 2 + k
                            em.dve(lambda h, ct=ct, tch=tch: h.tensor_tensor(
                                out=TMPF[:],
                                in0=OACC[:, ct * TO + tch * 512:ct * TO + tch * 512
                                        + 512],
                                in1=ps7[:, 0:512], op=ALU.mult), inc=True)
                            em.dve(lambda h, ct=ct, tch=tch, hb=hb:
                                   h.tensor_single_scalar(
                                       out=ONRM[:, ct * T + hb + tch * 512:
                                               ct * T + hb + tch * 512 + 512],
                                       in_=TMPF[:], scalar=GAM[:, k:k + 1],
                                       op=ALU.mult), inc=(k == 1))
                        em.wait("tensor", "dve")
                em.wait("sync", "dve")
                for tq in (half * 2, half * 2 + 1):
                    pos = {0: 0, 1: 2, 2: 3, 3: 1}[tq]
                    em.dma("sp", cc_o_i[:, pos * 1024:(pos + 1) * 1024]
                           .rearrange("(ct p) t -> p ct t", ct=4),
                           ONRM[:].rearrange("p (ct t) -> p ct t", ct=4)
                           [:, :, tq * 1024:(tq + 1) * 1024])
            em.wait_dma("gpsimd", "sp")
            em.collective("AllGather", ALU.bypass, [cc_o_i.opt()], [cc_o_o.opt()])
            if "onrm" in taps:
                em.wait("gpsimd", "dve")
                em.dma("gp", taps["onrm"].rearrange("(ct p) t -> p ct t", ct=4),
                       ONRM[:].rearrange("p (ct t) -> p ct t", ct=4))
                em.wait_dma("gpsimd", "gp")

        IDB9 = sb("IDB9_p9", [128, 128], BF16)

        def p9pre(em):
            em.dma("sp", IDB9[:], ident_b)
            em.wait_dma("sync", "sp")

        em.phase(p9pre)
        em.phase(p9)
        sb_free("c_msl9")

    def mk_p10():
        OGA = sb("OGA_p10", [128, 8 * T], BF16)
        OSL = sb("OSL_p10", [128, 8 * TO], BF16)
        WOT = sb("WOT_p10", [128, 8 * 1024], BF16)
        SC = sb("SC_p10", [128, 4096], F32)
        SD = sb("SD_p10", [128, 4096], F32)

        def p10(em):
            em.wait_cc("gpsimd")
            em.dma("gp", OGA[:].rearrange("p (ct t) -> p ct t", ct=8),
                   cc_o_o.rearrange("(ct p) t -> p ct t", ct=8))
            em.dma("gp", WOT[:].rearrange("p (kt n) -> p kt n", kt=DT),
                   wo1.rearrange("(kt p) n -> p kt n", kt=DT))
            em.wait_dma("vector", "gp")
            for ct in range(8):
                for tch in range(4):
                    em.dve(lambda h, ct=ct, tch=tch: h.tensor_single_scalar(
                        out=OSL[:, ct * TO + tch * 512:ct * TO + tch * 512 + 512],
                        in_=OGA[:, ct * T + TO + tch * 512:
                               ct * T + TO + tch * 512 + 512],
                        scalar=c_sel[:, 1:2], op=ALU.mult), inc=True)
                    em.dve(lambda h, ct=ct, tch=tch: h.scalar_tensor_tensor(
                        out=OSL[:, ct * TO + tch * 512:ct * TO + tch * 512 + 512],
                        in0=OGA[:, ct * T + tch * 512:ct * T + tch * 512 + 512],
                        scalar=c_sel[:, 0:1],
                        in1=OSL[:, ct * TO + tch * 512:ct * TO + tch * 512 + 512],
                        op0=ALU.mult, op1=ALU.add), inc=True)
            for tch in range(4):
                em.wait("sp", "dve")
                em.dma("sp", SC[:].rearrange("p (dt t) -> p dt t", dt=DT),
                       fm(s_x1)[:, :, tch * 512:(tch + 1) * 512])
                em.dma("sp", SD[:].rearrange("p (dt t) -> p dt t", dt=DT),
                       fm(s_ffn1)[:, :, tch * 512:(tch + 1) * 512])
                em.wait_dma("vector", "sp")
                for dt in range(DT):
                    bank = ps5 if dt % 2 == 0 else ps6
                    em.wait("tensor", "dve", lag=1)
                    for ct in range(8):
                        em.pe(lambda h, dt=dt, ct=ct, tch=tch, bank=bank: h.matmul(
                            bank[:, 0:512],
                            WOT[:, ct * 1024 + dt * 128:ct * 1024 + dt * 128 + 128],
                            OSL[:, ct * TO + tch * 512:ct * TO + tch * 512 + 512],
                            start=(ct == 0), stop=(ct == 7)), inc=(ct == 7))
                    em.wait("vector", "pe")
                    em.dve(lambda h, dt=dt, bank=bank: h.tensor_tensor(
                        out=SC[:, dt * 512:(dt + 1) * 512],
                        in0=SC[:, dt * 512:(dt + 1) * 512], in1=bank[:, 0:512],
                        op=ALU.add), inc=True)
                    em.dve(lambda h, dt=dt: h.tensor_tensor(
                        out=SC[:, dt * 512:(dt + 1) * 512],
                        in0=SC[:, dt * 512:(dt + 1) * 512],
                        in1=SD[:, dt * 512:(dt + 1) * 512],
                        op=ALU.add), inc=True)
                em.wait("sync", "dve")
                em.dma("sp", fm(y)[:, :, tch * 512:(tch + 1) * 512],
                       SC[:].rearrange("p (dt t) -> p dt t", dt=DT))
            em.wait_dma("sync", "sp")
            em.wait_dma("gpsimd", "gp")

        em.phase(p10)
        sb_free("OGA_p10")

    if STAGE >= 0:
        em.phase(p0)
    if STAGE >= 1:
        mk_p1()
    Ak = Av = None
    if STAGE >= 2:
        Ak = mk_p2()
        Av = mk_p3()
    if STAGE >= 4:
        mk_p4(Ak, Av)
        sb_free("Ak")
        sb_free("c_rowbc", side="right")
    if STAGE >= 5:
        ACC = mk_p5()
    if STAGE >= 6:
        mk_p6(ACC)
        sb_free("ACC_p5")
    if STAGE >= 7:
        mk_p7()
    if STAGE >= 8:
        Vc, PFM, PB, CGT = mk_p8()
    if STAGE >= 9:
        mk_p9(Vc, PFM, PB, CGT)
        sb_free("Vc_p8")
    if STAGE >= 10:
        mk_p10()

    def p4dbg_mk():
        Abuf = sb("Abuf_dbg", [128, DT * 512], BF16)
        SCd = sb("SCd_dbg", [128, DT * 512], F32)

        def p4dbg(em):
            em.wait_cc("gpsimd")
            em.dma("gp", ST[0:1, 0:256].rearrange("o (a n) -> o a n", a=2),
                   cc_z_o.rearrange("(a o) n -> o a n", a=2))
            em.wait_dma("vector", "gp")
            em.dve(lambda h: h.tensor_tensor(out=ST[0:1, 512:640],
                                             in0=ST[0:1, 0:128],
                                             in1=ST[0:1, 128:256], op=ALU.add),
                   inc=True)
            em.dve(lambda h: h.tensor_single_scalar(out=ST[0:1, 512:640],
                                                    in_=ST[0:1, 512:640],
                                                    scalar=1e-20, op=ALU.max),
                   inc=True)
            em.dve(lambda h: h.reciprocal(out=ST[0:1, 640:768],
                                          in_=ST[0:1, 512:640]), inc=True)
            em.wait("tensor", "dve")
            em.pe(lambda h: h.matmul(ps5[:, 0:128], c_onesr_f[:], ST[0:1, 640:768],
                                     start=True, stop=True), inc=True)
            em.wait("vector", "pe")
            em.dve(lambda h: h.tensor_copy(out=c_small[:, 3:4], in_=ps5[:, 0:1]),
                   inc=True)
            if "zz" in taps:
                em.dve(lambda h: h.tensor_copy(out=ST[0:1, 256:260],
                                               in_=c_small[0:1, 0:4]), inc=True)
                em.wait("sync", "dve")
                em.dma("sp", taps["zz"], ST[0:1, 0:260])
            for tch in range(TO // 512):
                em.dma("sp", Abuf[:].rearrange("p (dt t) -> p dt t", dt=DT),
                       fm(s_oun)[:, :, tch * 512:(tch + 1) * 512])
                em.wait_dma("vector", "sp")
                for dt in range(DT):
                    em.dve(lambda h, dt=dt: h.tensor_single_scalar(
                        out=SCd[:, dt * 512:(dt + 1) * 512],
                        in_=Abuf[:, dt * 512:(dt + 1) * 512],
                        scalar=c_small[:, 3:4], op=ALU.mult), inc=True)
                em.wait("sync", "dve")
                em.dma("sp", fm(taps["attn0"])[:, :, tch * 512:(tch + 1) * 512],
                       SCd[:].rearrange("p (dt t) -> p dt t", dt=DT))
                em.wait_dma("vector", "sp")
            em.wait_dma("sync", "sp")

        em.phase(p4dbg)
        sb_free("Abuf_dbg")

    if "attn0" in taps and STAGE >= 4:
        p4dbg_mk()

    def pfin_stub_mk():
        SCd = sb("SC_stub", [128, DT * 512], F32)

        def pfin_stub(em):
            em.dve(lambda h: h.memset(SCd[:], 0.0), inc=True)
            em.wait("sync", "dve")
            for tch in range(TO // 512):
                em.dma("sp", fm(y)[:, :, tch * 512:(tch + 1) * 512],
                       SCd[:].rearrange("p (dt t) -> p dt t", dt=DT))
            em.wait_dma("sync", "sp")

        em.phase(pfin_stub)
        sb_free("SC_stub")

    if STAGE < 10:
        pfin_stub_mk()

    return nc, taps



def _prep_core_inputs(inputs, core):
    b, par = core // 2, core % 2
    f32 = np.float32
    x = np.asarray(inputs["L0_attn_x"], f32)
    if par == 0:
        own = np.r_[np.arange(0, 1024), np.arange(3072, 4096)]
    else:
        own = np.arange(1024, 3072)
    d = {}
    d["x_own"] = np.ascontiguousarray(x[b][own].T)
    d["x_full"] = np.ascontiguousarray(x[b].T)
    d["row_idx"] = np.ascontiguousarray(own[None, :].astype(f32))
    d["col_idx"] = np.ascontiguousarray(
        np.arange(T, dtype=f32).reshape(T // 128, 128).T)
    d["ident_b"] = np.eye(128, dtype=ml_dtypes.bfloat16)
    d["ident_f"] = np.eye(128, dtype=f32)
    sl = np.tril(np.ones((128, 128), f32), -1)
    d["msl_neg"] = -sl
    d["msu_neg"] = -sl.T
    d["tril_f"] = np.tril(np.ones((128, 128), f32))
    d["triu_f"] = np.triu(np.ones((128, 128), f32))
    s01 = np.zeros((128, 2), f32)
    s01[:, par] = 1.0
    d["sel01"] = s01
    sh = np.zeros((2, 256), ml_dtypes.bfloat16)
    sh[0, 0:128] = 1.0
    sh[1, 128:256] = 1.0
    d["selh"] = sh
    s8 = np.zeros((8, 8, 128), ml_dtypes.bfloat16)
    for e_ in range(8):
        s8[e_, e_, :] = 1.0
    d["sel8"] = np.ascontiguousarray(s8.reshape(8, 8 * 128))
    s10 = np.zeros((10, 10, 128), f32)
    for e_ in range(10):
        s10[e_, e_, :] = 1.0
    d["sel10"] = np.ascontiguousarray(s10.reshape(10, 10 * 128))

    d["wq0"] = np.asarray(inputs["L0_attn_Wq"], f32)
    d["wk0"] = np.asarray(inputs["L0_attn_Wk"], f32)
    d["wv0"] = np.asarray(inputs["L0_attn_Wv"], f32)
    d["bq0"] = np.ascontiguousarray(np.asarray(inputs["L0_attn_bq"], f32)
                                    .reshape(DT, 128).T)
    d["bk0"] = np.ascontiguousarray(np.asarray(inputs["L0_attn_bk"], f32)
                                    .reshape(DT, 128).T)
    d["bv0"] = np.asarray(inputs["L0_attn_bv"], f32)[None, :]
    d["gw0"] = np.asarray(inputs["L0_ffn_gate_w"], f32)
    for k_ in ("ew1", "ew2", "ew3", "sw1", "sw2", "sw3"):
        d[k_] = np.asarray(inputs["L0_ffn_" + k_], f32)

    hs = slice(par * 2 * 256, (par * 2 + 2) * 256)
    d["wq1"] = np.ascontiguousarray(np.asarray(inputs["L1_attn_Wq"], f32).T[:, hs])
    d["wk1"] = np.ascontiguousarray(np.asarray(inputs["L1_attn_Wk"], f32).T[:, hs])
    d["wv1"] = np.ascontiguousarray(np.asarray(inputs["L1_attn_Wv"], f32).T[:, hs])
    d["wb1"] = np.ascontiguousarray(
        np.asarray(inputs["L1_attn_Wb"], f32).T[:, par * 2:par * 2 + 2])
    for nm, src in (("cvq", "L1_attn_conv_q"), ("cvk", "L1_attn_conv_k"),
                    ("cvv", "L1_attn_conv_v")):
        w = np.asarray(inputs[src], f32)[hs]
        d[nm] = np.ascontiguousarray(
            w.reshape(4, 128, 4).transpose(1, 0, 2).reshape(128, 16))
    for nm, src, K in (("firs", "L1_attn_fir_s", 3), ("firm", "L1_attn_fir_m", 15),
                       ("firl", "L1_attn_fir_l", 63)):
        fw = np.asarray(inputs[src], f32)[par * 2:par * 2 + 2]
        fw = fw.reshape(512, K).reshape(4, 128, K).transpose(1, 0, 2)
        d[nm] = np.ascontiguousarray(fw.reshape(128, 4 * K))
    gw = np.asarray(inputs["L1_attn_gate_w"], f32)[par * 2:par * 2 + 2]
    d["gw1"] = np.ascontiguousarray(gw.reshape(10, D).T)
    d["gb1"] = np.ascontiguousarray(
        np.asarray(inputs["L1_attn_gate_b"], f32)[par * 2:par * 2 + 2].reshape(1, 10))
    d["ltmp"] = np.ascontiguousarray(
        np.asarray(inputs["L1_attn_log_temp"], f32)[par * 2:par * 2 + 2].reshape(1, 2))
    d["epsp"] = np.ascontiguousarray(
        np.asarray(inputs["L1_attn_eps_param"], f32)[par * 2:par * 2 + 2].reshape(1, 2))
    d["gamm"] = np.ascontiguousarray(
        np.asarray(inputs["L1_attn_norm_w"], f32).reshape(2, 128).T)
    d["wo1"] = np.ascontiguousarray(np.asarray(inputs["L1_attn_Wo"], f32).T)
    d["w1t"] = np.ascontiguousarray(np.asarray(inputs["L1_ffn_w1"], f32).T)
    d["w2t"] = np.ascontiguousarray(np.asarray(inputs["L1_ffn_w2"], f32).T)
    d["b1f"] = np.ascontiguousarray(
        np.asarray(inputs["L1_ffn_b1"], f32).reshape(32, 128).T)
    d["b2f"] = np.ascontiguousarray(
        np.asarray(inputs["L1_ffn_b2"], f32).reshape(8, 128).T)
    return d


def kernel(**inputs):
    nc, _ = build_nc()
    in_maps = [_prep_core_inputs(inputs, c) for c in range(8)]
    res = run_bass_kernel_spmd(nc, in_maps, core_ids=list(range(8)))
    out = np.zeros((B, T, D), np.float32)
    for c in range(8):
        b, par = c // 2, c % 2
        if par == 0:
            own = np.r_[np.arange(0, 1024), np.arange(3072, 4096)]
        else:
            own = np.arange(1024, 3072)
        out[b][own] = np.asarray(res.results[c]["y"]).T
    return out


# version 04
# speedup vs baseline: 1.2456x; 1.1478x over previous
import numpy as np
import ml_dtypes
import concourse.bass as bass
import concourse.mybir as mybir
from concourse.bass_utils import run_bass_kernel_spmd

F32 = mybir.dt.float32
BF16 = mybir.dt.bfloat16
AF = mybir.ActivationFunctionType
ALU = mybir.AluOpType

B, T, D = 4, 4096, 1024
TO = 2048
DT = 8
NEGREF = -12.0
MAGIC = 12582912.0
PAIRS = [[0, 1], [2, 3], [4, 5], [6, 7]]
HMOE = 2048
CH = 128
NCH = T // CH
PAD = 64
STAGE = 10
DBG_SPILL_LIN = False

ENGS = ("sync", "gpsimd", "scalar", "vector", "tensor")


class EM:

    def __init__(self, nc):
        self.nc = nc
        self.cur = None
        self.h = None
        self.cnt = {e: 0 for e in ("pe", "dve", "act", "sp", "gp")}
        self.dmac = {q: 0 for q in ("sp", "gp", "act")}
        self.cc = 0
        self._sem_ctx = []
        self.sem = {}
        for name in ("pe", "dve", "act", "sp", "gp", "dsp", "dgp", "dact", "ccs"):
            g = nc.semaphore("sem_" + name)
            self.sem[name] = g.__enter__()
            self._sem_ctx.append(g)

    ENG2NAME = {"tensor": "pe", "vector": "dve", "scalar": "act",
                "sync": "sp", "gpsimd": "gp"}
    NORM = {"tensor": "pe", "vector": "dve", "scalar": "act", "sync": "sp",
            "gpsimd": "gp", "pe": "pe", "dve": "dve", "act": "act",
            "sp": "sp", "gp": "gp"}

    def phase(self, fn):
        snap = (dict(self.cnt), dict(self.dmac), self.cc)
        results = []
        with self.nc.Block() as blk:
            for pi, eng in enumerate(ENGS):
                self.cnt, self.dmac, self.cc = (dict(snap[0]), dict(snap[1]), snap[2])
                self.cur = self.ENG2NAME[eng]
                self._oplog_ref = getattr(self, "_oplog", None) if pi > 0 else None
                self._oplog = []
                self._opi = 0
                holder = {}

                def body(h, _fn=fn, _holder=holder):
                    self.h = h
                    _fn(self)
                    _holder["state"] = (dict(self.cnt), dict(self.dmac), self.cc)

                getattr(blk, eng)(body)
                results.append(holder["state"])
        for r in results[1:]:
            assert r == results[0], f"engine passes disagree: {results[0]} vs {r}"
        self.cnt, self.dmac, self.cc = (dict(results[0][0]), dict(results[0][1]),
                                       results[0][2])
        self.cur = None

    def _ck(self, tag):
        state = (tag, tuple(sorted(self.cnt.items())), tuple(sorted(self.dmac.items())))
        if self._oplog_ref is not None:
            assert self._opi < len(self._oplog_ref), f"extra op {state} in pass {self.cur}"
            ref = self._oplog_ref[self._opi]
            assert ref == state, f"pass {self.cur} op {self._opi}: {state} != ref {ref}"
        self._oplog.append(state)
        self._opi += 1

    def _emit(self, eng, f, inc):
        self._ck(("op", eng, inc))
        if self.cur == eng:
            inst = f(self.h)
            if inc:
                inst.then_inc(self.sem[eng], 1)
        if inc:
            self.cnt[eng] += 1

    def pe(self, f, inc=False):
        self._emit("pe", f, inc)

    def dve(self, f, inc=False):
        self._emit("dve", f, inc)

    def act(self, f, inc=False):
        self._emit("act", f, inc)

    def gp(self, f, inc=False):
        self._emit("gp", f, inc)

    def wait(self, waiter, target, lag=0):
        waiter, target = self.NORM[waiter], self.NORM[target]
        self._ck(("wait", waiter, target, lag))
        v = self.cnt[target] - lag
        if v <= 0:
            return
        if self.cur == waiter:
            self.h.wait_ge(self.sem[target], v)

    def mark(self, target):
        return self.cnt[self.NORM[target]]

    def wait_abs(self, waiter, target, value):
        waiter, target = self.NORM[waiter], self.NORM[target]
        self._ck(("wait_abs", waiter, target, value))
        if value <= 0:
            return
        if self.cur == waiter:
            self.h.wait_ge(self.sem[target], value)

    def dma(self, q, out, in_, inc=True):
        self._ck(("dma", q, inc))
        if self.cur == q:
            inst = self.h.dma_start(out=out, in_=in_)
            if inc:
                inst.then_inc(self.sem["d" + q], 16)
        if inc:
            self.dmac[q] += 16

    def mark_dma(self, q):
        return self.dmac[q]

    def wait_dma(self, waiter, q, lag=0, value=None):
        waiter = self.NORM[waiter]
        self._ck(("wait_dma", waiter, q, lag, value))
        v = (self.dmac[q] - 16 * lag) if value is None else value
        if v <= 0:
            return
        if self.cur == waiter:
            self.h.wait_ge(self.sem["d" + q], v)

    def collective(self, kind, op, ins, outs):
        if self.cur == "gp":
            self.h.collective_compute(kind, op, replica_groups=PAIRS,
                                      ins=ins, outs=outs).then_inc(self.sem["ccs"])
        self.cc += 1

    def wait_cc(self, waiter):
        waiter = self.NORM[waiter]
        if self.cur == waiter and self.cc > 0:
            self.h.wait_ge(self.sem["ccs"], self.cc)


def build_nc(debug_taps=()):
    nc = bass.Bass("TRN2", target_bir_lowering=False, debug=False,
                   detect_race_conditions=False)

    inp = {}

    def di(name, shape, dtype=F32):
        inp[name] = nc.dram_tensor(name, shape, dtype, kind="ExternalInput").ap()
        return inp[name]

    x_own = di("x_own", [D, TO])
    x_full = di("x_full", [D, T])
    row_idx = di("row_idx", [1, TO])
    col_idx = di("col_idx", [128, T // 128])
    ident_b = di("ident_b", [128, 128], BF16)
    ident_f = di("ident_f", [128, 128])
    msl_neg = di("msl_neg", [128, 128])
    msu_neg = di("msu_neg", [128, 128])
    tril_f = di("tril_f", [128, 128])
    triu_f = di("triu_f", [128, 128])
    sel01 = di("sel01", [128, 2])
    sel8 = di("sel8", [8, 8 * 128], BF16)
    sel10 = di("sel10", [10, 10 * 128])
    selh = di("selh", [2, 256], BF16)

    wq0 = di("wq0", [D, D])
    wk0 = di("wk0", [D, D])
    wv0 = di("wv0", [D, D])
    bq0 = di("bq0", [128, DT])
    bk0 = di("bk0", [128, DT])
    bv0 = di("bv0", [1, D])
    gw0 = di("gw0", [D, 8])
    ew1 = di("ew1", [8, D, HMOE])
    ew2 = di("ew2", [8, D, HMOE])
    ew3 = di("ew3", [8, HMOE, D])
    sw1 = di("sw1", [D, HMOE])
    sw2 = di("sw2", [D, HMOE])
    sw3 = di("sw3", [HMOE, D])

    wq1 = di("wq1", [D, 512])
    wk1 = di("wk1", [D, 512])
    wv1 = di("wv1", [D, 512])
    wb1 = di("wb1", [D, 2])
    cvq = di("cvq", [128, 16])
    cvk = di("cvk", [128, 16])
    cvv = di("cvv", [128, 16])
    firs = di("firs", [128, 4 * 3])
    firm = di("firm", [128, 4 * 15])
    firl = di("firl", [128, 4 * 63])
    gw1 = di("gw1", [D, 10])
    gb1 = di("gb1", [1, 10])
    ltmp = di("ltmp", [1, 2])
    epsp = di("epsp", [1, 2])
    gamm = di("gamm", [128, 2])
    wo1 = di("wo1", [D, D])
    w1t = di("w1t", [D, 4096])
    w2t = di("w2t", [4096, D])
    b1f = di("b1f", [128, 32])
    b2f = di("b2f", [128, 8])

    y = nc.dram_tensor("y", [D, TO], F32, kind="ExternalOutput").ap()

    s_xn0o = nc.dram_tensor("s_xn0o", [D, TO], BF16).ap()
    s_xn0f = nc.dram_tensor("s_xn0f", [D, T], BF16).ap()
    s_oun = nc.dram_tensor("s_oun", [D, TO], BF16).ap()
    s_x1 = nc.dram_tensor("s_x1", [D, TO], F32).ap()
    s_xq = nc.dram_tensor("s_xq", [D, TO], BF16).ap()
    s_ffn1 = nc.dram_tensor("s_ffn1", [D, TO], F32).ap()
    s_q = nc.dram_tensor("s_q", [D, TO], BF16).ap()
    s_qc = nc.dram_tensor("s_qc", [512, T], BF16).ap()
    s_kc = nc.dram_tensor("s_kc", [512, T], BF16).ap()

    cc_z_i = nc.dram_tensor("cc_z_i", [1, 128], F32).ap()
    cc_z_o = nc.dram_tensor("cc_z_o", [2, 128], F32).ap()
    cc_x_i = nc.dram_tensor("cc_x_i", [D, TO], BF16).ap()
    cc_x_o = nc.dram_tensor("cc_x_o", [2 * D, TO], BF16).ap()
    cc_o_i = nc.dram_tensor("cc_o_i", [512, T], BF16).ap()
    cc_o_o = nc.dram_tensor("cc_o_o", [1024, T], BF16).ap()

    taps = {}
    for tname in debug_taps:
        shp = {"xn0o": [D, TO], "q0": [D, TO], "attn0": [D, TO], "zz": [1, 260],
               "wdump": [128, 4096],
               "ffn0": [D, TO], "x1": [D, TO], "xn1f": [D, T], "ffn1": [D, TO],
               "qc1": [512, T], "kc1": [512, T], "vc1": [512, T], "beta": [2, T],
               "onrm": [512, T], "probs": [10, T], "dlt": [512, T],
               "dbg0": [128, 4096], "wef": [8, TO],
               "gdbg": [128, 128]}[tname]
        taps[tname] = nc.dram_tensor("tap_" + tname, shp, F32,
                                     kind="ExternalOutput").ap()

    em = EM(nc)
    fm = lambda ap, n=DT: ap.rearrange("(dt p) t -> p dt t", dt=n)

    alloc_stack = {"left": [], "right": []}

    def sb(name, shape, dtype, side="left"):
        g = nc.sbuf_tensor(name, shape, dtype, side=side)
        h = g.__enter__()
        alloc_stack[side].append((name, g))
        return h

    def sb_free(upto_name, side="left"):
        stk = alloc_stack[side]
        while stk:
            name, g = stk.pop()
            g.__exit__(None, None, None)
            if name == upto_name:
                return
        raise KeyError(upto_name)

    c_ones_b = sb("c_ones_b", [128, 1], BF16)
    c_ones_f = sb("c_ones_f", [128, 1], F32)
    c_onesr_f = sb("c_onesr_f", [1, 128], F32)
    c_bq = sb("c_bq", [128, DT], F32)
    c_bk = sb("c_bk", [128, DT], F32)
    c_sel = sb("c_sel", [128, 2], F32)
    c_small = sb("c_small", [128, 16], F32)
    c_zpad = sb("c_zpad", [1, 128], F32)
    STAT = sb("STAT", [2, 1024], F32)
    ST = sb("ST", [128, 1024], F32)
    c_ident_f = sb("c_ident_f", [128, 128], F32)
    c_rowbc = sb("c_rowbc", [128, TO], F32, side="right")
    c_colidx = sb("c_colidx", [128, 32], F32, side="right")

    ps1 = nc.psum_tensor("ps1", [128, 512], F32).__enter__()
    ps2 = nc.psum_tensor("ps2", [128, 512], F32).__enter__()
    ps3 = nc.psum_tensor("ps3", [128, 512], F32).__enter__()
    ps4 = nc.psum_tensor("ps4", [128, 512], F32).__enter__()
    ps5 = nc.psum_tensor("ps5", [128, 512], F32).__enter__()
    ps6 = nc.psum_tensor("ps6", [128, 512], F32).__enter__()
    ps7 = nc.psum_tensor("ps7", [128, 512], F32).__enter__()
    OB = [ps1, ps2, ps3, ps4]

    def p0(em):
        em.dma("sp", c_ident_f[:], ident_f)
        em.dma("sp", c_colidx[:], col_idx)
        em.dma("sp", c_bq[:], bq0)
        em.dma("sp", c_bk[:], bk0)
        em.dma("sp", c_sel[:], sel01)
        em.wait_dma("vector", "sp")
        em.dve(lambda h: h.memset(c_ones_b[:], 1.0), inc=True)
        em.dve(lambda h: h.memset(c_ones_f[:], 1.0), inc=True)
        em.dve(lambda h: h.memset(c_onesr_f[:], 1.0), inc=True)
        em.dve(lambda h: h.memset(c_zpad[:], 0.0), inc=True)
        em.dve(lambda h: h.memset(c_small[:, 0:4], 0.0), inc=True)
        em.dve(lambda h: h.memset(c_small[:, 4:5], NEGREF), inc=True)
        em.dve(lambda h: h.memset(c_small[:, 5:6], MAGIC), inc=True)
        em.dve(lambda h: h.memset(c_small[:, 6:16], 0.0), inc=True)
        em.dve(lambda h: h.tensor_single_scalar(out=c_bq[:], in_=c_bq[:],
                                                scalar=1.0 / 32.0, op=ALU.mult),
               inc=True)
        em.wait("tensor", "dve")
        for i in range(TO // 512):
            em.wait("sp", "pe")
            em.dma("sp", ST[0:1, 0:512], row_idx[0:1, i * 512:(i + 1) * 512])
            em.wait_dma("tensor", "sp")
            em.pe(lambda h, i=i: h.matmul(ps1[:, 0:512], c_onesr_f[:],
                                          ST[0:1, 0:512],
                                          start=True, stop=True), inc=True)
            em.wait("vector", "pe")
            em.dve(lambda h, i=i: h.tensor_copy(out=c_rowbc[:, i * 512:(i + 1) * 512],
                                                in_=ps1[:, 0:512]), inc=True)
            em.wait("tensor", "dve")

    def ln_chunk(em, SC, SD, cw, dst_fn):
        for dt in range(DT):
            em.pe(lambda h, dt=dt: h.matmul(
                ps5[0:1, 0:cw], c_ones_f[:], SC[:, dt * cw:(dt + 1) * cw],
                start=(dt == 0), stop=(dt == DT - 1)), inc=(dt == DT - 1))
        em.wait("scalar", "pe", lag=1)
        for dt in range(DT):
            em.act(lambda h, dt=dt: h.activation(
                SD[:, dt * cw:(dt + 1) * cw], SC[:, dt * cw:(dt + 1) * cw],
                AF.Square), inc=(dt == DT - 1))
        em.wait("tensor", "act")
        for dt in range(DT):
            em.pe(lambda h, dt=dt: h.matmul(
                ps6[0:1, 0:cw], c_ones_f[:], SD[:, dt * cw:(dt + 1) * cw],
                start=(dt == 0), stop=(dt == DT - 1)), inc=(dt == DT - 1))
        em.wait("vector", "pe")
        em.dve(lambda h: h.tensor_single_scalar(out=STAT[0:1, 0:cw], in_=ps5[0:1, 0:cw],
                                                scalar=1.0 / D, op=ALU.mult), inc=True)
        em.dve(lambda h: h.tensor_single_scalar(out=STAT[0:1, cw:2 * cw],
                                                in_=ps6[0:1, 0:cw],
                                                scalar=1.0 / D, op=ALU.mult), inc=True)
        em.dve(lambda h: h.tensor_tensor(out=ST[0:1, 0:cw], in0=STAT[0:1, 0:cw],
                                         in1=STAT[0:1, 0:cw], op=ALU.mult), inc=True)
        em.dve(lambda h: h.tensor_tensor(out=ST[0:1, 0:cw], in0=STAT[0:1, cw:2 * cw],
                                         in1=ST[0:1, 0:cw], op=ALU.subtract), inc=True)
        em.dve(lambda h: h.tensor_single_scalar(out=ST[0:1, 0:cw], in_=ST[0:1, 0:cw],
                                                scalar=1e-5, op=ALU.add), inc=True)
        em.wait("scalar", "dve")
        em.act(lambda h: h.activation(ST[0:1, cw:2 * cw], ST[0:1, 0:cw], AF.Sqrt),
               inc=True)
        em.wait("vector", "act")
        em.dve(lambda h: h.reciprocal(out=ST[0:1, 0:cw], in_=ST[0:1, cw:2 * cw]),
               inc=True)
        em.wait("tensor", "dve")
        nb = max(cw // 512, 1)
        bw = min(cw, 512)
        for i in range(nb):
            em.pe(lambda h, i=i: h.matmul(ps5[:, 0:bw], c_onesr_f[:],
                                          STAT[0:1, i * bw:(i + 1) * bw],
                                          start=True, stop=True), inc=True)
            em.pe(lambda h, i=i: h.matmul(ps6[:, 0:bw], c_onesr_f[:],
                                          ST[0:1, i * bw:(i + 1) * bw],
                                          start=True, stop=True), inc=True)
            em.wait("vector", "pe")
            for dt in range(DT):
                o0 = dt * cw + i * bw
                em.dve(lambda h, o0=o0, bw=bw: h.tensor_tensor(
                    out=SD[:, o0:o0 + bw], in0=SC[:, o0:o0 + bw],
                    in1=ps5[:, 0:bw], op=ALU.subtract), inc=True)
                em.dve(lambda h, o0=o0, bw=bw: h.tensor_tensor(
                    out=SD[:, o0:o0 + bw], in0=SD[:, o0:o0 + bw],
                    in1=ps6[:, 0:bw], op=ALU.mult), inc=True)
                dst_fn(em, dt, i * bw, bw)
            em.wait("tensor", "dve")

    CW = 512

    def mk_p1():
        Aq = sb("Aq", [128, DT * TO], BF16)
        Ax = sb("Ax", [128, DT * TO], BF16)
        Wst = sb("Wst_p1", [128, DT * 1024], BF16)
        SC = sb("SC_p1", [128, DT * CW], F32)
        SD = sb("SD_p1", [128, DT * CW], F32)

        def p1(em):
            em.dma("gp", Wst[:].rearrange("p (kt n) -> p kt n", kt=DT), fm(wq0))
            for tch in range(TO // CW):
                em.wait("sp", "pe")
                em.wait("sp", "dve")
                em.dma("sp", SC[:].rearrange("p (dt t) -> p dt t", dt=DT),
                       fm(x_own)[:, :, tch * CW:(tch + 1) * CW])
                em.wait_dma("tensor", "sp")
                em.wait_dma("scalar", "sp")
                em.wait_dma("vector", "sp")

                def dst(em, dt, off, bw, tch=tch):
                    em.dve(lambda h, dt=dt, off=off, bw=bw, tch=tch: h.tensor_copy(
                        out=Aq[:, 0:1][:, 0:0] if False else
                        Ax[:, dt * TO + tch * CW + off:dt * TO + tch * CW + off + bw],
                        in_=SD[:, dt * CW + off:dt * CW + off + bw]), inc=True)

                ln_chunk(em, SC, SD, CW, dst)
            em.wait("sync", "dve")
            em.dma("sp", fm(s_xn0o), Ax[:].rearrange("p (dt t) -> p dt t", dt=DT))
            em.wait_dma("tensor", "gp")
            for dqt in range(DT):
                for tch in range(TO // 512):
                    bank = ps1 if (dqt * 4 + tch) % 2 == 0 else ps2
                    em.wait("tensor", "act", lag=1)
                    for kt in range(DT):
                        em.pe(lambda h, dqt=dqt, tch=tch, kt=kt, bank=bank: h.matmul(
                            bank[:, 0:512],
                            Wst[:, kt * 1024 + dqt * 128:kt * 1024 + dqt * 128 + 128],
                            Ax[:, kt * TO + tch * 512:kt * TO + tch * 512 + 512],
                            start=(kt == 0), stop=(kt == DT - 1)), inc=(kt == DT - 1))
                    em.wait("scalar", "pe")
                    em.act(lambda h, dqt=dqt, tch=tch, bank=bank: h.activation(
                        Aq[:, dqt * TO + tch * 512:dqt * TO + tch * 512 + 512],
                        bank[:, 0:512], AF.Identity, scale=1.0 / 32.0,
                        bias=c_bq[:, dqt:dqt + 1]), inc=True)
            em.wait("sync", "act")
            em.dma("sp", fm(s_q), Aq[:].rearrange("p (dt t) -> p dt t", dt=DT))
            if "xn0o" in taps:
                em.wait("gpsimd", "act")
                em.dma("gp", fm(taps["xn0o"]),
                       Ax[:].rearrange("p (dt t) -> p dt t", dt=DT))
            if "q0" in taps:
                em.wait("gpsimd", "act")
                em.dma("gp", fm(taps["q0"]),
                       Aq[:].rearrange("p (dt t) -> p dt t", dt=DT))
            em.wait_dma("sync", "sp")
            em.wait_dma("gpsimd", "gp")

        em.phase(p1)
        sb_free("Aq")
        return None

    def mk_p2():
        Ak = sb("Ak", [128, DT * T], BF16)
        Wst = sb("Wst_p2", [128, DT * 1024], BF16)
        SC = sb("SC_p2", [128, DT * CW], F32)
        SD = sb("SD_p2", [128, DT * CW], F32)

        def p2(em):
            em.dma("gp", Wst[:].rearrange("p (kt n) -> p kt n", kt=DT), fm(wk0))
            em.wait_dma("tensor", "gp")
            for tch in range(T // CW):
                em.wait("sp", "pe")
                em.wait("sp", "dve")
                em.dma("sp", SC[:].rearrange("p (dt t) -> p dt t", dt=DT),
                       fm(x_full)[:, :, tch * CW:(tch + 1) * CW])
                em.wait_dma("tensor", "sp")
                em.wait_dma("scalar", "sp")
                em.wait_dma("vector", "sp")

                def dst(em, dt, off, bw, tch=tch):
                    pass

                ln_chunk(em, SC, SD, CW, dst)
                SCb = SC.bitcast(BF16)
                em.wait("vector", "pe")
                em.wait_dma("vector", "sp")
                em.dve(lambda h: h.tensor_copy(out=SCb[:, 0:DT * CW],
                                               in_=SD[:, 0:DT * CW]), inc=True)
                em.wait("sync", "dve")
                em.dma("sp", fm(s_xn0f)[:, :, tch * CW:(tch + 1) * CW],
                       SCb[:, 0:DT * CW].rearrange("p (dt t) -> p dt t", dt=DT))
                for dqt in range(DT):
                    bank = ps1 if dqt % 2 == 0 else ps2
                    em.wait("tensor", "act", lag=1)
                    em.wait("tensor", "dve")
                    for kt in range(DT):
                        em.pe(lambda h, dqt=dqt, kt=kt, bank=bank, tch=tch: h.matmul(
                            bank[:, 0:CW],
                            Wst[:, kt * 1024 + dqt * 128:kt * 1024 + dqt * 128 + 128],
                            SCb[:, kt * CW:(kt + 1) * CW],
                            start=(kt == 0), stop=(kt == DT - 1)), inc=(kt == DT - 1))
                    em.wait("scalar", "pe")
                    em.act(lambda h, dqt=dqt, tch=tch, bank=bank: h.activation(
                        Ak[:, dqt * T + tch * CW:dqt * T + tch * CW + CW],
                        bank[:, 0:CW], AF.Identity, bias=c_bk[:, dqt:dqt + 1]),
                        inc=True)
                em.wait("sp", "act")
            em.wait_dma("gpsimd", "gp")
            em.wait_dma("sync", "sp")

        em.phase(p2)
        sb_free("Wst_p2")
        return Ak

    def mk_p3():
        Av = sb("Av", [128, 32 * 1024], BF16)
        Wst = sb("Wst_p3", [128, DT * 1024], BF16)
        Xc = sb("Xc_p3", [128, DT * CW], BF16)
        bvb = ST

        def p3(em):
            em.dma("gp", Wst[:].rearrange("p (kt n) -> p kt n", kt=DT), fm(wv0))
            em.dma("sp", STAT[0:1, 0:D], bv0)
            em.wait_dma("tensor", "sp")
            for i in range(D // 512):
                em.pe(lambda h, i=i: h.matmul(ps5[:, 0:512], c_onesr_f[:],
                                              STAT[0:1, i * 512:(i + 1) * 512],
                                              start=True, stop=True), inc=True)
                em.wait("vector", "pe")
                em.dve(lambda h, i=i: h.tensor_copy(out=bvb[:, i * 512:(i + 1) * 512],
                                                    in_=ps5[:, 0:512]), inc=True)
                em.wait("tensor", "dve")
            em.wait_dma("tensor", "gp")
            for g in range(T // CW):
                em.wait("sp", "pe")
                em.dma("sp", Xc[:].rearrange("p (dt t) -> p dt t", dt=DT),
                       fm(s_xn0f)[:, :, g * CW:(g + 1) * CW])
                em.wait_dma("tensor", "sp")
                for tt in range(4):
                    for n in range(2):
                        bank = ps1 if (tt * 2 + n) % 2 == 0 else ps2
                        em.wait("tensor", "dve", lag=1)
                        for kt in range(DT):
                            em.pe(lambda h, tt=tt, n=n, kt=kt, bank=bank, g=g:
                                  h.matmul(
                                      bank[:, 0:512],
                                      Xc[:, kt * CW + tt * 128:
                                         kt * CW + tt * 128 + 128],
                                      Wst[:, kt * 1024 + n * 512:
                                          kt * 1024 + n * 512 + 512],
                                      start=(kt == 0), stop=(kt == DT - 1)),
                                  inc=(kt == DT - 1))
                        em.wait("vector", "pe")
                        gt = g * 4 + tt
                        em.dve(lambda h, gt=gt, n=n, bank=bank:
                               h.scalar_tensor_tensor(
                                   out=Av[:, gt * 1024 + n * 512:
                                          gt * 1024 + n * 512 + 512],
                                   in0=bank[:, 0:512], scalar=1.0,
                                   in1=bvb[:, n * 512:(n + 1) * 512],
                                   op0=ALU.mult, op1=ALU.add), inc=True)
            em.wait("sync", "dve")
            em.wait_dma("gpsimd", "gp")

        em.phase(p3)
        sb_free("Wst_p3")
        return Av

    def mk_p4(Ak, Av):
        Eall = sb("Eall_p4", [128, 32 * 256], BF16)
        Qrb = sb("Qrb_p4", [128, DT * 256], BF16)
        Wsc = sb("Wsc_p4", [128, 1536], BF16)

        def p4(em):
            W32 = Wsc.bitcast(F32)
            em.dve(lambda h: h.memset(c_small[:, 0:4], 0.0), inc=True)
            qld = [0]
            for rb in range(8):
                ncols = 2048 if rb < 4 else T
                ncts = ncols // 128
                em.wait("sp", "pe")
                em.dma("sp", Qrb[:].rearrange("p (dt t) -> p dt t", dt=DT),
                       fm(s_q)[:, :, rb * 256:(rb + 1) * 256])
                qld[0] = em.mark_dma("sp")
                s_done = [0] * ncts
                exp_done = [0] * ncts
                stt_done = [0] * ncts

                def emit_sgroup(ct, rb=rb, s_done=s_done, exp_done=exp_done):
                    sbank = ps5 if ct % 2 == 0 else ps6
                    if ct >= 2:
                        em.wait_abs("tensor", "act", exp_done[ct - 2])
                    if ct == 0:
                        em.wait_dma("tensor", "sp", value=qld[0])
                    for kt in range(DT):
                        em.pe(lambda h, ct=ct, kt=kt, sbank=sbank, rb=rb: h.matmul(
                            sbank[:, 0:256],
                            Ak[:, kt * T + ct * 128:kt * T + ct * 128 + 128],
                            Qrb[:, kt * 256:kt * 256 + 256],
                            start=(kt == 0), stop=(kt == DT - 1)),
                            inc=(kt == DT - 1))
                    s_done[ct] = em.mark("pe")

                emit_sgroup(0)
                for ct in range(ncts):
                    if ct + 1 < ncts:
                        emit_sgroup(ct + 1)
                    sbank = ps5 if ct % 2 == 0 else ps6
                    em.dve(lambda h, rb=rb, ct=ct: h.tensor_single_scalar(
                        out=W32[:, 0:256],
                        in_=c_rowbc[:, rb * 256:rb * 256 + 256],
                        scalar=c_colidx[:, ct:ct + 1], op=ALU.is_ge), inc=True)
                    em.wait_abs("scalar", "pe", s_done[ct])
                    if ct >= 1:
                        em.wait_abs("scalar", "dve", stt_done[ct - 1])
                    em.act(lambda h, sbank=sbank: h.activation(
                        Wsc[:, 1024:1280], sbank[:, 0:256], AF.Exp,
                        bias=c_small[:, 4:5]), inc=True)
                    exp_done[ct] = em.mark("act")
                    em.wait("vector", "act")
                    em.dve(lambda h, ct=ct: h.scalar_tensor_tensor(
                        out=Eall[:, ct * 256:ct * 256 + 256], in0=Wsc[:, 1024:1280],
                        scalar=1.0, in1=W32[:, 0:256], op0=ALU.mult, op1=ALU.mult,
                        accum_out=c_small[:, 2:3]), inc=True)
                    em.dve(lambda h: h.tensor_tensor(
                        out=c_small[:, 0:1], in0=c_small[:, 0:1],
                        in1=c_small[:, 2:3], op=ALU.add), inc=True)
                    stt_done[ct] = em.mark("dve")
                em.wait_abs("tensor", "dve", stt_done[ncts - 1])
                em.wait("tensor", "dve")
                for dvh in range(2):
                    if dvh == 1:
                        em.wait("tensor", "dve")
                    for ct in range(ncts):
                        for i in range(4):
                            dvt = dvh * 4 + i
                            em.pe(lambda h, ct=ct, dvt=dvt, i=i: h.matmul(
                                OB[i][:, 0:256],
                                Av[:, ct * 1024 + dvt * 128:
                                   ct * 1024 + dvt * 128 + 128],
                                Eall[:, ct * 256:ct * 256 + 256],
                                start=(ct == 0), stop=(ct == ncts - 1)),
                                inc=(ct == ncts - 1 and i == 3))
                    em.wait("vector", "pe")
                    em.wait_dma("vector", "sp")
                    for i in range(4):
                        em.dve(lambda h, i=i: h.tensor_copy(
                            out=ST[:, i * 256:(i + 1) * 256],
                            in_=OB[i][:, 0:256]), inc=(i == 3))
                    em.wait("gpsimd", "dve")
                    em.dma("gp", fm(s_oun)[:, dvh * 4:dvh * 4 + 4,
                                           rb * 256:(rb + 1) * 256],
                           ST[:, 0:1024].rearrange("p (dt t) -> p dt t", dt=4))
                    em.wait_dma("vector", "gp")
            em.wait("tensor", "dve")
            em.pe(lambda h: h.matmul(ps5[0:1, 0:1], c_small[:, 0:1],
                                     c_ones_f[:], start=True, stop=True), inc=True)
            em.wait("vector", "pe")
            em.dve(lambda h: h.tensor_copy(out=c_zpad[0:1, 0:1], in_=ps5[0:1, 0:1]),
                   inc=True)
            em.wait("sync", "dve")
            em.dma("sp", cc_z_i, c_zpad[:])
            em.wait_dma("gpsimd", "sp")
            em.collective("AllGather", ALU.bypass, [cc_z_i.opt()], [cc_z_o.opt()])

        em.phase(p4)
        sb_free("Eall_p4")

    def mk_p5():
        ACC = sb("ACC_p5", [128, DT * TO], F32)
        WEF = sb("WEF_p5", [8, TO], BF16)
        S8B = ST.bitcast(BF16)
        W1s = sb("W1s_p5", [128, DT * HMOE], BF16)
        W2s = sb("W2s_p5", [128, DT * HMOE], BF16)
        W3s = sb("W3s_p5", [128, 16 * 1024], BF16)
        Xt2 = sb("Xt_p5", [128, 2 * DT * 512], BF16)
        Gt = sb("Gt_p5", [128, 16 * 512], BF16)
        WB = sb("WB_p5", [128, 512], BF16)
        S8 = ST.ap()[0:8, 0:1024] if False else ST

        def p5(em):
            Tm = Xt2.bitcast(F32)[:, 2048:2560]
            GWB = ST.bitcast(BF16)
            Xt = Xt2[:, 0:DT * 512]
            for tch in range(4):
                em.wait("sp", "pe")
                em.wait("sp", "dve")
                em.dma("sp", Xt[:].rearrange("p (dt t) -> p dt t", dt=DT),
                       fm(s_xn0o)[:, :, tch * 512:(tch + 1) * 512])
                if tch == 0:
                    em.dma("sp", S8B[0:8, 0:1024], sel8)
                    em.dma("gp",
                           GWB[:, 1024:1088].rearrange("p (kt e) -> p kt e", kt=DT),
                           gw0.rearrange("(kt p) e -> p kt e", kt=DT))
                em.wait_dma("tensor", "sp")
                em.wait_dma("tensor", "gp")
                sb_ = lambda st: st * 128
                g_ = lambda st: Tm[:, sb_(st) + 0:sb_(st) + 8]
                m1_ = lambda st: Tm[:, sb_(st) + 8:sb_(st) + 9]
                m2_ = lambda st: Tm[:, sb_(st) + 9:sb_(st) + 10]
                den_ = lambda st: Tm[:, sb_(st) + 10:sb_(st) + 11]
                rde_ = lambda st: Tm[:, sb_(st) + 11:sb_(st) + 12]
                eq1_ = lambda st: Tm[:, sb_(st) + 16:sb_(st) + 24]
                eq2_ = lambda st: Tm[:, sb_(st) + 24:sb_(st) + 32]
                w_ = lambda st: Tm[:, sb_(st) + 32:sb_(st) + 40]
                gp2_ = lambda st: Tm[:, sb_(st) + 40:sb_(st) + 48]
                nm1_ = lambda st: Tm[:, sb_(st) + 48:sb_(st) + 49]
                ex_ = lambda st: Tm[:, sb_(st) + 56:sb_(st) + 64]
                for tt in range(4):
                    em.wait("tensor", "dve")
                    for kt in range(DT):
                        em.pe(lambda h, tch=tch, tt=tt, kt=kt: h.matmul(
                            ps7[:, 0:8],
                            Xt[:, kt * 512 + tt * 128:kt * 512 + tt * 128 + 128],
                            GWB[:, 1024 + kt * 8:1024 + kt * 8 + 8],
                            start=(kt == 0), stop=(kt == DT - 1)),
                            inc=(kt == DT - 1))
                    em.wait("vector", "pe")
                    em.dve(lambda h, tt=tt: h.tensor_copy(out=g_(tt),
                                                          in_=ps7[:, 0:8]),
                           inc=True)
                for st in range(4):
                    em.dve(lambda h, st=st: h.tensor_reduce(
                        out=m1_(st), in_=g_(st), axis=mybir.AxisListType.X,
                        op=ALU.max), inc=(st == 3))
                for st in range(4):
                    em.dve(lambda h, st=st: h.tensor_single_scalar(
                        out=eq1_(st), in_=g_(st), scalar=m1_(st),
                        op=ALU.is_ge), inc=(st == 3))
                for st in range(4):
                    em.dve(lambda h, st=st: h.scalar_tensor_tensor(
                        out=gp2_(st), in0=eq1_(st), scalar=-1e30, in1=g_(st),
                        op0=ALU.mult, op1=ALU.add), inc=(st == 3))
                for st in range(4):
                    em.dve(lambda h, st=st: h.tensor_reduce(
                        out=m2_(st), in_=gp2_(st), axis=mybir.AxisListType.X,
                        op=ALU.max), inc=(st == 3))
                for st in range(4):
                    em.dve(lambda h, st=st: h.tensor_single_scalar(
                        out=eq2_(st), in_=gp2_(st), scalar=m2_(st),
                        op=ALU.is_ge), inc=(st == 3))
                for st in range(4):
                    em.dve(lambda h, st=st: h.tensor_single_scalar(
                        out=nm1_(st), in_=m1_(st), scalar=-1.0,
                        op=ALU.mult), inc=(st == 3))
                em.wait("scalar", "dve")
                for st in range(4):
                    em.act(lambda h, st=st: h.activation(
                        ex_(st), g_(st), AF.Exp, bias=nm1_(st)),
                        inc=(st == 3))
                em.wait("vector", "act")
                for st in range(4):
                    em.dve(lambda h, st=st: h.tensor_tensor(
                        out=w_(st), in0=eq1_(st), in1=eq2_(st),
                        op=ALU.add), inc=(st == 3))
                for st in range(4):
                    em.dve(lambda h, st=st: h.tensor_tensor(
                        out=w_(st), in0=w_(st), in1=ex_(st),
                        op=ALU.mult), inc=(st == 3))
                for st in range(4):
                    em.dve(lambda h, st=st: h.tensor_reduce(
                        out=den_(st), in_=w_(st), axis=mybir.AxisListType.X,
                        op=ALU.add), inc=(st == 3))
                for st in range(4):
                    em.dve(lambda h, st=st: h.reciprocal(
                        out=rde_(st), in_=den_(st)), inc=(st == 3))
                for st in range(4):
                    em.dve(lambda h, st=st: h.tensor_single_scalar(
                        out=w_(st), in_=w_(st), scalar=rde_(st),
                        op=ALU.mult), inc=(st == 3))
                for st in range(4):
                    em.wait("tensor", "dve")
                    em.pe(lambda h, st=st: h.transpose(
                        ps6[0:8, 0:128], w_(st), c_ident_f[:]), inc=True)
                    em.wait("vector", "pe")
                    em.dve(lambda h, tch=tch, st=st: h.tensor_copy(
                        out=WEF[0:8, tch * 512 + st * 128:
                                tch * 512 + st * 128 + 128],
                        in_=ps6[0:8, 0:128]), inc=True)
            if "wef" in taps:
                em.wait("gp", "dve")
                em.dma("gp", taps["wef"], WEF[0:8, :])
                em.wait_dma("tensor", "gp")
            wdma12 = [0]
            wdma3 = [0]

            def load_w12(e):
                s1, s2 = (ew1[e], ew2[e]) if e < 8 else (sw1, sw2)
                em.dma("gp", W1s[:].rearrange("p (kt n) -> p kt n", kt=DT),
                       s1.rearrange("(kt p) n -> p kt n", kt=DT))
                em.dma("gp", W2s[:].rearrange("p (kt n) -> p kt n", kt=DT),
                       s2.rearrange("(kt p) n -> p kt n", kt=DT))
                wdma12[0] = em.mark_dma("gp")

            def load_w3(e):
                s3 = ew3[e] if e < 8 else sw3
                em.dma("gp", W3s[:].rearrange("p (kt n) -> p kt n", kt=16),
                       s3.rearrange("(kt p) n -> p kt n", kt=16))
                wdma3[0] = em.mark_dma("gp")

            load_w12(0)
            load_w3(0)
            xmark = [0, 0]
            em.wait("sp", "pe")
            em.dma("sp", Xt2[:, 0:DT * 512].rearrange("p (dt t) -> p dt t", dt=DT),
                   fm(s_xn0o)[:, :, 0:512])
            xmark[0] = em.mark_dma("sp")
            for e in range(9):
                em.wait_dma("tensor", "gp", value=wdma12[0])
                for tch in range(4):
                    g36 = e * 4 + tch
                    Xt = Xt2[:, (g36 % 2) * DT * 512:(g36 % 2) * DT * 512
                             + DT * 512]
                    if g36 + 1 < 36:
                        ntch = (g36 + 1) % 4
                        nb = (g36 + 1) % 2
                        em.wait("sp", "pe")
                        em.dma("sp", Xt2[:, nb * DT * 512:(nb + 1) * DT * 512]
                               .rearrange("p (dt t) -> p dt t", dt=DT),
                               fm(s_xn0o)[:, :, ntch * 512:(ntch + 1) * 512])
                        xmark[nb] = em.mark_dma("sp")
                    em.wait_dma("tensor", "sp", value=xmark[g36 % 2])
                    hdone = [0] * 16
                    for ht in range(16):
                        hb = ps5 if ht % 2 == 0 else ps6
                        h2b = OB[ht % 2]
                        em.wait("tensor", "dve", lag=1)
                        for kt in range(DT):
                            em.pe(lambda h, ht=ht, kt=kt, hb=hb: h.matmul(
                                hb[:, 0:512],
                                W1s[:, kt * HMOE + ht * 128:kt * HMOE + ht * 128 + 128],
                                Xt[:, kt * 512:kt * 512 + 512],
                                start=(kt == 0), stop=(kt == DT - 1)),
                                inc=(kt == DT - 1))
                        for kt in range(DT):
                            em.pe(lambda h, ht=ht, kt=kt, h2b=h2b: h.matmul(
                                h2b[:, 0:512],
                                W2s[:, kt * HMOE + ht * 128:kt * HMOE + ht * 128 + 128],
                                Xt[:, kt * 512:kt * 512 + 512],
                                start=(kt == 0), stop=(kt == DT - 1)),
                                inc=(kt == DT - 1))
                        em.wait("scalar", "pe", lag=1)
                        em.act(lambda h, hb=hb, ht=ht: h.activation(
                            Gt[:, ht * 512:(ht + 1) * 512],
                            hb[:, 0:512], AF.Silu), inc=True)
                        em.wait("vector", "act")
                        em.wait("vector", "pe")
                        em.dve(lambda h, ht=ht, h2b=h2b: h.tensor_tensor(
                            out=Gt[:, ht * 512:(ht + 1) * 512],
                            in0=Gt[:, ht * 512:(ht + 1) * 512],
                            in1=h2b[:, 0:512],
                            op=ALU.mult), inc=True)
                        hdone[ht] = em.mark("dve")
                    if tch == 3 and e < 8:
                        em.wait("gp", "pe")
                        load_w12(e + 1)
                    if tch == 0:
                        em.wait_dma("tensor", "gp", value=wdma3[0])
                    em.wait("tensor", "dve")
                    if e < 8:
                        em.pe(lambda h, e=e, tch=tch: h.matmul(
                            ps7[:, 0:512], S8B[0:8, e * 128:(e + 1) * 128],
                            WEF[0:8, tch * 512:(tch + 1) * 512],
                            start=True, stop=True), inc=True)
                        em.wait("vector", "pe")
                        em.dve(lambda h: h.tensor_copy(out=WB[:, 0:512],
                                                       in_=ps7[:, 0:512]), inc=True)
                        for ht in range(16):
                            em.dve(lambda h, ht=ht: h.tensor_tensor(
                                out=Gt[:, ht * 512:(ht + 1) * 512],
                                in0=Gt[:, ht * 512:(ht + 1) * 512],
                                in1=WB[:, 0:512], op=ALU.mult),
                                inc=(ht == 15))
                        em.wait("tensor", "dve")
                    for dth in range(2):
                        if dth == 1:
                            em.wait("tensor", "dve")
                        for ht in range(16):
                            for i in range(4):
                                dt = dth * 4 + i
                                em.pe(lambda h, ht=ht, i=i, dt=dt: h.matmul(
                                    OB[i][:, 0:512],
                                    W3s[:, ht * 1024 + dt * 128:
                                        ht * 1024 + dt * 128 + 128],
                                    Gt[:, ht * 512:(ht + 1) * 512],
                                    start=(ht == 0), stop=(ht == 15)),
                                    inc=(ht == 15 and i == 3))
                        em.wait("vector", "pe")
                        for i in range(4):
                            dt = dth * 4 + i
                            dst = ACC[:, dt * TO + tch * 512:dt * TO + tch * 512 + 512]
                            if e == 0:
                                em.dve(lambda h, i=i, dst=dst:
                                       h.tensor_copy(out=dst, in_=OB[i][:, 0:512]),
                                       inc=(i == 3))
                            else:
                                em.dve(lambda h, i=i, dst=dst: h.tensor_tensor(
                                    out=dst, in0=dst, in1=OB[i][:, 0:512],
                                    op=ALU.add), inc=(i == 3))
                    if tch == 3 and e < 8:
                        em.wait("gp", "pe")
                        load_w3(e + 1)
            if "ffn0" in taps:
                em.wait("gpsimd", "dve")
                em.dma("gp", fm(taps["ffn0"]),
                       ACC[:].rearrange("p (dt t) -> p dt t", dt=DT))
                em.wait_dma("gpsimd", "gp")

        em.phase(p5)
        sb_free("WEF_p5")
        return ACC

    def mk_p6(ACC):
        SC = sb("SC_p6", [128, DT * 512], F32)
        SD = sb("SD_p6", [128, DT * 512], F32)
        OC = sb("OC_p6", [128, DT * 512], BF16)

        def p6(em):
            em.wait_cc("gpsimd")
            em.dma("gp", ST[0:1, 0:256].rearrange("o (a n) -> o a n", a=2),
                   cc_z_o.rearrange("(a o) n -> o a n", a=2))
            em.wait_dma("vector", "gp")
            em.dve(lambda h: h.tensor_tensor(out=ST[0:1, 512:640],
                                             in0=ST[0:1, 0:128],
                                             in1=ST[0:1, 128:256], op=ALU.add),
                   inc=True)
            em.dve(lambda h: h.tensor_single_scalar(out=ST[0:1, 512:640],
                                                    in_=ST[0:1, 512:640],
                                                    scalar=1e-20, op=ALU.max),
                   inc=True)
            em.dve(lambda h: h.reciprocal(out=ST[0:1, 640:768],
                                          in_=ST[0:1, 512:640]), inc=True)
            em.wait("tensor", "dve")
            em.pe(lambda h: h.matmul(ps5[:, 0:128], c_onesr_f[:], ST[0:1, 640:768],
                                     start=True, stop=True), inc=True)
            em.wait("vector", "pe")
            em.dve(lambda h: h.tensor_copy(out=c_small[:, 3:4], in_=ps5[:, 0:1]),
                   inc=True)
            for tch in range(4):
                em.wait("sp", "pe")
                em.wait("sp", "dve")
                em.wait("sp", "act")
                em.dma("sp", SC[:].rearrange("p (dt t) -> p dt t", dt=DT),
                       fm(x_own)[:, :, tch * 512:(tch + 1) * 512])
                em.dma("sp", OC[:].rearrange("p (dt t) -> p dt t", dt=DT),
                       fm(s_oun)[:, :, tch * 512:(tch + 1) * 512])
                em.wait_dma("vector", "sp")
                em.wait_dma("tensor", "sp")
                em.wait_dma("scalar", "sp")
                for dt in range(DT):
                    em.dve(lambda h, dt=dt: h.scalar_tensor_tensor(
                        out=SC[:, dt * 512:(dt + 1) * 512],
                        in0=OC[:, dt * 512:(dt + 1) * 512],
                        scalar=c_small[:, 3:4],
                        in1=SC[:, dt * 512:(dt + 1) * 512],
                        op0=ALU.mult, op1=ALU.add), inc=True)
                    em.dve(lambda h, dt=dt, tch=tch: h.tensor_tensor(
                        out=SC[:, dt * 512:(dt + 1) * 512],
                        in0=SC[:, dt * 512:(dt + 1) * 512],
                        in1=ACC[:, dt * TO + tch * 512:dt * TO + tch * 512 + 512],
                        op=ALU.add), inc=True)
                em.wait("sync", "dve")
                em.dma("sp", fm(s_x1)[:, :, tch * 512:(tch + 1) * 512],
                       SC[:].rearrange("p (dt t) -> p dt t", dt=DT))

                def dst(em, dt, off, bw, tch=tch):
                    em.dve(lambda h, dt=dt, off=off, bw=bw: h.tensor_copy(
                        out=OC[:, dt * 512 + off:dt * 512 + off + bw],
                        in_=SD[:, dt * 512 + off:dt * 512 + off + bw]), inc=True)

                em.wait("tensor", "dve")
                em.wait("scalar", "dve")
                ln_chunk(em, SC, SD, 512, dst)
                em.wait("sync", "dve")
                em.dma("sp", fm(cc_x_i)[:, :, tch * 512:(tch + 1) * 512],
                       OC[:].rearrange("p (dt t) -> p dt t", dt=DT))
                em.wait("scalar", "dve")
                for dt in range(DT):
                    sl = slice(dt * 512, (dt + 1) * 512)
                    em.act(lambda h, sl=sl: h.activation(
                        SC[:, sl], SD[:, sl], AF.Identity, scale=16.0,
                        bias=c_small[:, 5:6]), inc=True)
                em.wait("vector", "act")
                em.wait_dma("vector", "sp")
                for dt in range(DT):
                    sl = slice(dt * 512, (dt + 1) * 512)
                    em.dve(lambda h, sl=sl: h.tensor_scalar(
                        out=SC[:, sl], in0=SC[:, sl], scalar1=MAGIC, scalar2=127.0,
                        op0=ALU.subtract, op1=ALU.min), inc=True)
                    em.dve(lambda h, sl=sl: h.tensor_scalar(
                        out=OC[:, sl], in0=SC[:, sl], scalar1=-128.0, scalar2=0.0625,
                        op0=ALU.max, op1=ALU.mult), inc=True)
                em.wait("sync", "dve")
                em.dma("sp", fm(s_xq)[:, :, tch * 512:(tch + 1) * 512],
                       OC[:].rearrange("p (dt t) -> p dt t", dt=DT))
                if "x1" in taps:
                    em.dma("sp", fm(taps["x1"])[:, :, tch * 512:(tch + 1) * 512],
                           SC[:, 0:0] if False else
                           fm(s_x1)[:, :, tch * 512:(tch + 1) * 512])
            em.wait_dma("gpsimd", "sp")
            em.collective("AllGather", ALU.bypass, [cc_x_i.opt()], [cc_x_o.opt()])

        em.phase(p6)
        sb_free("SC_p6")

    def mk_p7():
        W1q = sb("W1q_p7", [128, DT * 4096], BF16)
        W2q = sb("W2q_p7", [128, 32 * 1024], BF16)
        CB = sb("CB_p7", [128, 48], F32)

        def quant_weights(em):
            em.dma("sp", CB[:, 0:32], b1f)
            em.dma("sp", CB[:, 32:40], b2f)
            em.wait_dma("scalar", "sp")
            em.wait_dma("vector", "sp")
            em.act(lambda h: h.activation(CB[:, 0:32], CB[:, 0:32], AF.Identity,
                                          scale=64.0, bias=c_small[:, 5:6]),
                   inc=True)
            em.act(lambda h: h.activation(CB[:, 32:40], CB[:, 32:40], AF.Identity,
                                          scale=64.0, bias=c_small[:, 5:6]),
                   inc=True)
            em.wait("vector", "act")
            em.dve(lambda h: h.tensor_scalar(out=CB[:, 0:32], in0=CB[:, 0:32],
                                             scalar1=MAGIC, scalar2=127.0,
                                             op0=ALU.subtract, op1=ALU.min),
                   inc=True)
            em.dve(lambda h: h.tensor_scalar(out=CB[:, 32:40], in0=CB[:, 32:40],
                                             scalar1=MAGIC, scalar2=127.0,
                                             op0=ALU.subtract, op1=ALU.min), inc=True)
            em.dve(lambda h: h.tensor_single_scalar(out=CB[:, 0:32], in_=CB[:, 0:32],
                                                    scalar=-128.0, op=ALU.max),
                   inc=True)
            em.dve(lambda h: h.tensor_single_scalar(out=CB[:, 32:40],
                                                    in_=CB[:, 32:40],
                                                    scalar=-128.0, op=ALU.max),
                   inc=True)
            em.wait("scalar", "dve")
            em.act(lambda h: h.activation(CB[:, 0:32], CB[:, 0:32], AF.Identity,
                                          scale=0.25, bias=c_small[:, 5:6]),
                   inc=True)
            em.act(lambda h: h.activation(CB[:, 32:40], CB[:, 32:40], AF.Identity,
                                          scale=1.0 / 64.0), inc=True)
            plan = []
            for src, dstW, nkt, ncols in ((w1t, W1q, DT, 4096), (w2t, W2q, 32, 1024)):
                for ch in range((nkt * ncols) // 4096):
                    plan.append((src, dstW, nkt, ch, 4096 // nkt))

            def qload(i):
                src, dstW, nkt, ch, cpk = plan[i]
                buf = SC if i % 2 == 0 else SD
                q = "sp" if i % 2 == 0 else "gp"
                em.wait(q, "dve")
                em.dma(q, buf[:].rearrange("p (kt t) -> p kt t", kt=nkt),
                       src.rearrange("(kt p) n -> p kt n", kt=nkt)
                       [:, :, ch * cpk:(ch + 1) * cpk])
                qmarks[i] = em.mark_dma(q)

            qmarks = [0] * len(plan)
            qload(0)
            for i, (src, dstW, nkt, ch, cpk) in enumerate(plan):
                if i + 1 < len(plan):
                    qload(i + 1)
                buf = SC if i % 2 == 0 else SD
                q = "sp" if i % 2 == 0 else "gp"
                em.wait_dma("scalar", q, value=qmarks[i])
                em.act(lambda h, buf=buf: h.activation(
                    buf[:], buf[:], AF.Identity,
                    scale=64.0, bias=c_small[:, 5:6]), inc=True)
                em.wait("vector", "act")
                em.dve(lambda h, buf=buf: h.tensor_scalar(
                    out=buf[:], in0=buf[:],
                    scalar1=MAGIC, scalar2=127.0,
                    op0=ALU.subtract, op1=ALU.min), inc=True)
                em.dve(lambda h, buf=buf, dstW=dstW, nkt=nkt, ch=ch, cpk=cpk:
                       h.tensor_scalar(
                           out=dstW[:].rearrange("p (kt t) -> p kt t", kt=nkt)
                           [:, :, ch * cpk:(ch + 1) * cpk],
                           in0=buf[:].rearrange("p (kt t) -> p kt t", kt=nkt),
                           scalar1=-128.0, scalar2=1.0 / 64.0,
                           op0=ALU.max, op1=ALU.mult), inc=True)

        def p7main(em):
            Xq = SD.bitcast(BF16)
            Hq = SC.bitcast(BF16)
            em.wait("tensor", "dve")
            for tch in range(4):
                em.wait("sp", "pe")
                em.wait("sp", "dve")
                em.dma("sp", Xq[:, 0:DT * 512].rearrange("p (dt t) -> p dt t", dt=DT),
                       fm(s_xq)[:, :, tch * 512:(tch + 1) * 512])
                em.wait_dma("tensor", "sp")
                em.wait("vector", "pe")
                for ht in range(32):
                    hb = ps5 if ht % 2 == 0 else ps6
                    em.wait("tensor", "dve", lag=1)
                    for kt in range(DT):
                        em.pe(lambda h, ht=ht, kt=kt, hb=hb: h.matmul(
                            hb[:, 0:512],
                            W1q[:, kt * 4096 + ht * 128:kt * 4096 + ht * 128 + 128],
                            Xq[:, kt * 512:kt * 512 + 512],
                            start=(kt == 0), stop=(kt == DT - 1)),
                            inc=(kt == DT - 1))
                    em.wait("scalar", "pe")
                    em.act(lambda h, hb=hb, ht=ht: h.activation(
                        ST[:, 0:512], hb[:, 0:512], AF.Identity,
                        scale=16.0, bias=CB[:, ht:ht + 1]), inc=True)
                    em.wait("vector", "act")
                    em.dve(lambda h: h.tensor_scalar(
                        out=ST[:, 512:1024], in0=ST[:, 0:512],
                        scalar1=MAGIC, scalar2=127.0,
                        op0=ALU.subtract, op1=ALU.min), inc=True)
                    em.dve(lambda h, ht=ht: h.tensor_scalar(
                        out=HQ[:, ht * 512:(ht + 1) * 512], in0=ST[:, 512:1024],
                        scalar1=0.0, scalar2=0.0625,
                        op0=ALU.max, op1=ALU.mult), inc=True)
                em.wait("tensor", "dve")
                for dth in range(2):
                    if dth == 1:
                        em.wait("tensor", "act")
                    for ht in range(32):
                        for i in range(4):
                            dt = dth * 4 + i
                            em.pe(lambda h, ht=ht, i=i, dt=dt: h.matmul(
                                OB[i][:, 0:512],
                                W2q[:, ht * 1024 + dt * 128:ht * 1024 + dt * 128 + 128],
                                HQ[:, ht * 512:(ht + 1) * 512],
                                start=(ht == 0), stop=(ht == 31)),
                                inc=(ht == 31 and i == 3))
                    em.wait("scalar", "pe")
                    for i in range(4):
                        dt = dth * 4 + i
                        em.act(lambda h, i=i, dt=dt: h.activation(
                            SC[:, i * 512:(i + 1) * 512], OB[i][:, 0:512],
                            AF.Identity, scale=1.0,
                            bias=CB[:, 32 + dt:32 + dt + 1]), inc=(i == 3))
                    em.wait("sync", "act")
                    em.dma("sp", fm(s_ffn1)[:, dth * 4:dth * 4 + 4,
                                            tch * 512:(tch + 1) * 512],
                           SC[:, 0:2048].rearrange("p (dt t) -> p dt t", dt=4))
                    em.wait_dma("scalar", "sp")
            if "ffn1" in taps:
                em.dma("sp", taps["ffn1"], s_ffn1)
                em.wait_dma("sync", "sp")

        SC = sb("SC_p7", [128, 4096], F32)
        SD = sb("SD_p7", [128, 4096], F32)

        def p7(em):
            quant_weights(em)
            p7main(em)

        HQ = sb("HQ_p7", [128, 32 * 512], BF16)
        em.phase(p7)
        sb_free("W1q_p7")

    LPAD = PAD
    CTW = LPAD + T

    def mk_p8():
        Vc = sb("Vc_p8", [128, 4 * CTW], BF16)
        PFM = sb("PFM_p8", [10, T], F32)
        PB = sb("PB_p8", [2, T], BF16)
        CGT = sb("CGT_p8", [128, 8], F32)
        Xf = sb("Xf_p8", [128, DT * T], BF16)
        Wp = sb("Wp_p8", [128, DT * 512], BF16)
        DG = sb("DG_p8", [128, 4 * 128], BF16)
        CVW = sb("CVW_p8", [128, 16], F32)
        IDB = sb("IDB_p8", [128, 128], BF16)
        LIN = sb("LIN_p8", [128, 4 * CTW], BF16)
        OUT = sb("OUT_p8", [128, 4 * CTW], BF16)

        def p8(em):
            em.wait_cc("gpsimd")
            srcmap = {0: (0, 0), 1: (1, 0), 2: (1, 1), 3: (0, 1)}
            for tq in range(4):
                r, pos = srcmap[tq]
                em.dma("gp", Xf[:].rearrange("p (dt t) -> p dt t", dt=DT)
                       [:, :, tq * 1024:(tq + 1) * 1024],
                       cc_x_o[r * D:(r + 1) * D, pos * 1024:(pos + 1) * 1024]
                       .rearrange("(dt p) t -> p dt t", dt=DT))
            em.dma("sp", IDB[:], ident_b)
            em.wait_dma("vector", "gp")
            em.wait_dma("vector", "sp")

            if "xn1f" in taps:
                em.wait_dma("gp", "gp")
                em.dma("gp", fm(taps["xn1f"]),
                       Xf[:].rearrange("p (dt t) -> p dt t", dt=DT))
                em.wait_dma("gp", "gp")
            def proj_conv(wten, cvten, dst, do_norm, spill=None):
                em.wait("gp", "pe")
                em.dma("gp", Wp[:].rearrange("p (kt n) -> p kt n", kt=DT),
                       wten.rearrange("(kt p) n -> p kt n", kt=DT))
                em.dma("sp", CVW[:], cvten)
                em.wait("vector", "pe")
                em.dve(lambda h: h.memset(LIN[:].rearrange(
                    "p (ct t) -> p ct t", ct=4)[:, :, 0:LPAD], 0.0), inc=True)
                em.dve(lambda h, dst=dst: h.memset(dst[:].rearrange(
                    "p (ct t) -> p ct t", ct=4)[:, :, 0:LPAD], 0.0), inc=True)
                em.wait_dma("tensor", "gp")
                em.wait_dma("tensor", "sp")
                em.wait("tensor", "dve")
                for ct in range(4):
                    for tch in range(8):
                        bank = ps5 if (ct * 8 + tch) % 2 == 0 else ps6
                        em.wait("tensor", "act", lag=1)
                        for kt in range(DT):
                            em.pe(lambda h, ct=ct, tch=tch, kt=kt, bank=bank:
                                  h.matmul(
                                      bank[:, 0:512],
                                      Wp[:, kt * 512 + ct * 128:
                                         kt * 512 + ct * 128 + 128],
                                      Xf[:, kt * T + tch * 512:
                                         kt * T + tch * 512 + 512],
                                      start=(kt == 0), stop=(kt == DT - 1)),
                                  inc=(kt == DT - 1))
                        em.wait("scalar", "pe")
                        em.act(lambda h, ct=ct, tch=tch, bank=bank: h.activation(
                            LIN[:, ct * CTW + LPAD + tch * 512:
                                ct * CTW + LPAD + tch * 512 + 512],
                            bank[:, 0:512], AF.Identity), inc=True)
                if DBG_SPILL_LIN and spill is not None and "wdump" in taps:
                    em.wait("gpsimd", "pe")
                    em.dma("gp", taps["wdump"], Wp[:, 0:4096])
                if DBG_SPILL_LIN and spill is not None:
                    em.wait("sync", "act")
                    em.dma("sp", spill.rearrange("(ct p) t -> p ct t", ct=4),
                           LIN[:].rearrange("p (ct t) -> p ct t", ct=4)
                           [:, :, LPAD:LPAD + T])
                    em.wait_dma("tensor", "sp")
                    em.wait_dma("vector", "sp")
                em.wait("vector", "act")
                for ct in range(4):
                    em.wait("vector", "pe")
                    for j in range(4):
                        em.dve(lambda h, ct=ct, j=j: h.tensor_single_scalar(
                            out=DG[:, j * 128:j * 128 + 128],
                            in_=IDB[:], scalar=CVW[:, ct * 4 + j:ct * 4 + j + 1],
                            op=ALU.mult), inc=(j == 3))
                    for tch in range(8):
                        bank = ps5 if tch % 2 == 0 else ps6
                        em.wait("tensor", "act", lag=1)
                        em.wait("tensor", "dve")
                        for j in range(4):
                            em.pe(lambda h, ct=ct, tch=tch, j=j, bank=bank:
                                  h.matmul(
                                      bank[:, 0:512],
                                      DG[:, j * 128:j * 128 + 128],
                                      LIN[:, ct * CTW + LPAD + tch * 512 - 3 + j:
                                          ct * CTW + LPAD + tch * 512 - 3 + j + 512],
                                      start=(j == 0), stop=(j == 3)),
                                  inc=(j == 3))
                        em.wait("scalar", "pe")
                        em.act(lambda h, ct=ct, tch=tch, bank=bank, dst=dst:
                               h.activation(
                                   dst[:, ct * CTW + LPAD + tch * 512:
                                       ct * CTW + LPAD + tch * 512 + 512],
                                   bank[:, 0:512], AF.Silu), inc=True)
                em.wait("vector", "act")
                if do_norm:
                    for hh in range(2):
                        for tch in range(8):
                            em.wait("scalar", "pe")
                            for k in range(2):
                                ct = hh * 2 + k
                                em.act(lambda h, ct=ct, tch=tch, k=k: h.activation(
                                    LIN[:, k * 512:(k + 1) * 512],
                                    dst[:, ct * CTW + LPAD + tch * 512:
                                        ct * CTW + LPAD + tch * 512 + 512],
                                    AF.Square), inc=(k == 1))
                            em.wait("tensor", "act")
                            for k in range(2):
                                em.pe(lambda h, k=k: h.matmul(
                                    ps7[0:1, 0:512], c_ones_b[:],
                                    LIN[:, k * 512:(k + 1) * 512],
                                    start=(k == 0), stop=(k == 1)), inc=(k == 1))
                            em.wait("vector", "pe")
                            em.dve(lambda h: h.tensor_single_scalar(
                                out=ST[0:1, 0:512], in_=ps7[0:1, 0:512],
                                scalar=1e-6, op=ALU.add), inc=True)
                            em.wait("scalar", "dve")
                            em.act(lambda h: h.activation(ST[0:1, 512:1024],
                                                          ST[0:1, 0:512], AF.Sqrt),
                                   inc=True)
                            em.wait("vector", "act")
                            em.dve(lambda h: h.reciprocal(out=ST[0:1, 0:512],
                                                          in_=ST[0:1, 512:1024]),
                                   inc=True)
                            em.wait("tensor", "dve")
                            em.pe(lambda h: h.matmul(ps7[:, 0:512], c_onesr_f[:],
                                                     ST[0:1, 0:512],
                                                     start=True, stop=True),
                                  inc=True)
                            em.wait("vector", "pe")
                            for k in range(2):
                                ct = hh * 2 + k
                                sl = slice(ct * CTW + LPAD + tch * 512,
                                           ct * CTW + LPAD + tch * 512 + 512)
                                em.dve(lambda h, sl=sl: h.tensor_tensor(
                                    out=dst[:, sl], in0=dst[:, sl],
                                    in1=ps7[:, 0:512], op=ALU.mult), inc=(k == 1))
                            em.wait("tensor", "dve")
                            em.wait("scalar", "dve")
                if spill is not None and not DBG_SPILL_LIN:
                    em.wait("sync", "act")
                    em.wait("sync", "dve")
                    em.dma("sp", spill.rearrange("(ct p) t -> p ct t", ct=4),
                           dst[:].rearrange("p (ct t) -> p ct t", ct=4)
                           [:, :, LPAD:LPAD + T])

            proj_conv(wq1, cvq, OUT, True, spill=s_qc)
            em.wait_dma("tensor", "sp")
            proj_conv(wk1, cvk, OUT, True, spill=s_kc)
            em.wait_dma("tensor", "sp")
            proj_conv(wv1, cvv, Vc, False, spill=None)

            em.wait("gp", "pe")
            em.dma("gp", Wp[:, 0:DT * 2].rearrange("p (kt n) -> p kt n", kt=DT),
                   wb1.rearrange("(kt p) n -> p kt n", kt=DT))
            em.wait_dma("tensor", "gp")
            for tt8 in range(8):
                em.wait("tensor", "act")
                for kt in range(DT):
                    em.pe(lambda h, tt8=tt8, kt=kt: h.matmul(
                        ps5[0:2, 0:512],
                        Wp[:, kt * 2:kt * 2 + 2],
                        Xf[:, kt * T + tt8 * 512:kt * T + tt8 * 512 + 512],
                        start=(kt == 0), stop=(kt == DT - 1)),
                        inc=(kt == DT - 1))
                em.wait("scalar", "pe")
                em.act(lambda h, tt8=tt8: h.activation(
                    PB[0:2, tt8 * 512:tt8 * 512 + 512], ps5[0:2, 0:512],
                    AF.Sigmoid), inc=True)
            if "vc1" in taps:
                em.wait("gpsimd", "act")
                em.wait("gpsimd", "dve")
                em.dma("gp", taps["vc1"].rearrange("(ct p) t -> p ct t", ct=4),
                       Vc[:].rearrange("p (ct t) -> p ct t", ct=4)
                       [:, :, LPAD:LPAD + T])
            if "qc1" in taps:
                em.wait_dma("gp", "sp")
                em.dma("gp", taps["qc1"], s_qc)
            if "kc1" in taps:
                em.wait_dma("gp", "sp")
                em.dma("gp", taps["kc1"], s_kc)

            em.wait("vector", "act")
            em.dve(lambda h: h.memset(ST[0:1, 0:128], 0.0), inc=True)
            em.wait("sp", "dve")
            em.dma("sp", ST[0:1, 0:2], ltmp)
            em.dma("sp", ST[0:1, 128:130], epsp)
            em.wait_dma("scalar", "sp")
            em.act(lambda h: h.activation(ST[0:1, 256:384], ST[0:1, 0:128],
                                          AF.Exp), inc=True)
            em.wait("vector", "act")
            em.dve(lambda h: h.reciprocal(out=ST[0:1, 0:128], in_=ST[0:1, 256:384]),
                   inc=True)
            em.dve(lambda h: h.tensor_scalar(out=ST[0:1, 128:256],
                                             in0=ST[0:1, 128:256],
                                             scalar1=0.0, scalar2=0.2,
                                             op0=ALU.max, op1=ALU.min), inc=True)
            em.dve(lambda h: h.tensor_scalar(out=ST[0:1, 384:512],
                                             in0=ST[0:1, 128:256],
                                             scalar1=-5.0, scalar2=1.0,
                                             op0=ALU.mult, op1=ALU.add), inc=True)
            em.wait("tensor", "dve")
            for src_off, dst_col in ((0, 0), (128, 4), (384, 2)):
                em.pe(lambda h, src_off=src_off: h.matmul(
                    ps7[:, 0:128], c_onesr_f[:], ST[0:1, src_off:src_off + 128],
                    start=True, stop=True), inc=True)
                em.wait("vector", "pe")
                em.dve(lambda h, dst_col=dst_col: h.tensor_copy(
                    out=CGT[:, dst_col:dst_col + 2], in_=ps7[:, 0:2]), inc=True)
                em.wait("tensor", "dve")

            em.wait("gp", "pe")
            em.dma("gp", Wp[:, 0:DT * 16].rearrange("p (kt n) -> p kt n", kt=DT)
                   [:, :, 0:10], gw1.rearrange("(kt p) n -> p kt n", kt=DT))
            em.dma("sp", ST[0:1, 0:10], gb1)
            em.wait_dma("tensor", "sp")
            em.pe(lambda h: h.matmul(ps7[:, 0:128], c_onesr_f[:],
                                     ST[0:1, 0:128], start=True, stop=True),
                  inc=True)
            em.wait("vector", "pe")
            em.dve(lambda h: h.tensor_copy(out=c_small[:, 6:16], in_=ps7[:, 0:10]),
                   inc=True)
            em.wait_dma("tensor", "gp")
            Tg = ST
            bs_ = lambda sid: 64 * sid
            lg_ = lambda t: Tg[:, bs_(t * 2):bs_(t * 2) + 10]
            g5_ = lambda sid: Tg[:, bs_(sid) + 32:bs_(sid) + 37]
            mx_ = lambda sid: Tg[:, bs_(sid) + 16:bs_(sid) + 17]
            nmx_ = lambda sid: Tg[:, bs_(sid) + 17:bs_(sid) + 18]
            sm_ = lambda sid: Tg[:, bs_(sid) + 18:bs_(sid) + 19]
            rs_ = lambda sid: Tg[:, bs_(sid) + 19:bs_(sid) + 20]
            ex_ = lambda sid: Tg[:, bs_(sid) + 24:bs_(sid) + 29]
            ou_ = lambda t, hh: Tg[:, 512 + t * 16 + hh * 5:
                                   512 + t * 16 + hh * 5 + 5]
            ou10_ = lambda t: Tg[:, 512 + t * 16:512 + t * 16 + 10]
            pbank = [ps7, ps5]
            for tp in range(16):
                em.wait("tensor", "dve")
                for t in range(2):
                    for kt in range(DT):
                        em.pe(lambda h, tp=tp, t=t, kt=kt: h.matmul(
                            pbank[t][:, 0:10],
                            Xf[:, kt * T + (tp * 2 + t) * 128:
                               kt * T + (tp * 2 + t) * 128 + 128],
                            Wp[:, kt * 16:kt * 16 + 10],
                            start=(kt == 0), stop=(kt == DT - 1)),
                            inc=(kt == DT - 1))
                em.wait("vector", "pe")
                for t in range(2):
                    em.dve(lambda h, t=t: h.tensor_tensor(
                        out=lg_(t), in0=pbank[t][:, 0:10],
                        in1=c_small[:, 6:16], op=ALU.add), inc=(t == 1))
                ss = [(0, 0), (0, 1), (1, 0), (1, 1)]
                for t, hh in ss:
                    sid = t * 2 + hh
                    em.dve(lambda h, t=t, hh=hh, sid=sid: h.tensor_single_scalar(
                        out=g5_(sid), in_=lg_(t)[:, hh * 5:hh * 5 + 5],
                        scalar=CGT[:, hh:hh + 1], op=ALU.mult),
                        inc=(sid == 3))
                for t, hh in ss:
                    sid = t * 2 + hh
                    em.dve(lambda h, sid=sid: h.tensor_reduce(
                        out=mx_(sid), in_=g5_(sid), axis=mybir.AxisListType.X,
                        op=ALU.max), inc=(sid == 3))
                for t, hh in ss:
                    sid = t * 2 + hh
                    em.dve(lambda h, sid=sid: h.tensor_single_scalar(
                        out=nmx_(sid), in_=mx_(sid), scalar=-1.0,
                        op=ALU.mult), inc=(sid == 3))
                em.wait("scalar", "dve")
                for t, hh in ss:
                    sid = t * 2 + hh
                    em.act(lambda h, sid=sid: h.activation(
                        ex_(sid), g5_(sid), AF.Exp, bias=nmx_(sid)),
                        inc=(sid == 3))
                em.wait("vector", "act")
                for t, hh in ss:
                    sid = t * 2 + hh
                    em.dve(lambda h, sid=sid: h.tensor_reduce(
                        out=sm_(sid), in_=ex_(sid), axis=mybir.AxisListType.X,
                        op=ALU.add), inc=(sid == 3))
                for t, hh in ss:
                    sid = t * 2 + hh
                    em.dve(lambda h, sid=sid: h.reciprocal(
                        out=rs_(sid), in_=sm_(sid)), inc=(sid == 3))
                for t, hh in ss:
                    sid = t * 2 + hh
                    em.dve(lambda h, sid=sid: h.tensor_single_scalar(
                        out=ex_(sid), in_=ex_(sid), scalar=rs_(sid),
                        op=ALU.mult), inc=(sid == 3))
                for t, hh in ss:
                    sid = t * 2 + hh
                    em.dve(lambda h, sid=sid, hh=hh: h.tensor_single_scalar(
                        out=ex_(sid), in_=ex_(sid),
                        scalar=CGT[:, 2 + hh:3 + hh], op=ALU.mult),
                        inc=(sid == 3))
                for t, hh in ss:
                    sid = t * 2 + hh
                    em.dve(lambda h, sid=sid, t=t, hh=hh: h.tensor_single_scalar(
                        out=ou_(t, hh), in_=ex_(sid),
                        scalar=CGT[:, 4 + hh:5 + hh], op=ALU.add),
                        inc=(sid == 3))
                em.wait("tensor", "dve")
                for t in range(2):
                    em.pe(lambda h, t=t: h.transpose(
                        pbank[t][0:10, 128:256], ou10_(t), c_ident_f[:]),
                        inc=(t == 1))
                em.wait("vector", "pe")
                for t in range(2):
                    em.dve(lambda h, tp=tp, t=t: h.tensor_copy(
                        out=PFM[0:10, (tp * 2 + t) * 128:(tp * 2 + t) * 128 + 128],
                        in_=pbank[t][0:10, 128:256]), inc=(t == 1))
            if "probs" in taps:
                em.wait("gpsimd", "dve")
                em.dma("gp", taps["probs"], PFM[0:10, 0:T])
                em.wait_dma("gpsimd", "gp")
            em.wait_dma("gpsimd", "gp")
            em.wait_dma("sync", "sp")

        em.phase(p8)
        sb_free("Xf_p8")
        return Vc, PFM, PB, CGT

    def mk_p9(Vc, PFM, PB, CGT):
        c_selh = sb("c_selh9", [2, 256], BF16)
        c_msl = sb("c_msl9", [128, 128], F32)
        c_msu = sb("c_msu9", [128, 128], F32)
        c_tril = sb("c_tril9", [128, 128], F32)
        c_triu = sb("c_triu9", [128, 128], F32)
        FIRW = sb("FIRW_p9", [128, 4 * 81], F32)
        OACC = sb("OACC_p9", [128, 4 * TO], F32)
        ONRM = sb("ONRM_p9", [128, 4 * T], BF16)
        DGF = sb("DGF_p9", [128, 21 * 128], BF16)
        QS = sb("QS_p9", [128, 4 * 512], F32)
        KS = sb("KS_p9", [128, 4 * 512], F32)
        GAM = sb("GAM_p9", [128, 2], F32)
        CH = sb("CH_p9", [128, 32 * 128], F32)
        UW = sb("UW_p9", [128, 12 * 256], F32)
        SF = sb("SF_p9", [128, 2 * 512], F32)
        TMPF = sb("TMPF_p9", [128, 1024], F32)
        PBS = sb("PBS_p9", [128, 1024], F32)
        SEL10 = sb("SEL10_p9", [10, 10 * 128], F32)

        def cht(i):
            return CH[:, i * 128:(i + 1) * 128]

        UWr = UW[:, 0:512]
        UT = UW[:, 512:768]
        WT = UW[:, 768:1024]
        WF = UW[:, 1024:1280]
        KT = UW[:, 1280:1536]

        def p9(em):
            em.dma("sp", c_selh[:], selh)
            em.dma("sp", c_msl[:], msl_neg)
            em.dma("sp", c_msu[:], msu_neg)
            em.dma("sp", c_tril[:], tril_f)
            em.dma("sp", c_triu[:], triu_f)
            em.dma("sp", FIRW[:, 0:12], firs)
            em.dma("sp", FIRW[:, 12:72], firm)
            em.dma("sp", FIRW[:, 72:324], firl)
            em.dma("sp", GAM[:], gamm)
            em.dma("sp", SEL10[:], sel10)
            em.wait_dma("vector", "sp")
            em.dve(lambda h: h.memset(SF[:], 0.0), inc=True)

            def pbc_load(brow, toff, width, doff=0):
                em.wait("tensor", "dve")
                em.pe(lambda h, brow=brow, toff=toff, width=width: h.matmul(
                    ps7[:, 0:width], SEL10[0:10, brow * 128:(brow + 1) * 128],
                    PFM[0:10, toff:toff + width],
                    start=True, stop=True), inc=True)
                em.wait("vector", "pe")
                em.dve(lambda h, width=width, doff=doff: h.tensor_copy(
                    out=PBS[:, doff:doff + width], in_=ps7[:, 0:width]), inc=True)

            for half in range(2):
                hb = half * TO
                em.wait("vector", "pe")
                em.dve(lambda h: h.memset(OACC[:], 0.0), inc=True)
                for fi, (K, fw0) in enumerate(((3, 0), (15, 12), (63, 72))):
                    subs = ([(0, K)] if K <= 21
                            else [(0, 21), (21, 42), (42, 63)])
                    for ct in range(4):
                        hh = ct // 2
                        for si, (j0, j1) in enumerate(subs):
                            em.wait("vector", "pe")
                            for j in range(j0, j1):
                                em.dve(lambda h, ct=ct, j=j, j0=j0, fw0=fw0, K=K:
                                       h.tensor_single_scalar(
                                           out=DGF[:, (j - j0) * 128:
                                                   (j - j0) * 128 + 128],
                                           in_=IDB9[:],
                                           scalar=FIRW[:, fw0 + ct * K + j:
                                                       fw0 + ct * K + j + 1],
                                           op=ALU.mult), inc=(j == j1 - 1))
                            em.wait("tensor", "dve")
                            for tch in range(4):
                                bank = OB[tch]
                                for j in range(j0, j1):
                                    base = (ct * CTW + LPAD + hb + tch * 512
                                            - (K - 1) + j)
                                    em.pe(lambda h, j=j, j0=j0, base=base,
                                          bank=bank, j1=j1: h.matmul(
                                        bank[:, 0:512],
                                        DGF[:, (j - j0) * 128:
                                            (j - j0) * 128 + 128],
                                        Vc[:, base:base + 512],
                                        start=(j == j0),
                                        stop=(j == j1 - 1)),
                                        inc=(j == j1 - 1 and tch == 3))
                            for tch in range(4):
                                bank = OB[tch]
                                pbc_load(hh * 5 + fi, hb + tch * 512, 512)
                                em.wait("vector", "pe")
                                em.dve(lambda h, bank=bank: h.tensor_tensor(
                                    out=TMPF[:, 0:512], in0=bank[:, 0:512],
                                    in1=PBS[:, 0:512],
                                    op=ALU.mult), inc=True)
                                osl = slice(ct * TO + tch * 512,
                                            ct * TO + tch * 512 + 512)
                                em.dve(lambda h, osl=osl: h.tensor_tensor(
                                    out=OACC[:, osl], in0=OACC[:, osl],
                                    in1=TMPF[:, 0:512],
                                    op=ALU.add), inc=True)
                for ct in range(4):
                    hh = ct // 2
                    for tch in range(4):
                        pbc_load(hh * 5 + 4, hb + tch * 512, 512)
                        em.dve(lambda h, ct=ct, tch=tch, hb=hb: h.tensor_tensor(
                            out=TMPF[:, 0:512],
                            in0=Vc[:, ct * CTW + LPAD + hb + tch * 512:
                                   ct * CTW + LPAD + hb + tch * 512 + 512],
                            in1=PBS[:, 0:512], op=ALU.mult), inc=True)
                        osl = slice(ct * TO + tch * 512, ct * TO + tch * 512 + 512)
                        em.dve(lambda h, osl=osl: h.tensor_tensor(
                            out=OACC[:, osl], in0=OACC[:, osl],
                            in1=TMPF[:, 0:512],
                            op=ALU.add), inc=True)
                stm = [{"dve": 0, "pe": 0} for _ in range(2)]

                def chs(st_, i):
                    return CH[:, st_ * 2048 + i * 128:st_ * 2048 + (i + 1) * 128]

                def chunk_body(st_, half, ci):
                    hh = st_
                    pa, pc, po = ((ps5, ps6, ps1) if st_ == 0
                                  else (ps3, ps4, ps2))
                    M = stm[st_]

                    def wpe():
                        em.wait_abs("vector", "pe", M["pe"])

                    def wdve():
                        em.wait_abs("tensor", "dve", M["dve"])

                    UWrS = UW[:, st_ * 1536:st_ * 1536 + 512]
                    UTS = UW[:, st_ * 1536 + 512:st_ * 1536 + 768]
                    WTS = UW[:, st_ * 1536 + 768:st_ * 1536 + 1024]
                    WFS = UW[:, st_ * 1536 + 1024:st_ * 1536 + 1280]
                    KTS = UW[:, st_ * 1536 + 1280:st_ * 1536 + 1536]
                    UADJ = CH[:, st_ * 2048 + 1792:st_ * 2048 + 2048]
                    TMs = TMPF[:, st_ * 512:st_ * 512 + 512]
                    PBSs = PBS[:, st_ * 512:st_ * 512 + 512]
                    i4 = ci % 4
                    qn = [QS[:, hh * 1024 + k * 512 + i4 * 128:
                             hh * 1024 + k * 512 + i4 * 128 + 128]
                          for k in range(2)]
                    kn = [KS[:, hh * 1024 + k * 512 + i4 * 128:
                             hh * 1024 + k * 512 + i4 * 128 + 128]
                          for k in range(2)]
                    vsl = [Vc[:, (hh * 2 + k) * CTW + LPAD + ci * 128:
                              (hh * 2 + k) * CTW + LPAD + ci * 128 + 128]
                           for k in range(2)]
                    kb = [chs(st_, 12 + k) for k in range(2)]
                    vb = [chs(st_, 14 + k) for k in range(2)]
                    wdve()
                    em.pe(lambda h, ci=ci, hh=hh: h.matmul(
                        pa[:, 0:128], c_selh[0:2, hh * 128:hh * 128 + 128],
                        PB[0:2, ci * 128:ci * 128 + 128],
                        start=True, stop=True), inc=True)
                    M["pe"] = em.mark("pe")
                    yield
                    wpe()
                    for k in range(2):
                        em.dve(lambda h, k=k, kn=kn: h.tensor_tensor(
                            out=kb[k], in0=kn[k], in1=pa[:, 0:128],
                            op=ALU.mult), inc=True)
                        em.dve(lambda h, k=k, vsl=vsl: h.tensor_tensor(
                            out=vb[k], in0=vsl[k], in1=pa[:, 0:128],
                            op=ALU.mult), inc=(k == 1))
                    M["dve"] = em.mark("dve")
                    yield
                    wdve()
                    em.pe(lambda h, hh=hh, ci=ci: h.matmul(
                        pa[:, 0:128],
                        SEL10[0:10, (hh * 5 + 3) * 128:(hh * 5 + 3) * 128 + 128],
                        PFM[0:10, ci * 128:ci * 128 + 128],
                        start=True, stop=True), inc=True)
                    M["pe"] = em.mark("pe")
                    yield
                    wpe()
                    em.dve(lambda h, PBSs=PBSs: h.tensor_copy(
                        out=PBSs[:, 0:128], in_=pa[:, 0:128]), inc=True)
                    M["dve"] = em.mark("dve")
                    yield
                    wdve()
                    for k in range(2):
                        em.pe(lambda h, k=k: h.matmul(
                            pa[:, 0:128], kb[k], kn[k],
                            start=(k == 0), stop=(k == 1)), inc=(k == 1))
                    M["pe"] = em.mark("pe")
                    yield
                    wpe()
                    em.dve(lambda h: h.tensor_tensor(
                        out=chs(st_, 2), in0=pa[:, 0:128], in1=c_msl[:],
                        op=ALU.mult), inc=True)
                    M["dve"] = em.mark("dve")
                    yield
                    wdve()
                    for k in range(2):
                        em.pe(lambda h, k=k: h.matmul(
                            pa[:, 0:128], kn[k], kb[k],
                            start=(k == 0), stop=(k == 1)), inc=(k == 1))
                    M["pe"] = em.mark("pe")
                    yield
                    wpe()
                    em.dve(lambda h, TMs=TMs: h.tensor_tensor(
                        out=TMs[:, 0:128], in0=pa[:, 0:128],
                        in1=c_msu[:], op=ALU.mult), inc=True)
                    em.dve(lambda h, TMs=TMs: h.tensor_copy(
                        out=chs(st_, 3), in_=TMs[:, 0:128]), inc=True)
                    em.dve(lambda h, TMs=TMs: h.tensor_tensor(
                        out=chs(st_, 7), in0=TMs[:, 0:128], in1=c_ident_f[:],
                        op=ALU.add), inc=True)
                    M["dve"] = em.mark("dve")
                    yield
                    for lev in range(1, 7):
                        wdve()
                        em.pe(lambda h: h.matmul(pc[:, 0:128], chs(st_, 3),
                                                 chs(st_, 2), start=True,
                                                 stop=True), inc=True)
                        em.pe(lambda h: h.matmul(pa[:, 0:128], chs(st_, 2),
                                                 chs(st_, 3), start=True,
                                                 stop=True), inc=True)
                        M["pe"] = em.mark("pe")
                        yield
                        wpe()
                        em.dve(lambda h: h.tensor_tensor(
                            out=chs(st_, 6), in0=pc[:, 0:128], in1=c_ident_f[:],
                            op=ALU.add), inc=True)
                        em.dve(lambda h: h.tensor_copy(out=chs(st_, 2),
                                                       in_=pc[:, 0:128]),
                               inc=True)
                        em.dve(lambda h: h.tensor_copy(out=chs(st_, 3),
                                                       in_=pa[:, 0:128]),
                               inc=True)
                        M["dve"] = em.mark("dve")
                        yield
                        wdve()
                        em.pe(lambda h: h.matmul(pc[:, 0:128], chs(st_, 6),
                                                 chs(st_, 7), start=True,
                                                 stop=True), inc=True)
                        M["pe"] = em.mark("pe")
                        yield
                        wpe()
                        em.dve(lambda h: h.tensor_copy(out=chs(st_, 7),
                                                       in_=pc[:, 0:128]),
                               inc=True)
                        M["dve"] = em.mark("dve")
                        yield
                    wdve()
                    em.pe(lambda h: h.transpose(pa[:, 0:128], vb[0],
                                                c_ident_f[:]), inc=True)
                    em.pe(lambda h: h.transpose(pc[:, 0:128], vb[1],
                                                c_ident_f[:]), inc=True)
                    M["pe"] = em.mark("pe")
                    yield
                    wpe()
                    em.dve(lambda h, UWrS=UWrS: h.tensor_copy(
                        out=UWrS[:, 0:128], in_=pa[:, 0:128]), inc=True)
                    em.dve(lambda h, UWrS=UWrS: h.tensor_copy(
                        out=UWrS[:, 128:256], in_=pc[:, 0:128]), inc=True)
                    M["dve"] = em.mark("dve")
                    yield
                    wdve()
                    em.pe(lambda h: h.transpose(pa[:, 0:128], kb[0],
                                                c_ident_f[:]), inc=True)
                    em.pe(lambda h: h.transpose(pc[:, 0:128], kb[1],
                                                c_ident_f[:]), inc=True)
                    M["pe"] = em.mark("pe")
                    yield
                    wpe()
                    em.dve(lambda h, UWrS=UWrS: h.tensor_copy(
                        out=UWrS[:, 256:384], in_=pa[:, 0:128]), inc=True)
                    em.dve(lambda h, UWrS=UWrS: h.tensor_copy(
                        out=UWrS[:, 384:512], in_=pc[:, 0:128]), inc=True)
                    M["dve"] = em.mark("dve")
                    yield
                    wdve()
                    em.pe(lambda h, UWrS=UWrS: h.matmul(
                        pc[:, 0:512], chs(st_, 7), UWrS,
                        start=True, stop=True), inc=True)
                    M["pe"] = em.mark("pe")
                    yield
                    wpe()
                    em.dve(lambda h, UTS=UTS: h.tensor_copy(
                        out=UTS, in_=pc[:, 0:256]), inc=True)
                    em.dve(lambda h, WTS=WTS: h.tensor_copy(
                        out=WTS, in_=pc[:, 256:512]), inc=True)
                    M["dve"] = em.mark("dve")
                    yield
                    wdve()
                    em.pe(lambda h, WTS=WTS: h.transpose(
                        pa[:, 0:128], WTS[:, 0:128], c_ident_f[:]), inc=True)
                    em.pe(lambda h, WTS=WTS: h.transpose(
                        pc[:, 0:128], WTS[:, 128:256], c_ident_f[:]), inc=True)
                    M["pe"] = em.mark("pe")
                    yield
                    wpe()
                    em.dve(lambda h, WFS=WFS: h.tensor_copy(
                        out=WFS[:, 0:128], in_=pa[:, 0:128]), inc=True)
                    em.dve(lambda h, WFS=WFS: h.tensor_copy(
                        out=WFS[:, 128:256], in_=pc[:, 0:128]), inc=True)
                    M["dve"] = em.mark("dve")
                    yield
                    wdve()
                    em.pe(lambda h, kn=kn: h.transpose(
                        pa[:, 0:128], kn[0], c_ident_f[:]), inc=True)
                    em.pe(lambda h, kn=kn: h.transpose(
                        pc[:, 0:128], kn[1], c_ident_f[:]), inc=True)
                    M["pe"] = em.mark("pe")
                    yield
                    wpe()
                    em.dve(lambda h, KTS=KTS: h.tensor_copy(
                        out=KTS[:, 0:128], in_=pa[:, 0:128]), inc=True)
                    em.dve(lambda h, KTS=KTS: h.tensor_copy(
                        out=KTS[:, 128:256], in_=pc[:, 0:128]), inc=True)
                    M["dve"] = em.mark("dve")
                    yield
                    wdve()
                    for k in range(2):
                        em.pe(lambda h, k=k: h.matmul(
                            pa[:, 0:128], kn[k], qn[k],
                            start=(k == 0), stop=(k == 1)), inc=(k == 1))
                    M["pe"] = em.mark("pe")
                    yield
                    wpe()
                    em.dve(lambda h: h.tensor_tensor(
                        out=chs(st_, 11), in0=pa[:, 0:128], in1=c_triu[:],
                        op=ALU.mult), inc=True)
                    M["dve"] = em.mark("dve")
                    yield
                    wdve()
                    for k in range(2):
                        em.pe(lambda h, k=k, WFS=WFS, hh=hh: h.matmul(
                            pc[:, 0:256],
                            WFS[:, k * 128:k * 128 + 128],
                            SF[:, hh * 512 + k * 256:hh * 512 + k * 256 + 256],
                            start=(k == 0), stop=(k == 1)), inc=(k == 1))
                    M["pe"] = em.mark("pe")
                    yield
                    wpe()
                    em.dve(lambda h, TMs=TMs, UTS=UTS: h.tensor_tensor(
                        out=TMs[:, 0:256], in0=UTS, in1=pc[:, 0:256],
                        op=ALU.subtract), inc=True)
                    em.dve(lambda h, TMs=TMs, UADJ=UADJ: h.tensor_copy(
                        out=UADJ, in_=TMs[:, 0:256]), inc=True)
                    M["dve"] = em.mark("dve")
                    yield
                    for dvt in range(2):
                        wdve()
                        for k in range(2):
                            em.pe(lambda h, dvt=dvt, k=k, hh=hh, qn=qn:
                                  h.matmul(
                                      po[:, 0:128],
                                      SF[:, hh * 512 + k * 256 + dvt * 128:
                                         hh * 512 + k * 256 + dvt * 128 + 128],
                                      qn[k], start=(k == 0), stop=False),
                                  inc=False)
                        em.pe(lambda h, dvt=dvt, UADJ=UADJ: h.matmul(
                            po[:, 0:128],
                            UADJ[:, dvt * 128:dvt * 128 + 128],
                            chs(st_, 11), start=False, stop=True), inc=True)
                        M["pe"] = em.mark("pe")
                        yield
                        wpe()
                        em.dve(lambda h, TMs=TMs, PBSs=PBSs: h.tensor_tensor(
                            out=TMs[:, 256:384], in0=po[:, 0:128],
                            in1=PBSs[:, 0:128], op=ALU.mult), inc=True)
                        osl = slice((hh * 2 + dvt) * TO + (ci - half * 16) * 128,
                                    (hh * 2 + dvt) * TO + (ci - half * 16) * 128
                                    + 128)
                        em.dve(lambda h, osl=osl, TMs=TMs: h.tensor_tensor(
                            out=OACC[:, osl], in0=OACC[:, osl],
                            in1=TMs[:, 256:384], op=ALU.add), inc=True)
                        M["dve"] = em.mark("dve")
                        if "dlt" in taps:
                            em.wait("gp", "dve")
                            em.dma("gp", taps["dlt"]
                                   [(hh * 2 + dvt) * 128:
                                    (hh * 2 + dvt + 1) * 128,
                                    ci * 128:ci * 128 + 128],
                                   TMs[:, 256:384])
                            em.wait_dma("vector", "gp")
                        yield
                    for k in range(2):
                        wdve()
                        em.pe(lambda h, k=k, KTS=KTS, UADJ=UADJ: h.matmul(
                            pc[:, 0:256],
                            KTS[:, k * 128:k * 128 + 128],
                            UADJ[:, 0:256],
                            start=True, stop=True), inc=True)
                        M["pe"] = em.mark("pe")
                        yield
                        wpe()
                        ssl = slice(hh * 512 + k * 256, hh * 512 + k * 256 + 256)
                        em.dve(lambda h, ssl=ssl: h.tensor_tensor(
                            out=SF[:, ssl], in0=SF[:, ssl],
                            in1=pc[:, 0:256], op=ALU.add), inc=True)
                        M["dve"] = em.mark("dve")
                        yield

                for g in range(4):
                    gi = half * 16 + g * 4
                    em.wait("gp", "pe")
                    em.wait("gp", "dve")
                    em.dma("gp", QS[:].rearrange("p (hk t) -> p hk t", hk=4),
                           s_qc[:, gi * 128:gi * 128 + 512]
                           .rearrange("(hk p) t -> p hk t", hk=4))
                    em.dma("gp", KS[:].rearrange("p (hk t) -> p hk t", hk=4),
                           s_kc[:, gi * 128:gi * 128 + 512]
                           .rearrange("(hk p) t -> p hk t", hk=4))
                    em.wait_dma("tensor", "gp")
                    em.wait_dma("vector", "gp")
                    for i4 in range(4):
                        gens = [chunk_body(0, half, gi + i4),
                                chunk_body(1, half, gi + i4)]
                        alive = list(gens)
                        while alive:
                            nxt = []
                            for gg in alive:
                                try:
                                    next(gg)
                                    nxt.append(gg)
                                except StopIteration:
                                    pass
                            alive = nxt
                for hh in range(2):
                    for tch in range(4):
                        em.wait("scalar", "dve")
                        em.wait("scalar", "pe")
                        for k in range(2):
                            em.act(lambda h, k=k, hh=hh, tch=tch: h.activation(
                                QS[:, k * 512:(k + 1) * 512],
                                OACC[:, (hh * 2 + k) * TO + tch * 512:
                                     (hh * 2 + k) * TO + tch * 512 + 512],
                                AF.Square), inc=(k == 1))
                        em.wait("tensor", "act")
                        for k in range(2):
                            em.pe(lambda h, k=k: h.matmul(
                                ps7[0:1, 0:512], c_ones_f[:],
                                QS[:, k * 512:(k + 1) * 512],
                                start=(k == 0), stop=(k == 1)), inc=(k == 1))
                        em.wait("vector", "pe")
                        em.dve(lambda h: h.tensor_scalar(
                            out=ST[0:1, 0:512], in0=ps7[0:1, 0:512],
                            scalar1=1.0 / 256.0, scalar2=1e-5,
                            op0=ALU.mult, op1=ALU.add), inc=True)
                        em.wait("scalar", "dve")
                        em.act(lambda h: h.activation(ST[0:1, 512:1024],
                                                      ST[0:1, 0:512], AF.Sqrt),
                               inc=True)
                        em.wait("vector", "act")
                        em.dve(lambda h: h.reciprocal(out=ST[0:1, 0:512],
                                                      in_=ST[0:1, 512:1024]),
                               inc=True)
                        em.wait("tensor", "dve")
                        em.pe(lambda h: h.matmul(ps7[:, 0:512], c_onesr_f[:],
                                                 ST[0:1, 0:512],
                                                 start=True, stop=True), inc=True)
                        em.wait("vector", "pe")
                        for k in range(2):
                            ct = hh * 2 + k
                            em.dve(lambda h, ct=ct, tch=tch: h.tensor_tensor(
                                out=TMPF[:, 0:512],
                                in0=OACC[:, ct * TO + tch * 512:ct * TO + tch * 512
                                        + 512],
                                in1=ps7[:, 0:512], op=ALU.mult), inc=True)
                            em.dve(lambda h, ct=ct, tch=tch, hb=hb:
                                   h.tensor_single_scalar(
                                       out=ONRM[:, ct * T + hb + tch * 512:
                                               ct * T + hb + tch * 512 + 512],
                                       in_=TMPF[:, 0:512], scalar=GAM[:, k:k + 1],
                                       op=ALU.mult), inc=(k == 1))
                        em.wait("tensor", "dve")
                em.wait("sync", "dve")
                for tq in (half * 2, half * 2 + 1):
                    pos = {0: 0, 1: 2, 2: 3, 3: 1}[tq]
                    em.dma("sp", cc_o_i[:, pos * 1024:(pos + 1) * 1024]
                           .rearrange("(ct p) t -> p ct t", ct=4),
                           ONRM[:].rearrange("p (ct t) -> p ct t", ct=4)
                           [:, :, tq * 1024:(tq + 1) * 1024])
            em.wait_dma("gpsimd", "sp")
            em.collective("AllGather", ALU.bypass, [cc_o_i.opt()], [cc_o_o.opt()])
            if "onrm" in taps:
                em.wait("gpsimd", "dve")
                em.dma("gp", taps["onrm"].rearrange("(ct p) t -> p ct t", ct=4),
                       ONRM[:].rearrange("p (ct t) -> p ct t", ct=4))
                em.wait_dma("gpsimd", "gp")

        IDB9 = sb("IDB9_p9", [128, 128], BF16)

        def p9pre(em):
            em.dma("sp", IDB9[:], ident_b)
            em.wait_dma("sync", "sp")

        em.phase(p9pre)
        em.phase(p9)
        sb_free("c_msl9")

    def mk_p10():
        OGA = sb("OGA_p10", [128, 8 * T], BF16)
        OSL = sb("OSL_p10", [128, 8 * TO], BF16)
        WOT = sb("WOT_p10", [128, 8 * 1024], BF16)
        SC = sb("SC_p10", [128, 4096], F32)
        SD = sb("SD_p10", [128, 4096], F32)

        def p10(em):
            em.wait_cc("gpsimd")
            em.dma("gp", OGA[:].rearrange("p (ct t) -> p ct t", ct=8),
                   cc_o_o.rearrange("(ct p) t -> p ct t", ct=8))
            em.dma("gp", WOT[:].rearrange("p (kt n) -> p kt n", kt=DT),
                   wo1.rearrange("(kt p) n -> p kt n", kt=DT))
            em.wait_dma("vector", "gp")
            for ct in range(8):
                for tch in range(4):
                    em.dve(lambda h, ct=ct, tch=tch: h.tensor_single_scalar(
                        out=OSL[:, ct * TO + tch * 512:ct * TO + tch * 512 + 512],
                        in_=OGA[:, ct * T + TO + tch * 512:
                               ct * T + TO + tch * 512 + 512],
                        scalar=c_sel[:, 1:2], op=ALU.mult), inc=True)
                    em.dve(lambda h, ct=ct, tch=tch: h.scalar_tensor_tensor(
                        out=OSL[:, ct * TO + tch * 512:ct * TO + tch * 512 + 512],
                        in0=OGA[:, ct * T + tch * 512:ct * T + tch * 512 + 512],
                        scalar=c_sel[:, 0:1],
                        in1=OSL[:, ct * TO + tch * 512:ct * TO + tch * 512 + 512],
                        op0=ALU.mult, op1=ALU.add), inc=True)
            for tch in range(4):
                em.wait("sp", "dve")
                em.dma("sp", SC[:].rearrange("p (dt t) -> p dt t", dt=DT),
                       fm(s_x1)[:, :, tch * 512:(tch + 1) * 512])
                em.dma("sp", SD[:].rearrange("p (dt t) -> p dt t", dt=DT),
                       fm(s_ffn1)[:, :, tch * 512:(tch + 1) * 512])
                em.wait_dma("vector", "sp")
                for dt in range(DT):
                    bank = ps5 if dt % 2 == 0 else ps6
                    em.wait("tensor", "dve", lag=1)
                    for ct in range(8):
                        em.pe(lambda h, dt=dt, ct=ct, tch=tch, bank=bank: h.matmul(
                            bank[:, 0:512],
                            WOT[:, ct * 1024 + dt * 128:ct * 1024 + dt * 128 + 128],
                            OSL[:, ct * TO + tch * 512:ct * TO + tch * 512 + 512],
                            start=(ct == 0), stop=(ct == 7)), inc=(ct == 7))
                    em.wait("vector", "pe")
                    em.dve(lambda h, dt=dt, bank=bank: h.tensor_tensor(
                        out=SC[:, dt * 512:(dt + 1) * 512],
                        in0=SC[:, dt * 512:(dt + 1) * 512], in1=bank[:, 0:512],
                        op=ALU.add), inc=True)
                    em.dve(lambda h, dt=dt: h.tensor_tensor(
                        out=SC[:, dt * 512:(dt + 1) * 512],
                        in0=SC[:, dt * 512:(dt + 1) * 512],
                        in1=SD[:, dt * 512:(dt + 1) * 512],
                        op=ALU.add), inc=True)
                em.wait("sync", "dve")
                em.dma("sp", fm(y)[:, :, tch * 512:(tch + 1) * 512],
                       SC[:].rearrange("p (dt t) -> p dt t", dt=DT))
            em.wait_dma("sync", "sp")
            em.wait_dma("gpsimd", "gp")

        em.phase(p10)
        sb_free("OGA_p10")

    if STAGE >= 0:
        em.phase(p0)
    if STAGE >= 1:
        mk_p1()
    Ak = Av = None
    if STAGE >= 2:
        Ak = mk_p2()
        Av = mk_p3()
    if STAGE >= 4:
        mk_p4(Ak, Av)
        sb_free("Ak")
        sb_free("c_rowbc", side="right")
    if STAGE >= 5:
        ACC = mk_p5()
    if STAGE >= 6:
        mk_p6(ACC)
        sb_free("ACC_p5")
    if STAGE >= 7:
        mk_p7()
    if STAGE >= 8:
        Vc, PFM, PB, CGT = mk_p8()
    if STAGE >= 9:
        mk_p9(Vc, PFM, PB, CGT)
        sb_free("Vc_p8")
    if STAGE >= 10:
        mk_p10()

    def p4dbg_mk():
        Abuf = sb("Abuf_dbg", [128, DT * 512], BF16)
        SCd = sb("SCd_dbg", [128, DT * 512], F32)

        def p4dbg(em):
            em.wait_cc("gpsimd")
            em.dma("gp", ST[0:1, 0:256].rearrange("o (a n) -> o a n", a=2),
                   cc_z_o.rearrange("(a o) n -> o a n", a=2))
            em.wait_dma("vector", "gp")
            em.dve(lambda h: h.tensor_tensor(out=ST[0:1, 512:640],
                                             in0=ST[0:1, 0:128],
                                             in1=ST[0:1, 128:256], op=ALU.add),
                   inc=True)
            em.dve(lambda h: h.tensor_single_scalar(out=ST[0:1, 512:640],
                                                    in_=ST[0:1, 512:640],
                                                    scalar=1e-20, op=ALU.max),
                   inc=True)
            em.dve(lambda h: h.reciprocal(out=ST[0:1, 640:768],
                                          in_=ST[0:1, 512:640]), inc=True)
            em.wait("tensor", "dve")
            em.pe(lambda h: h.matmul(ps5[:, 0:128], c_onesr_f[:], ST[0:1, 640:768],
                                     start=True, stop=True), inc=True)
            em.wait("vector", "pe")
            em.dve(lambda h: h.tensor_copy(out=c_small[:, 3:4], in_=ps5[:, 0:1]),
                   inc=True)
            if "zz" in taps:
                em.dve(lambda h: h.tensor_copy(out=ST[0:1, 256:260],
                                               in_=c_small[0:1, 0:4]), inc=True)
                em.wait("sync", "dve")
                em.dma("sp", taps["zz"], ST[0:1, 0:260])
            for tch in range(TO // 512):
                em.dma("sp", Abuf[:].rearrange("p (dt t) -> p dt t", dt=DT),
                       fm(s_oun)[:, :, tch * 512:(tch + 1) * 512])
                em.wait_dma("vector", "sp")
                for dt in range(DT):
                    em.dve(lambda h, dt=dt: h.tensor_single_scalar(
                        out=SCd[:, dt * 512:(dt + 1) * 512],
                        in_=Abuf[:, dt * 512:(dt + 1) * 512],
                        scalar=c_small[:, 3:4], op=ALU.mult), inc=True)
                em.wait("sync", "dve")
                em.dma("sp", fm(taps["attn0"])[:, :, tch * 512:(tch + 1) * 512],
                       SCd[:].rearrange("p (dt t) -> p dt t", dt=DT))
                em.wait_dma("vector", "sp")
            em.wait_dma("sync", "sp")

        em.phase(p4dbg)
        sb_free("Abuf_dbg")

    if "attn0" in taps and STAGE >= 4:
        p4dbg_mk()

    def pfin_stub_mk():
        SCd = sb("SC_stub", [128, DT * 512], F32)

        def pfin_stub(em):
            em.dve(lambda h: h.memset(SCd[:], 0.0), inc=True)
            em.wait("sync", "dve")
            for tch in range(TO // 512):
                em.dma("sp", fm(y)[:, :, tch * 512:(tch + 1) * 512],
                       SCd[:].rearrange("p (dt t) -> p dt t", dt=DT))
            em.wait_dma("sync", "sp")

        em.phase(pfin_stub)
        sb_free("SC_stub")

    if STAGE < 10:
        pfin_stub_mk()

    return nc, taps



def _prep_core_inputs(inputs, core):
    b, par = core // 2, core % 2
    f32 = np.float32
    x = np.asarray(inputs["L0_attn_x"], f32)
    if par == 0:
        own = np.r_[np.arange(0, 1024), np.arange(3072, 4096)]
    else:
        own = np.arange(1024, 3072)
    d = {}
    d["x_own"] = np.ascontiguousarray(x[b][own].T)
    d["x_full"] = np.ascontiguousarray(x[b].T)
    d["row_idx"] = np.ascontiguousarray(own[None, :].astype(f32))
    d["col_idx"] = np.ascontiguousarray(
        np.arange(T, dtype=f32).reshape(T // 128, 128).T)
    d["ident_b"] = np.eye(128, dtype=ml_dtypes.bfloat16)
    d["ident_f"] = np.eye(128, dtype=f32)
    sl = np.tril(np.ones((128, 128), f32), -1)
    d["msl_neg"] = -sl
    d["msu_neg"] = -sl.T
    d["tril_f"] = np.tril(np.ones((128, 128), f32))
    d["triu_f"] = np.triu(np.ones((128, 128), f32))
    s01 = np.zeros((128, 2), f32)
    s01[:, par] = 1.0
    d["sel01"] = s01
    sh = np.zeros((2, 256), ml_dtypes.bfloat16)
    sh[0, 0:128] = 1.0
    sh[1, 128:256] = 1.0
    d["selh"] = sh
    s8 = np.zeros((8, 8, 128), ml_dtypes.bfloat16)
    for e_ in range(8):
        s8[e_, e_, :] = 1.0
    d["sel8"] = np.ascontiguousarray(s8.reshape(8, 8 * 128))
    s10 = np.zeros((10, 10, 128), f32)
    for e_ in range(10):
        s10[e_, e_, :] = 1.0
    d["sel10"] = np.ascontiguousarray(s10.reshape(10, 10 * 128))

    d["wq0"] = np.asarray(inputs["L0_attn_Wq"], f32)
    d["wk0"] = np.asarray(inputs["L0_attn_Wk"], f32)
    d["wv0"] = np.asarray(inputs["L0_attn_Wv"], f32)
    d["bq0"] = np.ascontiguousarray(np.asarray(inputs["L0_attn_bq"], f32)
                                    .reshape(DT, 128).T)
    d["bk0"] = np.ascontiguousarray(np.asarray(inputs["L0_attn_bk"], f32)
                                    .reshape(DT, 128).T)
    d["bv0"] = np.asarray(inputs["L0_attn_bv"], f32)[None, :]
    d["gw0"] = np.asarray(inputs["L0_ffn_gate_w"], f32)
    for k_ in ("ew1", "ew2", "ew3", "sw1", "sw2", "sw3"):
        d[k_] = np.asarray(inputs["L0_ffn_" + k_], f32)

    hs = slice(par * 2 * 256, (par * 2 + 2) * 256)
    d["wq1"] = np.ascontiguousarray(np.asarray(inputs["L1_attn_Wq"], f32).T[:, hs])
    d["wk1"] = np.ascontiguousarray(np.asarray(inputs["L1_attn_Wk"], f32).T[:, hs])
    d["wv1"] = np.ascontiguousarray(np.asarray(inputs["L1_attn_Wv"], f32).T[:, hs])
    d["wb1"] = np.ascontiguousarray(
        np.asarray(inputs["L1_attn_Wb"], f32).T[:, par * 2:par * 2 + 2])
    for nm, src in (("cvq", "L1_attn_conv_q"), ("cvk", "L1_attn_conv_k"),
                    ("cvv", "L1_attn_conv_v")):
        w = np.asarray(inputs[src], f32)[hs]
        d[nm] = np.ascontiguousarray(
            w.reshape(4, 128, 4).transpose(1, 0, 2).reshape(128, 16))
    for nm, src, K in (("firs", "L1_attn_fir_s", 3), ("firm", "L1_attn_fir_m", 15),
                       ("firl", "L1_attn_fir_l", 63)):
        fw = np.asarray(inputs[src], f32)[par * 2:par * 2 + 2]
        fw = fw.reshape(512, K).reshape(4, 128, K).transpose(1, 0, 2)
        d[nm] = np.ascontiguousarray(fw.reshape(128, 4 * K))
    gw = np.asarray(inputs["L1_attn_gate_w"], f32)[par * 2:par * 2 + 2]
    d["gw1"] = np.ascontiguousarray(gw.reshape(10, D).T)
    d["gb1"] = np.ascontiguousarray(
        np.asarray(inputs["L1_attn_gate_b"], f32)[par * 2:par * 2 + 2].reshape(1, 10))
    d["ltmp"] = np.ascontiguousarray(
        np.asarray(inputs["L1_attn_log_temp"], f32)[par * 2:par * 2 + 2].reshape(1, 2))
    d["epsp"] = np.ascontiguousarray(
        np.asarray(inputs["L1_attn_eps_param"], f32)[par * 2:par * 2 + 2].reshape(1, 2))
    d["gamm"] = np.ascontiguousarray(
        np.asarray(inputs["L1_attn_norm_w"], f32).reshape(2, 128).T)
    d["wo1"] = np.ascontiguousarray(np.asarray(inputs["L1_attn_Wo"], f32).T)
    d["w1t"] = np.ascontiguousarray(np.asarray(inputs["L1_ffn_w1"], f32).T)
    d["w2t"] = np.ascontiguousarray(np.asarray(inputs["L1_ffn_w2"], f32).T)
    d["b1f"] = np.ascontiguousarray(
        np.asarray(inputs["L1_ffn_b1"], f32).reshape(32, 128).T)
    d["b2f"] = np.ascontiguousarray(
        np.asarray(inputs["L1_ffn_b2"], f32).reshape(8, 128).T)
    return d


def kernel(**inputs):
    nc, _ = build_nc()
    in_maps = [_prep_core_inputs(inputs, c) for c in range(8)]
    res = run_bass_kernel_spmd(nc, in_maps, core_ids=list(range(8)))
    out = np.zeros((B, T, D), np.float32)
    for c in range(8):
        b, par = c // 2, c % 2
        if par == 0:
            own = np.r_[np.arange(0, 1024), np.arange(3072, 4096)]
        else:
            own = np.arange(1024, 3072)
        out[b][own] = np.asarray(res.results[c]["y"]).T
    return out
